# version 04
# baseline (speedup 1.0000x reference)
import math
import contextlib
import numpy as np
import concourse.bass as bass
import concourse.mybir as mybir
from concourse.bass_utils import run_bass_kernel_spmd

F32 = mybir.dt.float32
AF = mybir.ActivationFunctionType
ALU = mybir.AluOpType
AX = mybir.AxisListType.X

NEG = -1.0e30


class Cfg:
    def __init__(self, D=4096, S=4096, HQ=32, HKV=8, HD=128, E=8, R=16, RG=4,
                 eps=1e-5, theta=1e4, phases=(1, 2, 3, 4, 5), n_cores=8):
        self.D, self.S = D, S
        self.HQ, self.HKV, self.HD = HQ, HKV, HD
        self.E, self.R, self.RG = E, R, RG
        self.eps, self.theta = eps, theta
        self.phases = phases
        self.n_cores = n_cores
        self.T = S // RG
        self.TT = self.T // 128
        self.DC = D // 128
        self.ER = E * R
        self.G4 = HQ // HKV
        self.QKO = HQ + HKV
        self.QO = HQ * HD
        self.KO = HKV * HD
        self.VO = HKV * HD
        assert self.HD == 128 and self.ER == 128 and self.T % 128 == 0
        assert self.T % 512 == 0 and D % 512 == 0


def build_nc(cfg):
    c = cfg
    nc = bass.Bass("TRN2", target_bir_lowering=False)

    x_in = nc.dram_tensor("x_own", [c.T, c.D], F32, kind="ExternalInput").ap()
    wqkvT = nc.dram_tensor("wqkvT", [c.D, (c.HQ + 2 * c.HKV) * c.HD], F32, kind="ExternalInput").ap()
    woT = nc.dram_tensor("woT", [c.QO, c.D], F32, kind="ExternalInput").ap()
    routeT_d = nc.dram_tensor("routeT", [c.D, c.E], F32, kind="ExternalInput")
    a2T_d = nc.dram_tensor("a2T", [c.D, c.ER], F32, kind="ExternalInput")
    b2_d = nc.dram_tensor("b2", [c.ER, c.D], F32, kind="ExternalInput").ap()
    ident_d = nc.dram_tensor("ident", [128, 128], F32, kind="ExternalInput").ap()
    m8_d = nc.dram_tensor("m8", [c.E, c.ER], F32, kind="ExternalInput").ap()
    ropeq_c_d = nc.dram_tensor("ropeq_c", [c.HD, c.T], F32, kind="ExternalInput").ap()
    ropeq_s_d = nc.dram_tensor("ropeq_s", [c.HD, c.T], F32, kind="ExternalInput").ap()
    ropek_c_d = nc.dram_tensor("ropek_c", [c.HD, c.T], F32, kind="ExternalInput").ap()
    ropek_s_d = nc.dram_tensor("ropek_s", [c.HD, c.T], F32, kind="ExternalInput").ap()
    dmask_d = nc.dram_tensor("dmask", [128, c.RG * 128], F32, kind="ExternalInput").ap()
    y_out = nc.dram_tensor("y", [c.T, c.D], F32, kind="ExternalOutput").ap()

    x1_buf = nc.dram_tensor("x1_buf", [c.T, c.D], F32)
    ln2_buf = nc.dram_tensor("ln2_buf", [c.D, c.T], F32)
    q_buf = nc.dram_tensor("q_buf", [c.QO, c.T], F32)
    agk_in = nc.dram_tensor("agk_in", [c.KO, c.T], F32)
    agv_in = nc.dram_tensor("agv_in", [c.T, c.VO], F32)
    agk_out = nc.dram_tensor("agk_out", [c.RG * c.KO, c.T], F32)
    agv_out = nc.dram_tensor("agv_out", [c.RG * c.T, c.VO], F32)
    xn_buf = nc.dram_tensor("xn_buf", [c.T, c.D], F32)
    attn_buf = nc.dram_tensor("attn_buf", [c.QO, c.T], F32)

    stack = contextlib.ExitStack()

    def new_sem(name):
        return stack.enter_context(nc.semaphore(name))

    fs = {n: new_sem("f" + n) for n in
          ["FW", "Xa", "Xb", "S", "T", "V", "XWa", "XWb", "LWa", "LWb"]}
    sLd = new_sem("sLd")

    ident = stack.enter_context(nc.sbuf_tensor("c_ident", [128, 128], F32))
    m8 = stack.enter_context(nc.sbuf_tensor("c_m8", [c.E, c.ER], F32))
    dmask = stack.enter_context(nc.sbuf_tensor("c_dmask", [128, c.RG * 128], F32))
    routeT = stack.enter_context(nc.sbuf_tensor("c_routeT", [128, c.DC * c.E], F32))

    def ffn_phase(tag, src_dram, dst_dram, second_ln, ln2_dst, B, entry_waits):
        NB = c.D // 512
        OS = c.D // 512
        SW = 1 + c.DC + 1 + OS + ((1 + c.DC) if second_ln else 0)
        def ss_sqrt1(i): return B["S"] + i * SW + 1
        def ss_lnf(i, kc): return B["S"] + i * SW + 1 + kc + 1
        def ss_rwT(i): return B["S"] + i * SW + 1 + c.DC + 1
        def ss_f(i, os): return B["S"] + i * SW + 1 + c.DC + 1 + os + 1
        def ss_sqrt2(i): return B["S"] + i * SW + 1 + c.DC + 1 + OS + 1
        def ss_ln2f(i, kc): return B["S"] + i * SW + 1 + c.DC + 1 + OS + 1 + kc + 1
        TW = 3 * c.DC + 2 + OS + (c.DC if second_ln else 0)
        def st(i, k): return B["T"] + i * TW + k
        def st_tp(i, kc): return st(i, kc + 1)
        def st_route_all(i): return st(i, 2 * c.DC)
        def st_rwT(i): return st(i, 2 * c.DC + 1)
        def st_h_all(i): return st(i, 3 * c.DC + 2)
        def st_b(i, os): return st(i, 3 * c.DC + 2 + os + 1)
        def st_ln2tp(i, kc): return st(i, 3 * c.DC + 2 + OS + kc + 1)
        VW = 5 + (2 if second_ln else 0)
        def sv_veps(i): return B["V"] + i * VW + 1
        def sv_ln(i): return B["V"] + i * VW + 2
        def sv_rw(i): return B["V"] + i * VW + 3
        def sv_wh(i): return B["V"] + i * VW + 4
        def sv_xo(i): return B["V"] + i * VW + 5
        def sv_veps2(i): return B["V"] + i * VW + 6
        def sv_ln2(i): return B["V"] + i * VW + 7
        def par_ld(nm, i):
            return B[nm] + 16 * (i // 2 + 1)

        with contextlib.ExitStack() as est:
            en = est.enter_context
            a2T = en(nc.sbuf_tensor(tag + "a2T", [128, c.DC * c.ER], F32))
            b2 = en(nc.sbuf_tensor(tag + "b2", [c.ER, c.D], F32))
            xt2 = en(nc.sbuf_tensor(tag + "xt", [128, 2 * c.D], F32))
            stats = en(nc.sbuf_tensor(tag + "stats", [128, 6 * NB], F32))
            mv = en(nc.sbuf_tensor(tag + "mv", [128, 2], F32))
            veps = en(nc.sbuf_tensor(tag + "veps", [128, 1], F32))
            sd = en(nc.sbuf_tensor(tag + "sd", [128, 1], F32))
            rr = en(nc.sbuf_tensor(tag + "rr", [128, 1], F32))
            lnt = en(nc.sbuf_tensor(tag + "lnt", [128, c.D], F32))
            lnf = en(nc.sbuf_tensor(tag + "lnf", [128, c.D], F32))
            rwb = en(nc.sbuf_tensor(tag + "rw", [128, c.E + 1], F32))
            rwT = en(nc.sbuf_tensor(tag + "rwT", [c.E, 128], F32))
            wh = en(nc.sbuf_tensor(tag + "wh", [c.ER, 128], F32))
            pbs = en(nc.sbuf_tensor(tag + "pbs", [c.ER, 128], F32))
            fsb = en(nc.sbuf_tensor(tag + "fsb", [128, (c.D // 512) * 512], F32))
            xo2 = en(nc.sbuf_tensor(tag + "xo", [128, 2 * c.D], F32))
            ln2t = en(nc.sbuf_tensor(tag + "ln2t", [128, c.D], F32))
            ln2f2 = en(nc.sbuf_tensor(tag + "ln2f", [128, 2 * c.D], F32)) if second_ln else None
            ptA = en(nc.psum_tensor(tag + "ptA", [128, 512], F32))
            ptB = en(nc.psum_tensor(tag + "ptB", [128, 512], F32))
            prb = en(nc.psum_tensor(tag + "pr", [128, 512], F32))
            prwb = en(nc.psum_tensor(tag + "prw", [128, 512], F32))
            phb = en(nc.psum_tensor(tag + "ph", [128, 512], F32))
            pbb = en(nc.psum_tensor(tag + "pb", [128, 512], F32))
            pf = en(nc.psum_tensor(tag + "pf", [128, 2 * 512], F32))
            block = en(nc.Block())

            pr = prb[:, 0:c.E]
            prw = prwb[0:c.E, 0:128]
            ph = phb[:, 0:128]
            pb = pbb[:, 0:128]
            mxv = rwb[:, c.E:c.E + 1]
            rw = rwb[:, 0:c.E]

            @block.sync
            def _(sync):
                for s, v in entry_waits:
                    sync.wait_ge(s, v)
                srcA = bass.AP(a2T_d, 0, [[c.ER, 128], [128 * c.ER, c.DC], [1, c.ER]])
                sync.dma_start(out=a2T[:], in_=srcA).then_inc(fs["FW"], 16)
                sync.dma_start(out=b2[:], in_=b2_d[:]).then_inc(fs["FW"], 16)
                for i in range(c.TT):
                    if i >= 2:
                        sync.wait_ge(fs["V"], sv_xo(i - 2))
                    sx = fs["Xa"] if i % 2 == 0 else fs["Xb"]
                    sync.dma_start(out=xt2[:, (i % 2) * c.D:(i % 2 + 1) * c.D],
                                   in_=src_dram[i * 128:(i + 1) * 128, :]).then_inc(sx, 16)

            @block.vector
            def _(vector):
                for i in range(c.TT):
                    xt = xt2[:, (i % 2) * c.D:(i % 2 + 1) * c.D]
                    xo = xo2[:, (i % 2) * c.D:(i % 2 + 1) * c.D]
                    vector.wait_ge(fs["FW"], B["FW"] + 32)
                    vector.wait_ge(fs["Xa"] if i % 2 == 0 else fs["Xb"],
                                   par_ld("Xa" if i % 2 == 0 else "Xb", i))
                    for b in range(NB):
                        vector.bn_stats(stats[:, b * 6:(b + 1) * 6], xt[:, b * 512:(b + 1) * 512])
                    vector.drain()
                    vector.bn_aggr(mv[:], stats[:, 0:6 * NB])
                    vector.drain()
                    vector.tensor_scalar(veps[:], mv[:, 1:2], c.eps, None,
                                         op0=ALU.add).then_inc(fs["V"], 1)
                    vector.wait_ge(fs["S"], ss_sqrt1(i))
                    vector.reciprocal(rr[:], sd[:])
                    vector.drain()
                    if i >= 1:
                        vector.wait_ge(fs["T"], st(i - 1, c.DC))
                    vector.tensor_scalar(lnt[:], xt, mv[:, 0:1], rr[:],
                                         op0=ALU.subtract, op1=ALU.mult).then_inc(fs["V"], 1)
                    vector.wait_ge(fs["T"], st_route_all(i))
                    vector.reduce_max(mxv, pr, axis=AX)
                    vector.drain()
                    vector.tensor_scalar(rw, pr, mxv, mxv, op0=ALU.is_ge,
                                         op1=ALU.mult).then_inc(fs["V"], 1)
                    vector.wait_ge(fs["T"], st_h_all(i))
                    vector.tensor_copy(pbs[:], pb)
                    vector.drain()
                    vector.tensor_tensor(wh[:], ph, pbs[:], op=ALU.mult).then_inc(fs["V"], 1)
                    if i >= 2:
                        vector.wait_ge(fs["XWa"] if i % 2 == 0 else fs["XWb"],
                                       par_ld("XWa" if i % 2 == 0 else "XWb", i - 2))
                    for os in range(OS):
                        vector.wait_ge(fs["S"], ss_f(i, os))
                        ins = vector.tensor_tensor(xo[:, os * 512:(os + 1) * 512],
                                                   xt[:, os * 512:(os + 1) * 512],
                                                   fsb[:, os * 512:(os + 1) * 512],
                                                   op=ALU.add)
                        if os == OS - 1:
                            ins.then_inc(fs["V"], 1)
                    if second_ln:
                        vector.drain()
                        for b in range(NB):
                            vector.bn_stats(stats[:, b * 6:(b + 1) * 6], xo[:, b * 512:(b + 1) * 512])
                        vector.drain()
                        vector.bn_aggr(mv[:], stats[:, 0:6 * NB])
                        vector.drain()
                        vector.tensor_scalar(veps[:], mv[:, 1:2], c.eps, None,
                                             op0=ALU.add).then_inc(fs["V"], 1)
                        vector.wait_ge(fs["S"], ss_sqrt2(i))
                        vector.reciprocal(rr[:], sd[:])
                        vector.drain()
                        if i >= 1:
                            vector.wait_ge(fs["T"], st(i, 0))
                        vector.tensor_scalar(ln2t[:], xo, mv[:, 0:1], rr[:],
                                             op0=ALU.subtract, op1=ALU.mult).then_inc(fs["V"], 1)

            @block.scalar
            def _(scalar):
                for i in range(c.TT):
                    scalar.wait_ge(fs["V"], sv_veps(i))
                    scalar.activation(sd[:], veps[:], AF.Sqrt, bias=0.0,
                                      scale=1.0).then_inc(fs["S"], 1)
                    for kc in range(c.DC):
                        scalar.wait_ge(fs["T"], st_tp(i, kc))
                        pt = ptA if kc % 2 == 0 else ptB
                        scalar.copy(lnf[:, kc * 128:(kc + 1) * 128],
                                    pt[:, 0:128]).then_inc(fs["S"], 1)
                    scalar.wait_ge(fs["T"], st_rwT(i))
                    scalar.copy(rwT[:], prw).then_inc(fs["S"], 1)
                    for os in range(OS):
                        scalar.wait_ge(fs["T"], st_b(i, os))
                        scalar.mul(fsb[:, os * 512:(os + 1) * 512],
                                   pf[:, (os % 2) * 512:(os % 2 + 1) * 512],
                                   0.5).then_inc(fs["S"], 1)
                    if second_ln:
                        scalar.wait_ge(fs["V"], sv_veps2(i))
                        scalar.activation(sd[:], veps[:], AF.Sqrt, bias=0.0,
                                          scale=1.0).then_inc(fs["S"], 1)
                        if i >= 2:
                            scalar.wait_ge(fs["LWa"] if i % 2 == 0 else fs["LWb"],
                                           par_ld("LWa" if i % 2 == 0 else "LWb", i - 2))
                        ln2f = ln2f2[:, (i % 2) * c.D:(i % 2 + 1) * c.D]
                        for kc in range(c.DC):
                            scalar.wait_ge(fs["T"], st_ln2tp(i, kc))
                            pt = ptA if kc % 2 == 0 else ptB
                            scalar.copy(ln2f[:, kc * 128:(kc + 1) * 128],
                                        pt[:, 0:128]).then_inc(fs["S"], 1)

            @block.tensor
            def _(tensor):
                for i in range(c.TT):
                    tensor.wait_ge(fs["V"], sv_ln(i))
                    for kc in range(c.DC):
                        pt = ptA if kc % 2 == 0 else ptB
                        if kc >= 2:
                            tensor.wait_ge(fs["S"], ss_lnf(i, kc - 2))
                        elif i > 0:
                            tensor.wait_ge(fs["S"], B["S"] + i * SW)
                        tensor.transpose(pt[:, 0:128], lnt[:, kc * 128:(kc + 1) * 128],
                                         ident[:]).then_inc(fs["T"], 1)
                    for kc in range(c.DC):
                        tensor.wait_ge(fs["S"], ss_lnf(i, kc))
                        tensor.matmul(pr, lnf[:, kc * 128:(kc + 1) * 128],
                                      routeT[:, kc * c.E:(kc + 1) * c.E],
                                      start=(kc == 0), stop=(kc == c.DC - 1)).then_inc(fs["T"], 1)
                    tensor.wait_ge(fs["V"], sv_rw(i))
                    tensor.transpose(prw, rw, ident[:]).then_inc(fs["T"], 1)
                    tensor.wait_ge(fs["S"], ss_rwT(i))
                    tensor.matmul(pb, m8[:], rwT[:], start=True, stop=True).then_inc(fs["T"], 1)
                    for kc in range(c.DC):
                        tensor.matmul(ph, a2T[:, kc * c.ER:(kc + 1) * c.ER],
                                      lnf[:, kc * 128:(kc + 1) * 128],
                                      start=(kc == 0), stop=(kc == c.DC - 1)).then_inc(fs["T"], 1)
                    tensor.wait_ge(fs["V"], sv_wh(i))
                    for os in range(OS):
                        if os >= 2:
                            tensor.wait_ge(fs["S"], ss_f(i, os - 2))
                        elif i > 0:
                            tensor.wait_ge(fs["S"], ss_f(i - 1, OS - 1))
                        tensor.matmul(pf[:, (os % 2) * 512:(os % 2 + 1) * 512],
                                      wh[:], b2[:, os * 512:(os + 1) * 512],
                                      start=True, stop=True).then_inc(fs["T"], 1)
                    if second_ln:
                        tensor.wait_ge(fs["V"], sv_ln2(i))
                        for kc in range(c.DC):
                            pt = ptA if kc % 2 == 0 else ptB
                            if kc >= 2:
                                tensor.wait_ge(fs["S"], ss_ln2f(i, kc - 2))
                            else:
                                tensor.wait_ge(fs["S"], ss_f(i, OS - 1))
                            tensor.transpose(pt[:, 0:128], ln2t[:, kc * 128:(kc + 1) * 128],
                                             ident[:]).then_inc(fs["T"], 1)

            @block.gpsimd
            def _(gpsimd):
                for i in range(c.TT):
                    gpsimd.wait_ge(fs["V"], sv_xo(i))
                    sw = fs["XWa"] if i % 2 == 0 else fs["XWb"]
                    gpsimd.dma_start(out=dst_dram[i * 128:(i + 1) * 128, :],
                                     in_=xo2[:, (i % 2) * c.D:(i % 2 + 1) * c.D]).then_inc(sw, 16)
                    if second_ln:
                        gpsimd.wait_ge(fs["S"], B["S"] + (i + 1) * SW)
                        sl = fs["LWa"] if i % 2 == 0 else fs["LWb"]
                        dst = bass.AP(ln2_dst, i * 128,
                                      [[c.T, 128], [128 * c.T, c.DC], [1, 128]])
                        gpsimd.dma_start(out=dst,
                                         in_=ln2f2[:, (i % 2) * c.D:(i % 2 + 1) * c.D]).then_inc(sl, 16)
                na = (c.TT + 1) // 2
                nb = c.TT // 2
                gpsimd.wait_ge(fs["XWa"], B["XWa"] + 16 * na)
                gpsimd.wait_ge(fs["XWb"], B["XWb"] + 16 * nb)
                if second_ln:
                    gpsimd.wait_ge(fs["LWa"], B["LWa"] + 16 * na)
                    gpsimd.wait_ge(fs["LWb"], B["LWb"] + 16 * nb)

        na = (c.TT + 1) // 2
        nb = c.TT // 2
        return {"FW": B["FW"] + 32, "Xa": B["Xa"] + 16 * na, "Xb": B["Xb"] + 16 * nb,
                "S": B["S"] + c.TT * SW, "T": B["T"] + c.TT * TW, "V": B["V"] + c.TT * VW,
                "XWa": B["XWa"] + 16 * na, "XWb": B["XWb"] + 16 * nb,
                "LWa": B["LWa"] + (16 * na if second_ln else 0),
                "LWb": B["LWb"] + (16 * nb if second_ln else 0)}

    with stack:
        with nc.Block() as block:
            @block.sync
            def _(sync):
                sync.dma_start(out=ident[:], in_=ident_d[:]).then_inc(sLd, 16)
                sync.dma_start(out=m8[:], in_=m8_d[:]).then_inc(sLd, 16)
                sync.dma_start(out=dmask[:], in_=dmask_d[:]).then_inc(sLd, 16)
                src = bass.AP(routeT_d, 0, [[c.E, 128], [128 * c.E, c.DC], [1, c.E]])
                sync.dma_start(out=routeT[:], in_=src).then_inc(sLd, 16)
                sync.wait_ge(sLd, 64)

        B0 = {n: 0 for n in fs}
        if 1 in c.phases:
            B1 = ffn_phase("p1", x_in, x1_buf.ap(), True, ln2_buf, B0, [(sLd, 64)])
        else:
            B1 = B0

        if 2 in c.phases:
            NOC = c.QKO
            NTB = c.T // 512
            NVS = c.VO // 512
            sRope = new_sem("sRope")
            sLn2l = new_sem("sLn2l")
            sWla, sWlb = new_sem("sWla"), new_sem("sWlb")
            sWva, sWvb = new_sem("sWva"), new_sem("sWvb")
            sT2 = new_sem("sT2")
            sE2 = new_sem("sE2")
            sR2 = new_sem("sR2")
            sQwa, sQwb = new_sem("sQwa"), new_sem("sQwb")
            sVe = new_sem("sVe")
            sVwa, sVwb = new_sem("sVwa"), new_sem("sVwb")
            sT3 = new_sem("sT3")
            sCC = new_sem("sCC")
            with contextlib.ExitStack() as est:
                en = est.enter_context
                ln2f_all = en(nc.sbuf_tensor("p2ln2f", [128, c.DC * c.T], F32))
                wbuf = en(nc.sbuf_tensor("p2w", [128, 2 * 128], F32))
                wvbuf = en(nc.sbuf_tensor("p2wv", [128, 2 * 512], F32))
                qh = en(nc.sbuf_tensor("p2qh", [128, c.T], F32))
                qsw = en(nc.sbuf_tensor("p2qsw", [128, c.T], F32))
                t1b = en(nc.sbuf_tensor("p2t1", [128, c.T], F32))
                t2b = en(nc.sbuf_tensor("p2t2", [128, c.T], F32))
                rob = en(nc.sbuf_tensor("p2ro", [128, 2 * c.T], F32))
                cq = en(nc.sbuf_tensor("p2cq", [128, c.T], F32))
                sq = en(nc.sbuf_tensor("p2sq", [128, c.T], F32))
                ck = en(nc.sbuf_tensor("p2ck", [128, c.T], F32))
                sk = en(nc.sbuf_tensor("p2sk", [128, c.T], F32))
                vsb = en(nc.sbuf_tensor("p2v", [128, 2 * 512], F32))
                pbig = en(nc.psum_tensor("p2ps", [128, 8 * 512], F32))
                block = en(nc.Block())

                @block.sync
                def _(sync):
                    sync.dma_start(out=cq[:], in_=ropeq_c_d[:]).then_inc(sRope, 16)
                    sync.dma_start(out=sq[:], in_=ropeq_s_d[:]).then_inc(sRope, 16)
                    sync.dma_start(out=ck[:], in_=ropek_c_d[:]).then_inc(sRope, 16)
                    sync.dma_start(out=sk[:], in_=ropek_s_d[:]).then_inc(sRope, 16)
                    if 1 in c.phases:
                        sync.wait_ge(fs["LWa"], B1["LWa"])
                        sync.wait_ge(fs["LWb"], B1["LWb"])
                    q4 = c.DC // 4
                    for p in range(4):
                        src = bass.AP(ln2_buf, p * q4 * 128 * c.T,
                                      [[c.T, 128], [128 * c.T, q4], [1, c.T]])
                        sync.dma_start(out=ln2f_all[:, p * q4 * c.T:(p + 1) * q4 * c.T],
                                       in_=src).then_inc(sLn2l, 16)
                    for oc in range(NOC):
                        for kc in range(c.DC):
                            idx = oc * c.DC + kc
                            if idx >= 2:
                                sync.wait_ge(sT2, idx - 1)
                            sw = sWla if idx % 2 == 0 else sWlb
                            sync.dma_start(out=wbuf[:, (idx % 2) * 128:(idx % 2 + 1) * 128],
                                           in_=wqkvT[kc * 128:(kc + 1) * 128,
                                                     oc * 128:(oc + 1) * 128]).then_inc(sw, 16)
                    for os in range(NVS):
                        for kc in range(c.DC):
                            idx = os * c.DC + kc
                            if idx >= 2:
                                sync.wait_ge(sT3, idx - 1)
                            sw = sWva if idx % 2 == 0 else sWvb
                            co = c.QO + c.KO + os * 512
                            sync.dma_start(out=wvbuf[:, (idx % 2) * 512:(idx % 2 + 1) * 512],
                                           in_=wqkvT[kc * 128:(kc + 1) * 128,
                                                     co:co + 512]).then_inc(sw, 16)

                @block.tensor
                def _(tensor):
                    tensor.wait_ge(sLn2l, 64)
                    for oc in range(NOC):
                        for kc in range(c.DC):
                            idx = oc * c.DC + kc
                            tensor.wait_ge(sWla if idx % 2 == 0 else sWlb,
                                           16 * (idx // 2 + 1))
                            if kc == 0 and oc >= 2:
                                tensor.wait_ge(sE2, oc - 1)
                            w = wbuf[:, (idx % 2) * 128:(idx % 2 + 1) * 128]
                            for tb in range(NTB):
                                po = ((oc % 2) * NTB + tb) * 512
                                ins = tensor.matmul(pbig[:, po:po + 512], w,
                                                    ln2f_all[:, kc * c.T + tb * 512:kc * c.T + (tb + 1) * 512],
                                                    start=(kc == 0), stop=(kc == c.DC - 1))
                            ins.then_inc(sT2, 1)
                    for os in range(NVS):
                        for kc in range(c.DC):
                            idx = os * c.DC + kc
                            tensor.wait_ge(sWva if idx % 2 == 0 else sWvb,
                                           16 * (idx // 2 + 1))
                            if idx == 0:
                                tensor.wait_ge(sE2, NOC)
                            if kc == 0 and os >= 1:
                                tensor.wait_ge(sVe, os * c.TT)
                            w = wvbuf[:, (idx % 2) * 512:(idx % 2 + 1) * 512]
                            for tt in range(c.TT):
                                ins = tensor.matmul(pbig[:, tt * 512:(tt + 1) * 512],
                                                    ln2f_all[:, kc * c.T + tt * 128:kc * c.T + tt * 128 + 128],
                                                    w, start=(kc == 0), stop=(kc == c.DC - 1))
                            ins.then_inc(sT3, 1)

                @block.scalar
                def _(scalar):
                    for oc in range(NOC):
                        scalar.wait_ge(sT2, (oc + 1) * c.DC)
                        if oc >= 1:
                            scalar.wait_ge(sR2, oc)
                        for tb in range(NTB):
                            po = ((oc % 2) * NTB + tb) * 512
                            ins = scalar.copy(qh[:, tb * 512:(tb + 1) * 512], pbig[:, po:po + 512])
                        ins.then_inc(sE2, 1)
                    for os in range(NVS):
                        scalar.wait_ge(sT3, (os + 1) * c.DC)
                        for tt in range(c.TT):
                            idx = os * c.TT + tt
                            if idx >= 2:
                                scalar.wait_ge(sVwa if idx % 2 == 0 else sVwb,
                                               16 * ((idx - 2) // 2 + 1))
                            scalar.copy(vsb[:, (idx % 2) * 512:(idx % 2 + 1) * 512],
                                        pbig[:, tt * 512:(tt + 1) * 512]).then_inc(sVe, 1)

                @block.vector
                def _(vector):
                    H2 = c.HD // 2
                    vector.wait_ge(sRope, 64)
                    for oc in range(NOC):
                        cx, sx = (cq, sq) if oc < c.HQ else (ck, sk)
                        vector.wait_ge(sE2, oc + 1)
                        if oc >= 2:
                            vector.wait_ge(sQwa if oc % 2 == 0 else sQwb,
                                           16 * ((oc - 2) // 2 + 1))
                        ro = rob[:, (oc % 2) * c.T:(oc % 2 + 1) * c.T]
                        vector.tensor_copy(qsw[0:H2, :], qh[H2:2 * H2, :])
                        vector.tensor_tensor(t1b[:], qh[:], cx[:], op=ALU.mult)
                        vector.tensor_copy(qsw[H2:2 * H2, :], qh[0:H2, :])
                        vector.drain()
                        vector.tensor_tensor(t2b[0:H2, :], qsw[0:H2, :], sx[0:H2, :], op=ALU.mult)
                        vector.tensor_tensor(t2b[H2:2 * H2, :], qsw[H2:2 * H2, :], sx[H2:2 * H2, :], op=ALU.mult)
                        vector.drain()
                        vector.tensor_tensor(ro[0:H2, :], t1b[0:H2, :], t2b[0:H2, :], op=ALU.add)
                        vector.tensor_tensor(ro[H2:2 * H2, :], t1b[H2:2 * H2, :],
                                             t2b[H2:2 * H2, :], op=ALU.add).then_inc(sR2, 1)

                @block.gpsimd
                def _(gpsimd):
                    for oc in range(NOC):
                        gpsimd.wait_ge(sR2, oc + 1)
                        ro = rob[:, (oc % 2) * c.T:(oc % 2 + 1) * c.T]
                        sw = sQwa if oc % 2 == 0 else sQwb
                        if oc < c.HQ:
                            gpsimd.dma_start(out=q_buf.ap()[oc * 128:(oc + 1) * 128, :],
                                             in_=ro).then_inc(sw, 16)
                        else:
                            ko = oc - c.HQ
                            gpsimd.dma_start(out=agk_in.ap()[ko * 128:(ko + 1) * 128, :],
                                             in_=ro).then_inc(sw, 16)
                    for os in range(NVS):
                        for tt in range(c.TT):
                            idx = os * c.TT + tt
                            gpsimd.wait_ge(sVe, idx + 1)
                            sw = sVwa if idx % 2 == 0 else sVwb
                            gpsimd.dma_start(
                                out=agv_in.ap()[tt * 128:(tt + 1) * 128, os * 512:(os + 1) * 512],
                                in_=vsb[:, (idx % 2) * 512:(idx % 2 + 1) * 512]).then_inc(sw, 16)
                    gpsimd.wait_ge(sQwa, 16 * ((NOC + 1) // 2))
                    gpsimd.wait_ge(sQwb, 16 * (NOC // 2))
                    nvt = NVS * c.TT
                    gpsimd.wait_ge(sVwa, 16 * ((nvt + 1) // 2))
                    gpsimd.wait_ge(sVwb, 16 * (nvt // 2))
                    ng = c.n_cores // c.RG
                    rg = [[g * c.RG + r for r in range(c.RG)] for g in range(ng)]
                    gpsimd.collective_compute("AllGather", ALU.bypass, replica_groups=rg,
                                              ins=[agk_in.ap().opt()],
                                              outs=[agk_out.ap().opt()]).then_inc(sCC, 1)
                    gpsimd.collective_compute("AllGather", ALU.bypass, replica_groups=rg,
                                              ins=[agv_in.ap().opt()],
                                              outs=[agv_out.ap().opt()]).then_inc(sCC, 1)
                    gpsimd.wait_ge(sCC, 2)

        if 3 in c.phases:
            KW = c.RG * 128
            tiles = []
            kblocks = []
            for g in range(c.HKV):
                for qh_ in range(c.G4):
                    for u in range(c.TT):
                        j = len(tiles)
                        k0 = len(kblocks)
                        for t in range(u + 1):
                            kblocks.append((g, qh_, u, t, j))
                        tiles.append((g, qh_, u, k0, k0 + u))
            NT = len(tiles)
            NK = len(kblocks)
            sKVa, sKVb = new_sem("sKVa"), new_sem("sKVb")
            sQla, sQlb = new_sem("sQla"), new_sem("sQlb")
            sSb = new_sem("sSb")
            sSe = new_sem("sSe")
            sMsk = new_sem("sMsk")
            sNm = new_sem("sNm")
            sPx = new_sem("sPx")
            sLi = new_sem("sLi")
            sPT = new_sem("sPT")
            sPTe = new_sem("sPTe")
            sPV = new_sem("sPV")
            sVfin = new_sem("sVfin")
            sAtp = new_sem("sAtp")
            sAe = new_sem("sAe")
            sAwa, sAwb = new_sem("sAwa"), new_sem("sAwb")

            with contextlib.ExitStack() as est:
                en = est.enter_context
                Kg = en(nc.sbuf_tensor("p3K", [128, c.RG * c.T], F32))
                Vg2 = en(nc.sbuf_tensor("p3V", [128, 2 * c.RG * c.T], F32))
                qt2 = en(nc.sbuf_tensor("p3q", [128, 2 * 128], F32))
                Srow2 = en(nc.sbuf_tensor("p3S", [128, 2 * c.RG * c.T], F32))
                PTsb = en(nc.sbuf_tensor("p3PT", [128, 2 * KW], F32))
                out2 = en(nc.sbuf_tensor("p3out", [128, 2 * 128], F32))
                af2 = en(nc.sbuf_tensor("p3af", [128, 2 * 128], F32))
                mt2 = en(nc.sbuf_tensor("p3mt", [128, 2 * c.TT], F32))
                nm2 = en(nc.sbuf_tensor("p3nm", [128, 2], F32))
                lr2 = en(nc.sbuf_tensor("p3lr", [128, 2], F32))
                li2 = en(nc.sbuf_tensor("p3li", [128, 2], F32))
                pS = en(nc.psum_tensor("p3Sp", [128, 2 * KW], F32))
                pPT = en(nc.psum_tensor("p3PTp", [128, 2 * KW], F32))
                ppvb = en(nc.psum_tensor("p3pv", [128, 2 * 512], F32))
                ptpb = en(nc.psum_tensor("p3tp", [128, 512], F32))
                block = en(nc.Block())

                ptp = ptpb[:, 0:128]

                def ppv(par):
                    return ppvb[:, par * 512:par * 512 + 128]

                @block.sync
                def _(sync):
                    if 2 in c.phases:
                        sync.wait_ge(sCC, 2)
                    for g in range(c.HKV):
                        skv = sKVa if g % 2 == 0 else sKVb
                        if g >= 1:
                            sync.wait_ge(sSb, tiles[g * c.G4 * c.TT - 1][4] + 1)
                        if g >= 2:
                            sync.wait_ge(sPV, tiles[(g - 1) * c.G4 * c.TT - 1][4] + 1)
                        srcK = bass.AP(agk_out, g * 128 * c.T,
                                       [[c.T, 128], [c.KO * c.T, c.RG], [1, c.T]])
                        sync.dma_start(out=Kg[:], in_=srcK).then_inc(skv, 16)
                        srcV = bass.AP(agv_out, g * 128,
                                       [[c.VO, 128], [c.T * c.VO, c.RG],
                                        [128 * c.VO, c.TT], [1, 128]])
                        sync.dma_start(out=Vg2[:, (g % 2) * c.RG * c.T:(g % 2 + 1) * c.RG * c.T],
                                       in_=srcV).then_inc(skv, 16)
                        for qh_ in range(c.G4):
                            h = g * c.G4 + qh_
                            for u in range(c.TT):
                                j = (g * c.G4 + qh_) * c.TT + u
                                if j >= 2:
                                    sync.wait_ge(sSb, tiles[j - 2][4] + 1)
                                sq_ = sQla if j % 2 == 0 else sQlb
                                sync.dma_start(out=qt2[:, (j % 2) * 128:(j % 2 + 1) * 128],
                                               in_=q_buf.ap()[h * 128:(h + 1) * 128,
                                                              u * 128:(u + 1) * 128]).then_inc(sq_, 16)

                @block.tensor
                def _(tensor):
                    def emit_passB(jm):
                        g, qh_, u, k0, ke = tiles[jm]
                        jp = jm % 2
                        Prow = Srow2[:, jp * c.RG * c.T:jp * c.RG * c.T + (u + 1) * KW]
                        tensor.wait_ge(sPx, jm + 1)
                        for t in range(u + 1):
                            k = k0 + t
                            par = k % 2
                            if k >= 2:
                                tensor.wait_ge(sPTe, k - 1)
                            for r in range(c.RG):
                                ins = tensor.transpose(
                                    pPT[:, par * KW + r * 128:par * KW + (r + 1) * 128],
                                    Prow[:, t * KW + r * 128:t * KW + (r + 1) * 128],
                                    ident[:])
                            ins.then_inc(sPT, 1)
                            tensor.wait_ge(sPTe, k + 1)
                            Vg = Vg2[:, (g % 2) * c.RG * c.T:(g % 2 + 1) * c.RG * c.T]
                            for r in range(c.RG):
                                ins = tensor.matmul(
                                    ppv(jp),
                                    PTsb[:, par * KW + r * 128:par * KW + (r + 1) * 128],
                                    Vg[:, (r * c.TT + t) * 128:(r * c.TT + t + 1) * 128],
                                    start=(t == 0 and r == 0), stop=(t == u and r == c.RG - 1))
                            ins.then_inc(sPV, 1)

                    def emit_ttp(jm):
                        tensor.wait_ge(sVfin, jm + 1)
                        if jm >= 1:
                            tensor.wait_ge(sAe, jm)
                        tensor.transpose(ptp, out2[:, (jm % 2) * 128:(jm % 2 + 1) * 128],
                                         ident[:]).then_inc(sAtp, 1)

                    for j, (g, qh_, u, k0, ke) in enumerate(tiles):
                        qt = qt2[:, (j % 2) * 128:(j % 2 + 1) * 128]
                        for t in range(u + 1):
                            k = k0 + t
                            par = k % 2
                            if k >= 2:
                                tensor.wait_ge(sSe, k - 1)
                            if t == 0:
                                tensor.wait_ge(sQla if j % 2 == 0 else sQlb,
                                               16 * (j // 2 + 1))
                                tensor.wait_ge(sKVa if g % 2 == 0 else sKVb,
                                               32 * (g // 2 + 1))
                                if j >= 2:
                                    tensor.wait_ge(sVfin, j - 1)
                            rhs = bass.AP(Kg, t * 128,
                                          [[c.RG * c.T, 128], [c.T, c.RG], [1, 128]])
                            tensor.matmul(pS[:, par * KW:(par + 1) * KW], qt, rhs,
                                          start=True, stop=True).then_inc(sSb, 1)
                        if j >= 1:
                            emit_passB(j - 1)
                            emit_ttp(j - 1)
                    emit_passB(NT - 1)
                    emit_ttp(NT - 1)

                @block.scalar
                def _(scalar):
                    def emit_sae(jm):
                        scalar.wait_ge(sAtp, jm + 1)
                        if jm >= 2:
                            scalar.wait_ge(sAwa if jm % 2 == 0 else sAwb,
                                           16 * ((jm - 2) // 2 + 1))
                        scalar.copy(af2[:, (jm % 2) * 128:(jm % 2 + 1) * 128],
                                    ptp).then_inc(sAe, 1)

                    for j, (g, qh_, u, k0, ke) in enumerate(tiles):
                        jp = j % 2
                        Srow = Srow2[:, jp * c.RG * c.T:jp * c.RG * c.T + (u + 1) * KW]
                        for t in range(u + 1):
                            k = k0 + t
                            par = k % 2
                            scalar.wait_ge(sSb, k + 1)
                            if j >= 2 and t == 0:
                                scalar.wait_ge(sPT, tiles[j - 2][4] + 1)
                            scalar.copy(Srow[:, t * KW:(t + 1) * KW],
                                        pS[:, par * KW:(par + 1) * KW]).then_inc(sSe, 1)
                        if j >= 1:
                            jm = j - 1
                            gm, qm, um, km0, kme = tiles[jm]
                            for t in range(um + 1):
                                k = km0 + t
                                par = k % 2
                                scalar.wait_ge(sPT, k + 1)
                                if k >= 2:
                                    scalar.wait_ge(sPV, k - 1)
                                scalar.copy(PTsb[:, par * KW:(par + 1) * KW],
                                            pPT[:, par * KW:(par + 1) * KW]).then_inc(sPTe, 1)
                            emit_sae(jm)
                        scalar.wait_ge(sNm, j + 1)
                        scalar.activation(Srow[:], Srow[:], AF.Exp,
                                          bias=nm2[:, jp:jp + 1], scale=1.0,
                                          accum_out=lr2[:, jp:jp + 1]).then_inc(sPx, 1)
                    jm = NT - 1
                    gm, qm, um, km0, kme = tiles[jm]
                    for t in range(um + 1):
                        k = km0 + t
                        par = k % 2
                        scalar.wait_ge(sPT, k + 1)
                        if k >= 2:
                            scalar.wait_ge(sPV, k - 1)
                        scalar.copy(PTsb[:, par * KW:(par + 1) * KW],
                                    pPT[:, par * KW:(par + 1) * KW]).then_inc(sPTe, 1)
                    emit_sae(jm)

                @block.gpsimd
                def _(gpsimd):
                    for jm, (gg, qq, uu, _k0, _ke) in enumerate(tiles):
                        h = gg * c.G4 + qq
                        gpsimd.wait_ge(sAe, jm + 1)
                        sw = sAwa if jm % 2 == 0 else sAwb
                        gpsimd.dma_start(
                            out=attn_buf.ap()[h * 128:(h + 1) * 128, uu * 128:(uu + 1) * 128],
                            in_=af2[:, (jm % 2) * 128:(jm % 2 + 1) * 128]).then_inc(sw, 16)
                    gpsimd.wait_ge(sAwa, 16 * ((NT + 1) // 2))
                    gpsimd.wait_ge(sAwb, 16 * (NT // 2))

                @block.vector
                def _(vector):
                    for j, (g, qh_, u, k0, ke) in enumerate(tiles):
                        jp = j % 2
                        Srow = Srow2[:, jp * c.RG * c.T:jp * c.RG * c.T + (u + 1) * KW]
                        mt = mt2[:, jp * c.TT:jp * c.TT + c.TT]
                        vector.wait_ge(sSe, ke + 1)
                        vector.tensor_tensor(Srow[:, u * KW:(u + 1) * KW],
                                             Srow[:, u * KW:(u + 1) * KW],
                                             dmask[:], op=ALU.add).then_inc(sMsk, 1)
                        for t in range(u):
                            vector.reduce_max(mt[:, t:t + 1], Srow[:, t * KW:(t + 1) * KW],
                                              axis=AX)
                        vector.drain()
                        vector.reduce_max(mt[:, u:u + 1], Srow[:, u * KW:(u + 1) * KW],
                                          axis=AX)
                        vector.drain()
                        vector.reduce_max(nm2[:, jp:jp + 1], mt[:, 0:u + 1], axis=AX,
                                          negate=True).then_inc(sNm, 1)
                        vector.wait_ge(sPx, j + 1)
                        vector.reciprocal(li2[:, jp:jp + 1], lr2[:, jp:jp + 1])
                        vector.wait_ge(sPV, ke + 1)
                        if j >= 2:
                            vector.wait_ge(sAtp, j - 1)
                        vector.drain()
                        vector.tensor_scalar(out2[:, jp * 128:(jp + 1) * 128], ppv(jp),
                                             li2[:, jp:jp + 1], None,
                                             op0=ALU.mult).then_inc(sVfin, 1)

        if 4 in c.phases:
            OS4 = c.D // 512
            sWoa, sWob = new_sem("sWoa"), new_sem("sWob")
            sX1r = [new_sem("sX1r%d" % i) for i in range(4)]
            sT4 = new_sem("sT4")
            sXe = new_sem("sXe")
            sXna, sXnb = new_sem("sXna"), new_sem("sXnb")
            sAl = new_sem("sAl")
            with contextlib.ExitStack() as est:
                en = est.enter_context
                attn_f = en(nc.sbuf_tensor("p4attn", [128, c.HQ * c.T], F32))
                wob = en(nc.sbuf_tensor("p4w", [128, 2 * 512], F32))
                x1b = en(nc.sbuf_tensor("p4x1", [128, 4 * 512], F32))
                xnb = en(nc.sbuf_tensor("p4xn", [128, 2 * 512], F32))
                ps8 = en(nc.psum_tensor("p4ps", [128, 8 * 512], F32))
                block = en(nc.Block())

                @block.sync
                def _(sync):
                    if 3 in c.phases:
                        sync.wait_ge(sAwa, 16 * ((NT + 1) // 2))
                        sync.wait_ge(sAwb, 16 * (NT // 2))
                    q4 = c.HQ // 4
                    for p in range(4):
                        srcA = bass.AP(attn_buf, p * q4 * 128 * c.T,
                                       [[c.T, 128], [128 * c.T, q4], [1, c.T]])
                        sync.dma_start(out=attn_f[:, p * q4 * c.T:(p + 1) * q4 * c.T],
                                       in_=srcA).then_inc(sAl, 16)
                    for os in range(OS4):
                        for kc in range(c.HQ):
                            idx = os * c.HQ + kc
                            if idx >= 2:
                                sync.wait_ge(sT4, idx - 1)
                            sw = sWoa if idx % 2 == 0 else sWob
                            sync.dma_start(out=wob[:, (idx % 2) * 512:(idx % 2 + 1) * 512],
                                           in_=woT[kc * 128:(kc + 1) * 128,
                                                   os * 512:(os + 1) * 512]).then_inc(sw, 16)

                @block.gpsimd
                def _(gpsimd):
                    if 1 in c.phases:
                        gpsimd.wait_ge(fs["XWa"], B1["XWa"])
                        gpsimd.wait_ge(fs["XWb"], B1["XWb"])
                    tot = OS4 * c.TT

                    def x1_load(idx):
                        os_, tt_ = idx // c.TT, idx % c.TT
                        gpsimd.dma_start(out=x1b[:, (idx % 4) * 512:(idx % 4 + 1) * 512],
                                         in_=x1_buf.ap()[tt_ * 128:(tt_ + 1) * 128,
                                                         os_ * 512:(os_ + 1) * 512]).then_inc(sX1r[idx % 4], 16)

                    for idx in range(min(4, tot)):
                        x1_load(idx)
                    for idx in range(tot):
                        os_, tt_ = idx // c.TT, idx % c.TT
                        gpsimd.wait_ge(sXe, idx + 1)
                        if idx + 4 < tot:
                            x1_load(idx + 4)
                        sw = sXna if idx % 2 == 0 else sXnb
                        gpsimd.dma_start(out=xn_buf.ap()[tt_ * 128:(tt_ + 1) * 128,
                                                         os_ * 512:(os_ + 1) * 512],
                                         in_=xnb[:, (idx % 2) * 512:(idx % 2 + 1) * 512]).then_inc(sw, 16)
                    gpsimd.wait_ge(sXna, 16 * ((tot + 1) // 2))
                    gpsimd.wait_ge(sXnb, 16 * (tot // 2))

                @block.tensor
                def _(tensor):
                    tensor.wait_ge(sAl, 64)
                    for os in range(OS4):
                        for kc in range(c.HQ):
                            idx = os * c.HQ + kc
                            tensor.wait_ge(sWoa if idx % 2 == 0 else sWob,
                                           16 * (idx // 2 + 1))
                            if kc == 0 and os >= 1:
                                tensor.wait_ge(sXe, os * c.TT)
                            w = wob[:, (idx % 2) * 512:(idx % 2 + 1) * 512]
                            for tt in range(c.TT):
                                ins = tensor.matmul(ps8[:, tt * 512:(tt + 1) * 512],
                                                    attn_f[:, kc * c.T + tt * 128:kc * c.T + tt * 128 + 128],
                                                    w, start=(kc == 0), stop=(kc == c.HQ - 1))
                            ins.then_inc(sT4, 1)

                @block.vector
                def _(vector):
                    for os in range(OS4):
                        vector.wait_ge(sT4, (os + 1) * c.HQ)
                        for tt in range(c.TT):
                            idx = os * c.TT + tt
                            vector.wait_ge(sX1r[idx % 4], 16 * (idx // 4 + 1))
                            if idx >= 2:
                                vector.wait_ge(sXna if idx % 2 == 0 else sXnb,
                                               16 * ((idx - 2) // 2 + 1))
                            vector.tensor_tensor(xnb[:, (idx % 2) * 512:(idx % 2 + 1) * 512],
                                                 x1b[:, (idx % 4) * 512:(idx % 4 + 1) * 512],
                                                 ps8[:, tt * 512:(tt + 1) * 512],
                                                 op=ALU.add).then_inc(sXe, 1)

        if 5 in c.phases:
            entry = []
            if 4 in c.phases:
                tot = (c.D // 512) * c.TT
                entry = [(sXna, 16 * ((tot + 1) // 2)), (sXnb, 16 * (tot // 2))]
            ffn_phase("p5", xn_buf.ap(), y_out, False, None, B1, entry)

    return nc


def _host_prep(cfg):
    c = cfg
    ident = np.eye(128, dtype=np.float32)
    m8 = np.zeros((c.E, c.ER), dtype=np.float32)
    for e in range(c.E):
        m8[e, e * c.R:(e + 1) * c.R] = 1.0
    per_core = []
    for core in range(c.n_cores):
        g, l = core // c.RG, core % c.RG
        pos = (l + c.RG * np.arange(c.T)).astype(np.float64)
        inv = 1.0 / (c.theta ** (np.arange(0, c.HD, 2, dtype=np.float64) / c.HD))
        fr = np.outer(pos, inv)
        cos = np.cos(fr).T
        sin = np.sin(fr).T
        scale = 1.0 / math.sqrt(c.HD)
        ropeq_c = (np.concatenate([cos, cos], axis=0) * scale).astype(np.float32)
        ropeq_s = (np.concatenate([-sin, sin], axis=0) * scale).astype(np.float32)
        ropek_c = np.concatenate([cos, cos], axis=0).astype(np.float32)
        ropek_s = np.concatenate([-sin, sin], axis=0).astype(np.float32)
        i_ = np.arange(128)[:, None]
        j_ = np.arange(128)[None, :]
        dm = np.zeros((128, c.RG * 128), dtype=np.float32)
        for r in range(c.RG):
            d = 1 if r > l else 0
            dm[:, r * 128:(r + 1) * 128] = np.where(i_ >= j_ + d, 0.0, NEG).astype(np.float32)
        per_core.append({"ident": ident, "m8": m8,
                         "ropeq_c": np.ascontiguousarray(ropeq_c),
                         "ropeq_s": np.ascontiguousarray(ropeq_s),
                         "ropek_c": np.ascontiguousarray(ropek_c),
                         "ropek_s": np.ascontiguousarray(ropek_s),
                         "dmask": dm})
    return per_core


def make_in_maps(cfg, x, wqkv, wo, route, A, Bw):
    c = cfg
    wqkvT = np.ascontiguousarray(np.asarray(wqkv).T)
    woT = np.ascontiguousarray(np.asarray(wo).T)
    routeT = np.ascontiguousarray(np.asarray(route).T)
    a2T = np.ascontiguousarray(np.asarray(A).reshape(c.E * c.R, c.D).T)
    b2 = np.ascontiguousarray(np.transpose(np.asarray(Bw), (0, 2, 1)).reshape(c.E * c.R, c.D))
    consts = _host_prep(c)
    in_maps = []
    for core in range(c.n_cores):
        g, l = core // c.RG, core % c.RG
        x_own = np.ascontiguousarray(np.asarray(x)[g, l::c.RG, :], dtype=np.float32)
        m = dict(consts[core])
        m.update({"x_own": x_own, "wqkvT": wqkvT, "woT": woT, "routeT": routeT,
                  "a2T": a2T, "b2": b2})
        in_maps.append(m)
    return in_maps


def kernel(L0_attn_hidden_states, L0_attn_Wqkv, L0_attn_Wo,
           L0_ffn_route_w, L0_ffn_A, L0_ffn_Bw):
    cfg = Cfg()
    in_maps = make_in_maps(cfg, L0_attn_hidden_states, L0_attn_Wqkv, L0_attn_Wo,
                           L0_ffn_route_w, L0_ffn_A, L0_ffn_Bw)
    nc = build_nc(cfg)
    res = run_bass_kernel_spmd(nc, in_maps, core_ids=list(range(cfg.n_cores)))
    out = np.empty((cfg.n_cores // cfg.RG, cfg.S, cfg.D), dtype=np.float32)
    for core in range(cfg.n_cores):
        g, l = core // cfg.RG, core % cfg.RG
        out[g, l::cfg.RG, :] = res.results[core]["y"]
    return out


# version 05
# speedup vs baseline: 1.0959x; 1.0959x over previous
import math
import contextlib
import numpy as np
import concourse.bass as bass
import concourse.mybir as mybir
from concourse.bass_utils import run_bass_kernel_spmd

F32 = mybir.dt.float32
AF = mybir.ActivationFunctionType
ALU = mybir.AluOpType
AX = mybir.AxisListType.X

NEG = -1.0e30


class Cfg:
    def __init__(self, D=4096, S=4096, HQ=32, HKV=8, HD=128, E=8, R=16, RG=4,
                 eps=1e-5, theta=1e4, phases=(1, 2, 3, 4, 5), n_cores=8):
        self.D, self.S = D, S
        self.HQ, self.HKV, self.HD = HQ, HKV, HD
        self.E, self.R, self.RG = E, R, RG
        self.eps, self.theta = eps, theta
        self.phases = phases
        self.n_cores = n_cores
        self.T = S // RG
        self.TT = self.T // 128
        self.DC = D // 128
        self.ER = E * R
        self.G4 = HQ // HKV
        self.QKO = HQ + HKV
        self.QO = HQ * HD
        self.KO = HKV * HD
        self.VO = HKV * HD
        assert self.HD == 128 and self.ER == 128 and self.T % 128 == 0
        assert self.T % 512 == 0 and D % 512 == 0


def build_nc(cfg):
    c = cfg
    nc = bass.Bass("TRN2", target_bir_lowering=False)

    x_in = nc.dram_tensor("x_own", [c.T, c.D], F32, kind="ExternalInput").ap()
    wqkvT = nc.dram_tensor("wqkvT", [c.D, (c.HQ + 2 * c.HKV) * c.HD], F32, kind="ExternalInput").ap()
    woT = nc.dram_tensor("woT", [c.QO, c.D], F32, kind="ExternalInput").ap()
    routeT_d = nc.dram_tensor("routeT", [c.D, c.E], F32, kind="ExternalInput")
    a2T_d = nc.dram_tensor("a2T", [c.D, c.ER], F32, kind="ExternalInput")
    b2_d = nc.dram_tensor("b2", [c.ER, c.D], F32, kind="ExternalInput").ap()
    ident_d = nc.dram_tensor("ident", [128, 128], F32, kind="ExternalInput").ap()
    m8_d = nc.dram_tensor("m8", [c.E, c.ER], F32, kind="ExternalInput").ap()
    ropeq_c_d = nc.dram_tensor("ropeq_c", [c.HD, c.T], F32, kind="ExternalInput").ap()
    ropeq_s_d = nc.dram_tensor("ropeq_s", [c.HD, c.T], F32, kind="ExternalInput").ap()
    ropek_c_d = nc.dram_tensor("ropek_c", [c.HD, c.T], F32, kind="ExternalInput").ap()
    ropek_s_d = nc.dram_tensor("ropek_s", [c.HD, c.T], F32, kind="ExternalInput").ap()
    dmask_d = nc.dram_tensor("dmask", [128, c.RG * 128], F32, kind="ExternalInput").ap()
    y_out = nc.dram_tensor("y", [c.T, c.D], F32, kind="ExternalOutput").ap()

    x1_buf = nc.dram_tensor("x1_buf", [c.T, c.D], F32)
    ln2_buf = nc.dram_tensor("ln2_buf", [c.D, c.T], F32)
    q_buf = nc.dram_tensor("q_buf", [c.QO, c.T], F32)
    agk_in = nc.dram_tensor("agk_in", [c.KO, c.T], F32)
    agv_in = nc.dram_tensor("agv_in", [c.T, c.VO], F32)
    agk_out = nc.dram_tensor("agk_out", [c.RG * c.KO, c.T], F32)
    agv_out = nc.dram_tensor("agv_out", [c.RG * c.T, c.VO], F32)
    xn_buf = nc.dram_tensor("xn_buf", [c.T, c.D], F32)
    attn_buf = nc.dram_tensor("attn_buf", [c.QO, c.T], F32)

    stack = contextlib.ExitStack()

    def new_sem(name):
        return stack.enter_context(nc.semaphore(name))

    fs = {n: new_sem("f" + n) for n in
          ["FW", "Xa", "Xb", "S", "T", "V", "XWa", "XWb", "LWa", "LWb"]}
    sLd = new_sem("sLd")

    ident = stack.enter_context(nc.sbuf_tensor("c_ident", [128, 128], F32))
    m8 = stack.enter_context(nc.sbuf_tensor("c_m8", [c.E, c.ER], F32))
    dmask = stack.enter_context(nc.sbuf_tensor("c_dmask", [128, c.RG * 128], F32))
    routeT = stack.enter_context(nc.sbuf_tensor("c_routeT", [128, c.DC * c.E], F32))

    def ffn_phase(tag, src_dram, dst_dram, second_ln, ln2_dst, B, entry_waits):
        NB = c.D // 512
        OS = c.D // 512
        SW = 1 + c.DC + 1 + OS + ((1 + c.DC) if second_ln else 0)
        def ss_sqrt1(i): return B["S"] + i * SW + 1
        def ss_lnf(i, kc): return B["S"] + i * SW + 1 + kc + 1
        def ss_rwT(i): return B["S"] + i * SW + 1 + c.DC + 1
        def ss_f(i, os): return B["S"] + i * SW + 1 + c.DC + 1 + os + 1
        def ss_sqrt2(i): return B["S"] + i * SW + 1 + c.DC + 1 + OS + 1
        def ss_ln2f(i, kc): return B["S"] + i * SW + 1 + c.DC + 1 + OS + 1 + kc + 1
        TW = 3 * c.DC + 2 + OS + (c.DC if second_ln else 0)
        def st(i, k): return B["T"] + i * TW + k
        def st_tp(i, kc): return st(i, kc + 1)
        def st_route_all(i): return st(i, 2 * c.DC)
        def st_rwT(i): return st(i, 2 * c.DC + 1)
        def st_h_all(i): return st(i, 3 * c.DC + 2)
        def st_b(i, os): return st(i, 3 * c.DC + 2 + os + 1)
        def st_ln2tp(i, kc): return st(i, 3 * c.DC + 2 + OS + kc + 1)
        VW = 5 + (2 if second_ln else 0)
        def sv_veps(i): return B["V"] + i * VW + 1
        def sv_ln(i): return B["V"] + i * VW + 2
        def sv_rw(i): return B["V"] + i * VW + 3
        def sv_wh(i): return B["V"] + i * VW + 4
        def sv_xo(i): return B["V"] + i * VW + 5
        def sv_veps2(i): return B["V"] + i * VW + 6
        def sv_ln2(i): return B["V"] + i * VW + 7
        def par_ld(nm, i):
            return B[nm] + 16 * (i // 2 + 1)

        with contextlib.ExitStack() as est:
            en = est.enter_context
            a2T = en(nc.sbuf_tensor(tag + "a2T", [128, c.DC * c.ER], F32))
            b2 = en(nc.sbuf_tensor(tag + "b2", [c.ER, c.D], F32))
            xt2 = en(nc.sbuf_tensor(tag + "xt", [128, 2 * c.D], F32))
            stats = en(nc.sbuf_tensor(tag + "stats", [128, 6 * NB], F32))
            mv = en(nc.sbuf_tensor(tag + "mv", [128, 2], F32))
            veps = en(nc.sbuf_tensor(tag + "veps", [128, 1], F32))
            sd = en(nc.sbuf_tensor(tag + "sd", [128, 1], F32))
            rr = en(nc.sbuf_tensor(tag + "rr", [128, 1], F32))
            lnt = en(nc.sbuf_tensor(tag + "lnt", [128, c.D], F32))
            lnf = en(nc.sbuf_tensor(tag + "lnf", [128, c.D], F32))
            rwb = en(nc.sbuf_tensor(tag + "rw", [128, c.E + 1], F32))
            rwT = en(nc.sbuf_tensor(tag + "rwT", [c.E, 128], F32))
            wh = en(nc.sbuf_tensor(tag + "wh", [c.ER, 128], F32))
            pbs = en(nc.sbuf_tensor(tag + "pbs", [c.ER, 128], F32))
            fsb = en(nc.sbuf_tensor(tag + "fsb", [128, (c.D // 512) * 512], F32))
            xo2 = en(nc.sbuf_tensor(tag + "xo", [128, 2 * c.D], F32))
            ln2t = en(nc.sbuf_tensor(tag + "ln2t", [128, c.D], F32))
            ln2f2 = en(nc.sbuf_tensor(tag + "ln2f", [128, 2 * c.D], F32)) if second_ln else None
            ptA = en(nc.psum_tensor(tag + "ptA", [128, 512], F32))
            ptB = en(nc.psum_tensor(tag + "ptB", [128, 512], F32))
            prb = en(nc.psum_tensor(tag + "pr", [128, 512], F32))
            prwb = en(nc.psum_tensor(tag + "prw", [128, 512], F32))
            phb = en(nc.psum_tensor(tag + "ph", [128, 512], F32))
            pbb = en(nc.psum_tensor(tag + "pb", [128, 512], F32))
            pf = en(nc.psum_tensor(tag + "pf", [128, 2 * 512], F32))
            block = en(nc.Block())

            pr = prb[:, 0:c.E]
            prw = prwb[0:c.E, 0:128]
            ph = phb[:, 0:128]
            pb = pbb[:, 0:128]
            mxv = rwb[:, c.E:c.E + 1]
            rw = rwb[:, 0:c.E]

            @block.sync
            def _(sync):
                for s, v in entry_waits:
                    sync.wait_ge(s, v)
                srcA = bass.AP(a2T_d, 0, [[c.ER, 128], [128 * c.ER, c.DC], [1, c.ER]])
                sync.dma_start(out=a2T[:], in_=srcA).then_inc(fs["FW"], 16)
                sync.dma_start(out=b2[:], in_=b2_d[:]).then_inc(fs["FW"], 16)
                for i in range(c.TT):
                    if i >= 2:
                        sync.wait_ge(fs["V"], sv_xo(i - 2))
                    sx = fs["Xa"] if i % 2 == 0 else fs["Xb"]
                    sync.dma_start(out=xt2[:, (i % 2) * c.D:(i % 2 + 1) * c.D],
                                   in_=src_dram[i * 128:(i + 1) * 128, :]).then_inc(sx, 16)

            @block.vector
            def _(vector):
                for i in range(c.TT):
                    xt = xt2[:, (i % 2) * c.D:(i % 2 + 1) * c.D]
                    xo = xo2[:, (i % 2) * c.D:(i % 2 + 1) * c.D]
                    vector.wait_ge(fs["FW"], B["FW"] + 32)
                    vector.wait_ge(fs["Xa"] if i % 2 == 0 else fs["Xb"],
                                   par_ld("Xa" if i % 2 == 0 else "Xb", i))
                    for b in range(NB):
                        vector.bn_stats(stats[:, b * 6:(b + 1) * 6], xt[:, b * 512:(b + 1) * 512])
                    vector.drain()
                    vector.bn_aggr(mv[:], stats[:, 0:6 * NB])
                    vector.drain()
                    vector.tensor_scalar(veps[:], mv[:, 1:2], c.eps, None,
                                         op0=ALU.add).then_inc(fs["V"], 1)
                    vector.wait_ge(fs["S"], ss_sqrt1(i))
                    vector.reciprocal(rr[:], sd[:])
                    vector.drain()
                    if i >= 1:
                        vector.wait_ge(fs["T"], st(i - 1, c.DC))
                    vector.tensor_scalar(lnt[:], xt, mv[:, 0:1], rr[:],
                                         op0=ALU.subtract, op1=ALU.mult).then_inc(fs["V"], 1)
                    vector.wait_ge(fs["T"], st_route_all(i))
                    vector.reduce_max(mxv, pr, axis=AX)
                    vector.drain()
                    vector.tensor_scalar(rw, pr, mxv, mxv, op0=ALU.is_ge,
                                         op1=ALU.mult).then_inc(fs["V"], 1)
                    vector.wait_ge(fs["T"], st_h_all(i))
                    vector.tensor_copy(pbs[:], pb)
                    vector.drain()
                    vector.tensor_tensor(wh[:], ph, pbs[:], op=ALU.mult).then_inc(fs["V"], 1)
                    if i >= 2:
                        vector.wait_ge(fs["XWa"] if i % 2 == 0 else fs["XWb"],
                                       par_ld("XWa" if i % 2 == 0 else "XWb", i - 2))
                    for os in range(OS):
                        vector.wait_ge(fs["S"], ss_f(i, os))
                        ins = vector.tensor_tensor(xo[:, os * 512:(os + 1) * 512],
                                                   xt[:, os * 512:(os + 1) * 512],
                                                   fsb[:, os * 512:(os + 1) * 512],
                                                   op=ALU.add)
                        if os == OS - 1:
                            ins.then_inc(fs["V"], 1)
                    if second_ln:
                        vector.drain()
                        for b in range(NB):
                            vector.bn_stats(stats[:, b * 6:(b + 1) * 6], xo[:, b * 512:(b + 1) * 512])
                        vector.drain()
                        vector.bn_aggr(mv[:], stats[:, 0:6 * NB])
                        vector.drain()
                        vector.tensor_scalar(veps[:], mv[:, 1:2], c.eps, None,
                                             op0=ALU.add).then_inc(fs["V"], 1)
                        vector.wait_ge(fs["S"], ss_sqrt2(i))
                        vector.reciprocal(rr[:], sd[:])
                        vector.drain()
                        if i >= 1:
                            vector.wait_ge(fs["T"], st(i, 0))
                        vector.tensor_scalar(ln2t[:], xo, mv[:, 0:1], rr[:],
                                             op0=ALU.subtract, op1=ALU.mult).then_inc(fs["V"], 1)

            @block.scalar
            def _(scalar):
                for i in range(c.TT):
                    scalar.wait_ge(fs["V"], sv_veps(i))
                    scalar.activation(sd[:], veps[:], AF.Sqrt, bias=0.0,
                                      scale=1.0).then_inc(fs["S"], 1)
                    for kc in range(c.DC):
                        scalar.wait_ge(fs["T"], st_tp(i, kc))
                        pt = ptA if kc % 2 == 0 else ptB
                        scalar.copy(lnf[:, kc * 128:(kc + 1) * 128],
                                    pt[:, 0:128]).then_inc(fs["S"], 1)
                    scalar.wait_ge(fs["T"], st_rwT(i))
                    scalar.copy(rwT[:], prw).then_inc(fs["S"], 1)
                    for os in range(OS):
                        scalar.wait_ge(fs["T"], st_b(i, os))
                        scalar.mul(fsb[:, os * 512:(os + 1) * 512],
                                   pf[:, (os % 2) * 512:(os % 2 + 1) * 512],
                                   0.5).then_inc(fs["S"], 1)
                    if second_ln:
                        scalar.wait_ge(fs["V"], sv_veps2(i))
                        scalar.activation(sd[:], veps[:], AF.Sqrt, bias=0.0,
                                          scale=1.0).then_inc(fs["S"], 1)
                        if i >= 2:
                            scalar.wait_ge(fs["LWa"] if i % 2 == 0 else fs["LWb"],
                                           par_ld("LWa" if i % 2 == 0 else "LWb", i - 2))
                        ln2f = ln2f2[:, (i % 2) * c.D:(i % 2 + 1) * c.D]
                        for kc in range(c.DC):
                            scalar.wait_ge(fs["T"], st_ln2tp(i, kc))
                            pt = ptA if kc % 2 == 0 else ptB
                            scalar.copy(ln2f[:, kc * 128:(kc + 1) * 128],
                                        pt[:, 0:128]).then_inc(fs["S"], 1)

            @block.tensor
            def _(tensor):
                for i in range(c.TT):
                    tensor.wait_ge(fs["V"], sv_ln(i))
                    for kc in range(c.DC):
                        pt = ptA if kc % 2 == 0 else ptB
                        if kc >= 2:
                            tensor.wait_ge(fs["S"], ss_lnf(i, kc - 2))
                        elif i > 0:
                            tensor.wait_ge(fs["S"], B["S"] + i * SW)
                        tensor.transpose(pt[:, 0:128], lnt[:, kc * 128:(kc + 1) * 128],
                                         ident[:]).then_inc(fs["T"], 1)
                    for kc in range(c.DC):
                        tensor.wait_ge(fs["S"], ss_lnf(i, kc))
                        tensor.matmul(pr, lnf[:, kc * 128:(kc + 1) * 128],
                                      routeT[:, kc * c.E:(kc + 1) * c.E],
                                      start=(kc == 0), stop=(kc == c.DC - 1)).then_inc(fs["T"], 1)
                    tensor.wait_ge(fs["V"], sv_rw(i))
                    tensor.transpose(prw, rw, ident[:]).then_inc(fs["T"], 1)
                    tensor.wait_ge(fs["S"], ss_rwT(i))
                    tensor.matmul(pb, m8[:], rwT[:], start=True, stop=True).then_inc(fs["T"], 1)
                    for kc in range(c.DC):
                        tensor.matmul(ph, a2T[:, kc * c.ER:(kc + 1) * c.ER],
                                      lnf[:, kc * 128:(kc + 1) * 128],
                                      start=(kc == 0), stop=(kc == c.DC - 1)).then_inc(fs["T"], 1)
                    tensor.wait_ge(fs["V"], sv_wh(i))
                    for os in range(OS):
                        if os >= 2:
                            tensor.wait_ge(fs["S"], ss_f(i, os - 2))
                        elif i > 0:
                            tensor.wait_ge(fs["S"], ss_f(i - 1, OS - 1))
                        tensor.matmul(pf[:, (os % 2) * 512:(os % 2 + 1) * 512],
                                      wh[:], b2[:, os * 512:(os + 1) * 512],
                                      start=True, stop=True).then_inc(fs["T"], 1)
                    if second_ln:
                        tensor.wait_ge(fs["V"], sv_ln2(i))
                        for kc in range(c.DC):
                            pt = ptA if kc % 2 == 0 else ptB
                            if kc >= 2:
                                tensor.wait_ge(fs["S"], ss_ln2f(i, kc - 2))
                            else:
                                tensor.wait_ge(fs["S"], ss_f(i, OS - 1))
                            tensor.transpose(pt[:, 0:128], ln2t[:, kc * 128:(kc + 1) * 128],
                                             ident[:]).then_inc(fs["T"], 1)

            @block.gpsimd
            def _(gpsimd):
                for i in range(c.TT):
                    gpsimd.wait_ge(fs["V"], sv_xo(i))
                    sw = fs["XWa"] if i % 2 == 0 else fs["XWb"]
                    gpsimd.dma_start(out=dst_dram[i * 128:(i + 1) * 128, :],
                                     in_=xo2[:, (i % 2) * c.D:(i % 2 + 1) * c.D]).then_inc(sw, 16)
                    if second_ln:
                        gpsimd.wait_ge(fs["S"], B["S"] + (i + 1) * SW)
                        sl = fs["LWa"] if i % 2 == 0 else fs["LWb"]
                        dst = bass.AP(ln2_dst, i * 128,
                                      [[c.T, 128], [128 * c.T, c.DC], [1, 128]])
                        gpsimd.dma_start(out=dst,
                                         in_=ln2f2[:, (i % 2) * c.D:(i % 2 + 1) * c.D]).then_inc(sl, 16)
                na = (c.TT + 1) // 2
                nb = c.TT // 2
                gpsimd.wait_ge(fs["XWa"], B["XWa"] + 16 * na)
                gpsimd.wait_ge(fs["XWb"], B["XWb"] + 16 * nb)
                if second_ln:
                    gpsimd.wait_ge(fs["LWa"], B["LWa"] + 16 * na)
                    gpsimd.wait_ge(fs["LWb"], B["LWb"] + 16 * nb)

        na = (c.TT + 1) // 2
        nb = c.TT // 2
        return {"FW": B["FW"] + 32, "Xa": B["Xa"] + 16 * na, "Xb": B["Xb"] + 16 * nb,
                "S": B["S"] + c.TT * SW, "T": B["T"] + c.TT * TW, "V": B["V"] + c.TT * VW,
                "XWa": B["XWa"] + 16 * na, "XWb": B["XWb"] + 16 * nb,
                "LWa": B["LWa"] + (16 * na if second_ln else 0),
                "LWb": B["LWb"] + (16 * nb if second_ln else 0)}

    with stack:
        with nc.Block() as block:
            @block.sync
            def _(sync):
                sync.dma_start(out=ident[:], in_=ident_d[:]).then_inc(sLd, 16)
                sync.dma_start(out=m8[:], in_=m8_d[:]).then_inc(sLd, 16)
                sync.dma_start(out=dmask[:], in_=dmask_d[:]).then_inc(sLd, 16)
                src = bass.AP(routeT_d, 0, [[c.E, 128], [128 * c.E, c.DC], [1, c.E]])
                sync.dma_start(out=routeT[:], in_=src).then_inc(sLd, 16)
                sync.wait_ge(sLd, 64)

        B0 = {n: 0 for n in fs}
        if 1 in c.phases:
            B1 = ffn_phase("p1", x_in, x1_buf.ap(), True, ln2_buf, B0, [(sLd, 64)])
        else:
            B1 = B0

        if 2 in c.phases:
            NOC = c.QKO
            NTB = c.T // 512
            NVS = c.VO // 512
            sRope = new_sem("sRope")
            sLn2l = new_sem("sLn2l")
            sWla, sWlb = new_sem("sWla"), new_sem("sWlb")
            sWva, sWvb = new_sem("sWva"), new_sem("sWvb")
            sT2 = new_sem("sT2")
            sE2 = new_sem("sE2")
            sR2 = new_sem("sR2")
            sQwa, sQwb = new_sem("sQwa"), new_sem("sQwb")
            sVe = new_sem("sVe")
            sVwa, sVwb = new_sem("sVwa"), new_sem("sVwb")
            sT3 = new_sem("sT3")
            sCC = new_sem("sCC")
            with contextlib.ExitStack() as est:
                en = est.enter_context
                ln2f_all = en(nc.sbuf_tensor("p2ln2f", [128, c.DC * c.T], F32))
                wbuf = en(nc.sbuf_tensor("p2w", [128, 2 * 128], F32))
                wvbuf = en(nc.sbuf_tensor("p2wv", [128, 2 * 512], F32))
                qh = en(nc.sbuf_tensor("p2qh", [128, c.T], F32))
                qsw = en(nc.sbuf_tensor("p2qsw", [128, c.T], F32))
                t1b = en(nc.sbuf_tensor("p2t1", [128, c.T], F32))
                t2b = en(nc.sbuf_tensor("p2t2", [128, c.T], F32))
                rob = en(nc.sbuf_tensor("p2ro", [128, 2 * c.T], F32))
                cq = en(nc.sbuf_tensor("p2cq", [128, c.T], F32))
                sq = en(nc.sbuf_tensor("p2sq", [128, c.T], F32))
                ck = en(nc.sbuf_tensor("p2ck", [128, c.T], F32))
                sk = en(nc.sbuf_tensor("p2sk", [128, c.T], F32))
                vsb = en(nc.sbuf_tensor("p2v", [128, 2 * 512], F32))
                pbig = en(nc.psum_tensor("p2ps", [128, 8 * 512], F32))
                block = en(nc.Block())

                @block.sync
                def _(sync):
                    sync.dma_start(out=cq[:], in_=ropeq_c_d[:]).then_inc(sRope, 16)
                    sync.dma_start(out=sq[:], in_=ropeq_s_d[:]).then_inc(sRope, 16)
                    sync.dma_start(out=ck[:], in_=ropek_c_d[:]).then_inc(sRope, 16)
                    sync.dma_start(out=sk[:], in_=ropek_s_d[:]).then_inc(sRope, 16)
                    if 1 in c.phases:
                        sync.wait_ge(fs["LWa"], B1["LWa"])
                        sync.wait_ge(fs["LWb"], B1["LWb"])
                    q4 = c.DC // 4
                    for p in range(4):
                        src = bass.AP(ln2_buf, p * q4 * 128 * c.T,
                                      [[c.T, 128], [128 * c.T, q4], [1, c.T]])
                        sync.dma_start(out=ln2f_all[:, p * q4 * c.T:(p + 1) * q4 * c.T],
                                       in_=src).then_inc(sLn2l, 16)
                    for oc in range(NOC):
                        for kc in range(c.DC):
                            idx = oc * c.DC + kc
                            if idx >= 2:
                                sync.wait_ge(sT2, idx - 1)
                            sw = sWla if idx % 2 == 0 else sWlb
                            sync.dma_start(out=wbuf[:, (idx % 2) * 128:(idx % 2 + 1) * 128],
                                           in_=wqkvT[kc * 128:(kc + 1) * 128,
                                                     oc * 128:(oc + 1) * 128]).then_inc(sw, 16)
                    for os in range(NVS):
                        for kc in range(c.DC):
                            idx = os * c.DC + kc
                            if idx >= 2:
                                sync.wait_ge(sT3, idx - 1)
                            sw = sWva if idx % 2 == 0 else sWvb
                            co = c.QO + c.KO + os * 512
                            sync.dma_start(out=wvbuf[:, (idx % 2) * 512:(idx % 2 + 1) * 512],
                                           in_=wqkvT[kc * 128:(kc + 1) * 128,
                                                     co:co + 512]).then_inc(sw, 16)

                @block.tensor
                def _(tensor):
                    tensor.wait_ge(sLn2l, 64)
                    for oc in range(NOC):
                        for kc in range(c.DC):
                            idx = oc * c.DC + kc
                            tensor.wait_ge(sWla if idx % 2 == 0 else sWlb,
                                           16 * (idx // 2 + 1))
                            if kc == 0 and oc >= 2:
                                tensor.wait_ge(sE2, oc - 1)
                            w = wbuf[:, (idx % 2) * 128:(idx % 2 + 1) * 128]
                            for tb in range(NTB):
                                po = ((oc % 2) * NTB + tb) * 512
                                ins = tensor.matmul(pbig[:, po:po + 512], w,
                                                    ln2f_all[:, kc * c.T + tb * 512:kc * c.T + (tb + 1) * 512],
                                                    start=(kc == 0), stop=(kc == c.DC - 1))
                            ins.then_inc(sT2, 1)
                    for os in range(NVS):
                        for kc in range(c.DC):
                            idx = os * c.DC + kc
                            tensor.wait_ge(sWva if idx % 2 == 0 else sWvb,
                                           16 * (idx // 2 + 1))
                            if idx == 0:
                                tensor.wait_ge(sE2, NOC)
                            if kc == 0 and os >= 1:
                                tensor.wait_ge(sVe, os * c.TT)
                            w = wvbuf[:, (idx % 2) * 512:(idx % 2 + 1) * 512]
                            for tt in range(c.TT):
                                ins = tensor.matmul(pbig[:, tt * 512:(tt + 1) * 512],
                                                    ln2f_all[:, kc * c.T + tt * 128:kc * c.T + tt * 128 + 128],
                                                    w, start=(kc == 0), stop=(kc == c.DC - 1))
                            ins.then_inc(sT3, 1)

                @block.scalar
                def _(scalar):
                    for oc in range(NOC):
                        scalar.wait_ge(sT2, (oc + 1) * c.DC)
                        if oc >= 1:
                            scalar.wait_ge(sR2, oc)
                        for tb in range(NTB):
                            po = ((oc % 2) * NTB + tb) * 512
                            ins = scalar.copy(qh[:, tb * 512:(tb + 1) * 512], pbig[:, po:po + 512])
                        ins.then_inc(sE2, 1)
                    for os in range(NVS):
                        scalar.wait_ge(sT3, (os + 1) * c.DC)
                        for tt in range(c.TT):
                            idx = os * c.TT + tt
                            if idx >= 2:
                                scalar.wait_ge(sVwa if idx % 2 == 0 else sVwb,
                                               16 * ((idx - 2) // 2 + 1))
                            scalar.copy(vsb[:, (idx % 2) * 512:(idx % 2 + 1) * 512],
                                        pbig[:, tt * 512:(tt + 1) * 512]).then_inc(sVe, 1)

                @block.vector
                def _(vector):
                    H2 = c.HD // 2
                    vector.wait_ge(sRope, 64)
                    for oc in range(NOC):
                        cx, sx = (cq, sq) if oc < c.HQ else (ck, sk)
                        vector.wait_ge(sE2, oc + 1)
                        if oc >= 2:
                            vector.wait_ge(sQwa if oc % 2 == 0 else sQwb,
                                           16 * ((oc - 2) // 2 + 1))
                        ro = rob[:, (oc % 2) * c.T:(oc % 2 + 1) * c.T]
                        vector.tensor_copy(qsw[0:H2, :], qh[H2:2 * H2, :])
                        vector.tensor_tensor(t1b[:], qh[:], cx[:], op=ALU.mult)
                        vector.tensor_copy(qsw[H2:2 * H2, :], qh[0:H2, :])
                        vector.drain()
                        vector.tensor_tensor(t2b[0:H2, :], qsw[0:H2, :], sx[0:H2, :], op=ALU.mult)
                        vector.tensor_tensor(t2b[H2:2 * H2, :], qsw[H2:2 * H2, :], sx[H2:2 * H2, :], op=ALU.mult)
                        vector.drain()
                        vector.tensor_tensor(ro[0:H2, :], t1b[0:H2, :], t2b[0:H2, :], op=ALU.add)
                        vector.tensor_tensor(ro[H2:2 * H2, :], t1b[H2:2 * H2, :],
                                             t2b[H2:2 * H2, :], op=ALU.add).then_inc(sR2, 1)

                @block.gpsimd
                def _(gpsimd):
                    for oc in range(NOC):
                        gpsimd.wait_ge(sR2, oc + 1)
                        ro = rob[:, (oc % 2) * c.T:(oc % 2 + 1) * c.T]
                        sw = sQwa if oc % 2 == 0 else sQwb
                        if oc < c.HQ:
                            gpsimd.dma_start(out=q_buf.ap()[oc * 128:(oc + 1) * 128, :],
                                             in_=ro).then_inc(sw, 16)
                        else:
                            ko = oc - c.HQ
                            gpsimd.dma_start(out=agk_in.ap()[ko * 128:(ko + 1) * 128, :],
                                             in_=ro).then_inc(sw, 16)
                    for os in range(NVS):
                        for tt in range(c.TT):
                            idx = os * c.TT + tt
                            gpsimd.wait_ge(sVe, idx + 1)
                            sw = sVwa if idx % 2 == 0 else sVwb
                            gpsimd.dma_start(
                                out=agv_in.ap()[tt * 128:(tt + 1) * 128, os * 512:(os + 1) * 512],
                                in_=vsb[:, (idx % 2) * 512:(idx % 2 + 1) * 512]).then_inc(sw, 16)
                    gpsimd.wait_ge(sQwa, 16 * ((NOC + 1) // 2))
                    gpsimd.wait_ge(sQwb, 16 * (NOC // 2))
                    nvt = NVS * c.TT
                    gpsimd.wait_ge(sVwa, 16 * ((nvt + 1) // 2))
                    gpsimd.wait_ge(sVwb, 16 * (nvt // 2))
                    ng = c.n_cores // c.RG
                    rg = [[g * c.RG + r for r in range(c.RG)] for g in range(ng)]
                    gpsimd.collective_compute("AllGather", ALU.bypass, replica_groups=rg,
                                              ins=[agk_in.ap().opt()],
                                              outs=[agk_out.ap().opt()]).then_inc(sCC, 1)
                    gpsimd.collective_compute("AllGather", ALU.bypass, replica_groups=rg,
                                              ins=[agv_in.ap().opt()],
                                              outs=[agv_out.ap().opt()]).then_inc(sCC, 1)
                    gpsimd.wait_ge(sCC, 2)

        if 3 in c.phases:
            KW = c.RG * 128
            tiles = []
            kblocks = []
            for g in range(c.HKV):
                for qh_ in range(c.G4):
                    for u in range(c.TT):
                        j = len(tiles)
                        k0 = len(kblocks)
                        for t in range(u + 1):
                            kblocks.append((g, qh_, u, t, j))
                        tiles.append((g, qh_, u, k0, k0 + u))
            NT = len(tiles)
            NK = len(kblocks)
            sKVa, sKVb = new_sem("sKVa"), new_sem("sKVb")
            sQla, sQlb = new_sem("sQla"), new_sem("sQlb")
            sSb = new_sem("sSb")
            sSe = new_sem("sSe")
            sMsk = new_sem("sMsk")
            sNm = new_sem("sNm")
            sPx = new_sem("sPx")
            sLi = new_sem("sLi")
            sPT = new_sem("sPT")
            sPTe = new_sem("sPTe")
            sPV = new_sem("sPV")
            sVfin = new_sem("sVfin")
            sAtp = new_sem("sAtp")
            sAe = new_sem("sAe")
            sAwa, sAwb = new_sem("sAwa"), new_sem("sAwb")

            with contextlib.ExitStack() as est:
                en = est.enter_context
                Kg2 = en(nc.sbuf_tensor("p3K", [128, 2 * c.RG * c.T], F32))
                Vg2 = en(nc.sbuf_tensor("p3V", [128, 2 * c.RG * c.T], F32))
                qt2 = en(nc.sbuf_tensor("p3q", [128, 2 * 128], F32))
                Srow2 = en(nc.sbuf_tensor("p3S", [128, 3 * c.RG * c.T], F32))
                PTsb = en(nc.sbuf_tensor("p3PT", [128, 2 * KW], F32))
                out2 = en(nc.sbuf_tensor("p3out", [128, 2 * 128], F32))
                af2 = en(nc.sbuf_tensor("p3af", [128, 2 * 128], F32))
                mt2 = en(nc.sbuf_tensor("p3mt", [128, 3 * c.TT], F32))
                nm2 = en(nc.sbuf_tensor("p3nm", [128, 3], F32))
                lr2 = en(nc.sbuf_tensor("p3lr", [128, 3], F32))
                li2 = en(nc.sbuf_tensor("p3li", [128, 3], F32))
                pS = en(nc.psum_tensor("p3Sp", [128, 2 * KW], F32))
                pPT = en(nc.psum_tensor("p3PTp", [128, 2 * KW], F32))
                ppvb = en(nc.psum_tensor("p3pv", [128, 2 * 512], F32))
                ptpb = en(nc.psum_tensor("p3tp", [128, 512], F32))
                block = en(nc.Block())

                ptp = ptpb[:, 0:128]

                def ppv(par):
                    return ppvb[:, par * 512:par * 512 + 128]

                @block.sync
                def _(sync):
                    if 2 in c.phases:
                        sync.wait_ge(sCC, 2)
                    for g in range(c.HKV):
                        skv = sKVa if g % 2 == 0 else sKVb
                        if g >= 2:
                            sync.wait_ge(sSb, tiles[(g - 1) * c.G4 * c.TT - 1][4] + 1)
                            sync.wait_ge(sPV, tiles[(g - 1) * c.G4 * c.TT - 1][4] + 1)
                        srcK = bass.AP(agk_out, g * 128 * c.T,
                                       [[c.T, 128], [c.KO * c.T, c.RG], [1, c.T]])
                        sync.dma_start(out=Kg2[:, (g % 2) * c.RG * c.T:(g % 2 + 1) * c.RG * c.T],
                                       in_=srcK).then_inc(skv, 16)
                        srcV = bass.AP(agv_out, g * 128,
                                       [[c.VO, 128], [c.T * c.VO, c.RG],
                                        [128 * c.VO, c.TT], [1, 128]])
                        sync.dma_start(out=Vg2[:, (g % 2) * c.RG * c.T:(g % 2 + 1) * c.RG * c.T],
                                       in_=srcV).then_inc(skv, 16)
                        for qh_ in range(c.G4):
                            h = g * c.G4 + qh_
                            for u in range(c.TT):
                                j = (g * c.G4 + qh_) * c.TT + u
                                if j >= 2:
                                    sync.wait_ge(sSb, tiles[j - 2][4] + 1)
                                sq_ = sQla if j % 2 == 0 else sQlb
                                sync.dma_start(out=qt2[:, (j % 2) * 128:(j % 2 + 1) * 128],
                                               in_=q_buf.ap()[h * 128:(h + 1) * 128,
                                                              u * 128:(u + 1) * 128]).then_inc(sq_, 16)

                @block.tensor
                def _(tensor):
                    def emit_passB(jm):
                        g, qh_, u, k0, ke = tiles[jm]
                        js = jm % 3
                        jp = jm % 2
                        Prow = Srow2[:, js * c.RG * c.T:js * c.RG * c.T + (u + 1) * KW]
                        Vg = Vg2[:, (g % 2) * c.RG * c.T:(g % 2 + 1) * c.RG * c.T]
                        tensor.wait_ge(sPx, jm + 1)
                        if jm >= 2:
                            tensor.wait_ge(sVfin, jm - 1)

                        def tp(t):
                            k = k0 + t
                            par = k % 2
                            if k >= 2:
                                tensor.wait_ge(sPTe, k - 1)
                            for r in range(c.RG):
                                ins = tensor.transpose(
                                    pPT[:, par * KW + r * 128:par * KW + (r + 1) * 128],
                                    Prow[:, t * KW + r * 128:t * KW + (r + 1) * 128],
                                    ident[:])
                            ins.then_inc(sPT, 1)

                        def pv(t):
                            k = k0 + t
                            par = k % 2
                            tensor.wait_ge(sPTe, k + 1)
                            for r in range(c.RG):
                                ins = tensor.matmul(
                                    ppv(jp),
                                    PTsb[:, par * KW + r * 128:par * KW + (r + 1) * 128],
                                    Vg[:, (r * c.TT + t) * 128:(r * c.TT + t + 1) * 128],
                                    start=(t == 0 and r == 0), stop=(t == u and r == c.RG - 1))
                            ins.then_inc(sPV, 1)

                        tp(0)
                        for t in range(1, u + 1):
                            tp(t)
                            pv(t - 1)
                        pv(u)

                    def emit_ttp(jm):
                        tensor.wait_ge(sVfin, jm + 1)
                        if jm >= 1:
                            tensor.wait_ge(sAe, jm)
                        tensor.transpose(ptp, out2[:, (jm % 2) * 128:(jm % 2 + 1) * 128],
                                         ident[:]).then_inc(sAtp, 1)

                    for j, (g, qh_, u, k0, ke) in enumerate(tiles):
                        qt = qt2[:, (j % 2) * 128:(j % 2 + 1) * 128]
                        for t in range(u + 1):
                            k = k0 + t
                            par = k % 2
                            if k >= 2:
                                tensor.wait_ge(sSe, k - 1)
                            if t == 0:
                                tensor.wait_ge(sQla if j % 2 == 0 else sQlb,
                                               16 * (j // 2 + 1))
                                tensor.wait_ge(sKVa if g % 2 == 0 else sKVb,
                                               32 * (g // 2 + 1))
                            rhs = bass.AP(Kg2, (g % 2) * c.RG * c.T + t * 128,
                                          [[2 * c.RG * c.T, 128], [c.T, c.RG], [1, 128]])
                            tensor.matmul(pS[:, par * KW:(par + 1) * KW], qt, rhs,
                                          start=True, stop=True).then_inc(sSb, 1)
                        if j >= 1:
                            emit_passB(j - 1)
                            emit_ttp(j - 1)
                    emit_passB(NT - 1)
                    emit_ttp(NT - 1)

                @block.scalar
                def _(scalar):
                    def emit_sae(jm):
                        scalar.wait_ge(sAtp, jm + 1)
                        if jm >= 2:
                            scalar.wait_ge(sAwa if jm % 2 == 0 else sAwb,
                                           16 * ((jm - 2) // 2 + 1))
                        scalar.copy(af2[:, (jm % 2) * 128:(jm % 2 + 1) * 128],
                                    ptp).then_inc(sAe, 1)

                    for j, (g, qh_, u, k0, ke) in enumerate(tiles):
                        js = j % 3
                        Srow = Srow2[:, js * c.RG * c.T:js * c.RG * c.T + (u + 1) * KW]
                        for t in range(u + 1):
                            k = k0 + t
                            par = k % 2
                            scalar.wait_ge(sSb, k + 1)
                            if j >= 3 and t == 0:
                                scalar.wait_ge(sPT, tiles[j - 3][4] + 1)
                            scalar.copy(Srow[:, t * KW:(t + 1) * KW],
                                        pS[:, par * KW:(par + 1) * KW]).then_inc(sSe, 1)
                        if j >= 1:
                            jm = j - 1
                            gm, qm, um, km0, kme = tiles[jm]
                            for t in range(um + 1):
                                k = km0 + t
                                par = k % 2
                                scalar.wait_ge(sPT, k + 1)
                                if k >= 2:
                                    scalar.wait_ge(sPV, k - 1)
                                scalar.copy(PTsb[:, par * KW:(par + 1) * KW],
                                            pPT[:, par * KW:(par + 1) * KW]).then_inc(sPTe, 1)
                            emit_sae(jm)
                        scalar.wait_ge(sNm, j + 1)
                        scalar.activation(Srow[:], Srow[:], AF.Exp,
                                          bias=nm2[:, js:js + 1], scale=1.0,
                                          accum_out=lr2[:, js:js + 1]).then_inc(sPx, 1)
                    jm = NT - 1
                    gm, qm, um, km0, kme = tiles[jm]
                    for t in range(um + 1):
                        k = km0 + t
                        par = k % 2
                        scalar.wait_ge(sPT, k + 1)
                        if k >= 2:
                            scalar.wait_ge(sPV, k - 1)
                        scalar.copy(PTsb[:, par * KW:(par + 1) * KW],
                                    pPT[:, par * KW:(par + 1) * KW]).then_inc(sPTe, 1)
                    emit_sae(jm)

                @block.gpsimd
                def _(gpsimd):
                    for jm, (gg, qq, uu, _k0, _ke) in enumerate(tiles):
                        h = gg * c.G4 + qq
                        gpsimd.wait_ge(sAe, jm + 1)
                        sw = sAwa if jm % 2 == 0 else sAwb
                        gpsimd.dma_start(
                            out=attn_buf.ap()[h * 128:(h + 1) * 128, uu * 128:(uu + 1) * 128],
                            in_=af2[:, (jm % 2) * 128:(jm % 2 + 1) * 128]).then_inc(sw, 16)
                    gpsimd.wait_ge(sAwa, 16 * ((NT + 1) // 2))
                    gpsimd.wait_ge(sAwb, 16 * (NT // 2))

                @block.vector
                def _(vector):
                    for j, (g, qh_, u, k0, ke) in enumerate(tiles):
                        js = j % 3
                        jp = j % 2
                        Srow = Srow2[:, js * c.RG * c.T:js * c.RG * c.T + (u + 1) * KW]
                        mt = mt2[:, js * c.TT:js * c.TT + c.TT]
                        for t in range(u):
                            vector.wait_ge(sSe, k0 + t + 1)
                            vector.reduce_max(mt[:, t:t + 1], Srow[:, t * KW:(t + 1) * KW],
                                              axis=AX)
                        vector.wait_ge(sSe, ke + 1)
                        vector.tensor_tensor(Srow[:, u * KW:(u + 1) * KW],
                                             Srow[:, u * KW:(u + 1) * KW],
                                             dmask[:], op=ALU.add).then_inc(sMsk, 1)
                        vector.drain()
                        vector.reduce_max(mt[:, u:u + 1], Srow[:, u * KW:(u + 1) * KW],
                                          axis=AX)
                        vector.drain()
                        vector.reduce_max(nm2[:, js:js + 1], mt[:, 0:u + 1], axis=AX,
                                          negate=True).then_inc(sNm, 1)
                        vector.wait_ge(sPx, j + 1)
                        vector.reciprocal(li2[:, js:js + 1], lr2[:, js:js + 1])
                        vector.wait_ge(sPV, ke + 1)
                        if j >= 2:
                            vector.wait_ge(sAtp, j - 1)
                        vector.drain()
                        vector.tensor_scalar(out2[:, jp * 128:(jp + 1) * 128], ppv(jp),
                                             li2[:, js:js + 1], None,
                                             op0=ALU.mult).then_inc(sVfin, 1)

        if 4 in c.phases:
            OS4 = c.D // 512
            sWoa, sWob = new_sem("sWoa"), new_sem("sWob")
            sX1r = [new_sem("sX1r%d" % i) for i in range(4)]
            sT4 = new_sem("sT4")
            sXe = new_sem("sXe")
            sXna, sXnb = new_sem("sXna"), new_sem("sXnb")
            sAl = new_sem("sAl")
            with contextlib.ExitStack() as est:
                en = est.enter_context
                attn_f = en(nc.sbuf_tensor("p4attn", [128, c.HQ * c.T], F32))
                wob = en(nc.sbuf_tensor("p4w", [128, 2 * 512], F32))
                x1b = en(nc.sbuf_tensor("p4x1", [128, 4 * 512], F32))
                xnb = en(nc.sbuf_tensor("p4xn", [128, 2 * 512], F32))
                ps8 = en(nc.psum_tensor("p4ps", [128, 8 * 512], F32))
                block = en(nc.Block())

                @block.sync
                def _(sync):
                    if 3 in c.phases:
                        sync.wait_ge(sAwa, 16 * ((NT + 1) // 2))
                        sync.wait_ge(sAwb, 16 * (NT // 2))
                    q4 = c.HQ // 4
                    for p in range(4):
                        srcA = bass.AP(attn_buf, p * q4 * 128 * c.T,
                                       [[c.T, 128], [128 * c.T, q4], [1, c.T]])
                        sync.dma_start(out=attn_f[:, p * q4 * c.T:(p + 1) * q4 * c.T],
                                       in_=srcA).then_inc(sAl, 16)
                    for os in range(OS4):
                        for kc in range(c.HQ):
                            idx = os * c.HQ + kc
                            if idx >= 2:
                                sync.wait_ge(sT4, idx - 1)
                            sw = sWoa if idx % 2 == 0 else sWob
                            sync.dma_start(out=wob[:, (idx % 2) * 512:(idx % 2 + 1) * 512],
                                           in_=woT[kc * 128:(kc + 1) * 128,
                                                   os * 512:(os + 1) * 512]).then_inc(sw, 16)

                @block.gpsimd
                def _(gpsimd):
                    if 1 in c.phases:
                        gpsimd.wait_ge(fs["XWa"], B1["XWa"])
                        gpsimd.wait_ge(fs["XWb"], B1["XWb"])
                    tot = OS4 * c.TT

                    def x1_load(idx):
                        os_, tt_ = idx // c.TT, idx % c.TT
                        gpsimd.dma_start(out=x1b[:, (idx % 4) * 512:(idx % 4 + 1) * 512],
                                         in_=x1_buf.ap()[tt_ * 128:(tt_ + 1) * 128,
                                                         os_ * 512:(os_ + 1) * 512]).then_inc(sX1r[idx % 4], 16)

                    for idx in range(min(4, tot)):
                        x1_load(idx)
                    for idx in range(tot):
                        os_, tt_ = idx // c.TT, idx % c.TT
                        gpsimd.wait_ge(sXe, idx + 1)
                        if idx + 4 < tot:
                            x1_load(idx + 4)
                        sw = sXna if idx % 2 == 0 else sXnb
                        gpsimd.dma_start(out=xn_buf.ap()[tt_ * 128:(tt_ + 1) * 128,
                                                         os_ * 512:(os_ + 1) * 512],
                                         in_=xnb[:, (idx % 2) * 512:(idx % 2 + 1) * 512]).then_inc(sw, 16)
                    gpsimd.wait_ge(sXna, 16 * ((tot + 1) // 2))
                    gpsimd.wait_ge(sXnb, 16 * (tot // 2))

                @block.tensor
                def _(tensor):
                    tensor.wait_ge(sAl, 64)
                    for os in range(OS4):
                        for kc in range(c.HQ):
                            idx = os * c.HQ + kc
                            tensor.wait_ge(sWoa if idx % 2 == 0 else sWob,
                                           16 * (idx // 2 + 1))
                            if kc == 0 and os >= 1:
                                tensor.wait_ge(sXe, os * c.TT)
                            w = wob[:, (idx % 2) * 512:(idx % 2 + 1) * 512]
                            for tt in range(c.TT):
                                ins = tensor.matmul(ps8[:, tt * 512:(tt + 1) * 512],
                                                    attn_f[:, kc * c.T + tt * 128:kc * c.T + tt * 128 + 128],
                                                    w, start=(kc == 0), stop=(kc == c.HQ - 1))
                            ins.then_inc(sT4, 1)

                @block.vector
                def _(vector):
                    for os in range(OS4):
                        vector.wait_ge(sT4, (os + 1) * c.HQ)
                        for tt in range(c.TT):
                            idx = os * c.TT + tt
                            vector.wait_ge(sX1r[idx % 4], 16 * (idx // 4 + 1))
                            if idx >= 2:
                                vector.wait_ge(sXna if idx % 2 == 0 else sXnb,
                                               16 * ((idx - 2) // 2 + 1))
                            vector.tensor_tensor(xnb[:, (idx % 2) * 512:(idx % 2 + 1) * 512],
                                                 x1b[:, (idx % 4) * 512:(idx % 4 + 1) * 512],
                                                 ps8[:, tt * 512:(tt + 1) * 512],
                                                 op=ALU.add).then_inc(sXe, 1)

        if 5 in c.phases:
            entry = []
            if 4 in c.phases:
                tot = (c.D // 512) * c.TT
                entry = [(sXna, 16 * ((tot + 1) // 2)), (sXnb, 16 * (tot // 2))]
            ffn_phase("p5", xn_buf.ap(), y_out, False, None, B1, entry)

    return nc


def _host_prep(cfg):
    c = cfg
    ident = np.eye(128, dtype=np.float32)
    m8 = np.zeros((c.E, c.ER), dtype=np.float32)
    for e in range(c.E):
        m8[e, e * c.R:(e + 1) * c.R] = 1.0
    per_core = []
    for core in range(c.n_cores):
        g, l = core // c.RG, core % c.RG
        pos = (l + c.RG * np.arange(c.T)).astype(np.float64)
        inv = 1.0 / (c.theta ** (np.arange(0, c.HD, 2, dtype=np.float64) / c.HD))
        fr = np.outer(pos, inv)
        cos = np.cos(fr).T
        sin = np.sin(fr).T
        scale = 1.0 / math.sqrt(c.HD)
        ropeq_c = (np.concatenate([cos, cos], axis=0) * scale).astype(np.float32)
        ropeq_s = (np.concatenate([-sin, sin], axis=0) * scale).astype(np.float32)
        ropek_c = np.concatenate([cos, cos], axis=0).astype(np.float32)
        ropek_s = np.concatenate([-sin, sin], axis=0).astype(np.float32)
        i_ = np.arange(128)[:, None]
        j_ = np.arange(128)[None, :]
        dm = np.zeros((128, c.RG * 128), dtype=np.float32)
        for r in range(c.RG):
            d = 1 if r > l else 0
            dm[:, r * 128:(r + 1) * 128] = np.where(i_ >= j_ + d, 0.0, NEG).astype(np.float32)
        per_core.append({"ident": ident, "m8": m8,
                         "ropeq_c": np.ascontiguousarray(ropeq_c),
                         "ropeq_s": np.ascontiguousarray(ropeq_s),
                         "ropek_c": np.ascontiguousarray(ropek_c),
                         "ropek_s": np.ascontiguousarray(ropek_s),
                         "dmask": dm})
    return per_core


def make_in_maps(cfg, x, wqkv, wo, route, A, Bw):
    c = cfg
    wqkvT = np.ascontiguousarray(np.asarray(wqkv).T)
    woT = np.ascontiguousarray(np.asarray(wo).T)
    routeT = np.ascontiguousarray(np.asarray(route).T)
    a2T = np.ascontiguousarray(np.asarray(A).reshape(c.E * c.R, c.D).T)
    b2 = np.ascontiguousarray(np.transpose(np.asarray(Bw), (0, 2, 1)).reshape(c.E * c.R, c.D))
    consts = _host_prep(c)
    in_maps = []
    for core in range(c.n_cores):
        g, l = core // c.RG, core % c.RG
        x_own = np.ascontiguousarray(np.asarray(x)[g, l::c.RG, :], dtype=np.float32)
        m = dict(consts[core])
        m.update({"x_own": x_own, "wqkvT": wqkvT, "woT": woT, "routeT": routeT,
                  "a2T": a2T, "b2": b2})
        in_maps.append(m)
    return in_maps


def kernel(L0_attn_hidden_states, L0_attn_Wqkv, L0_attn_Wo,
           L0_ffn_route_w, L0_ffn_A, L0_ffn_Bw):
    cfg = Cfg()
    in_maps = make_in_maps(cfg, L0_attn_hidden_states, L0_attn_Wqkv, L0_attn_Wo,
                           L0_ffn_route_w, L0_ffn_A, L0_ffn_Bw)
    nc = build_nc(cfg)
    res = run_bass_kernel_spmd(nc, in_maps, core_ids=list(range(cfg.n_cores)))
    out = np.empty((cfg.n_cores // cfg.RG, cfg.S, cfg.D), dtype=np.float32)
    for core in range(cfg.n_cores):
        g, l = core // cfg.RG, core % cfg.RG
        out[g, l::cfg.RG, :] = res.results[core]["y"]
    return out


# version 06
# speedup vs baseline: 1.1070x; 1.0101x over previous
import math
import contextlib
import numpy as np
import concourse.bass as bass
import concourse.mybir as mybir
from concourse.bass_utils import run_bass_kernel_spmd

F32 = mybir.dt.float32
AF = mybir.ActivationFunctionType
ALU = mybir.AluOpType
AX = mybir.AxisListType.X

NEG = -1.0e30


class Cfg:
    def __init__(self, D=4096, S=4096, HQ=32, HKV=8, HD=128, E=8, R=16, RG=4,
                 eps=1e-5, theta=1e4, phases=(1, 2, 3, 4, 5), n_cores=8):
        self.D, self.S = D, S
        self.HQ, self.HKV, self.HD = HQ, HKV, HD
        self.E, self.R, self.RG = E, R, RG
        self.eps, self.theta = eps, theta
        self.phases = phases
        self.n_cores = n_cores
        self.T = S // RG
        self.TT = self.T // 128
        self.DC = D // 128
        self.ER = E * R
        self.G4 = HQ // HKV
        self.QKO = HQ + HKV
        self.QO = HQ * HD
        self.KO = HKV * HD
        self.VO = HKV * HD
        assert self.HD == 128 and self.ER == 128 and self.T % 128 == 0
        assert self.T % 512 == 0 and D % 512 == 0


def build_nc(cfg):
    c = cfg
    nc = bass.Bass("TRN2", target_bir_lowering=False)

    x_in = nc.dram_tensor("x_own", [c.T, c.D], F32, kind="ExternalInput").ap()
    wqkvT = nc.dram_tensor("wqkvT", [c.D, (c.HQ + 2 * c.HKV) * c.HD], F32, kind="ExternalInput").ap()
    woT = nc.dram_tensor("woT", [c.QO, c.D], F32, kind="ExternalInput").ap()
    routeT_d = nc.dram_tensor("routeT", [c.D, c.E], F32, kind="ExternalInput")
    a2T_d = nc.dram_tensor("a2T", [c.D, c.ER], F32, kind="ExternalInput")
    b2_d = nc.dram_tensor("b2", [c.ER, c.D], F32, kind="ExternalInput").ap()
    ident_d = nc.dram_tensor("ident", [128, 128], F32, kind="ExternalInput").ap()
    m8_d = nc.dram_tensor("m8", [c.E, c.ER], F32, kind="ExternalInput").ap()
    ropeq_c_d = nc.dram_tensor("ropeq_c", [c.HD, c.T], F32, kind="ExternalInput").ap()
    ropeq_s_d = nc.dram_tensor("ropeq_s", [c.HD, c.T], F32, kind="ExternalInput").ap()
    ropek_c_d = nc.dram_tensor("ropek_c", [c.HD, c.T], F32, kind="ExternalInput").ap()
    ropek_s_d = nc.dram_tensor("ropek_s", [c.HD, c.T], F32, kind="ExternalInput").ap()
    dmask_d = nc.dram_tensor("dmask", [128, c.RG * 128], F32, kind="ExternalInput").ap()
    y_out = nc.dram_tensor("y", [c.T, c.D], F32, kind="ExternalOutput").ap()

    x1_buf = nc.dram_tensor("x1_buf", [c.T, c.D], F32)
    ln2_buf = nc.dram_tensor("ln2_buf", [c.D, c.T], F32)
    q_buf = nc.dram_tensor("q_buf", [c.QO, c.T], F32)
    agk_in = nc.dram_tensor("agk_in", [c.KO, c.T], F32)
    agv_in = nc.dram_tensor("agv_in", [c.T, c.VO], F32)
    agk_out = nc.dram_tensor("agk_out", [c.RG * c.KO, c.T], F32)
    agv_out = nc.dram_tensor("agv_out", [c.RG * c.T, c.VO], F32)
    xn_buf = nc.dram_tensor("xn_buf", [c.T, c.D], F32)
    attn_buf = nc.dram_tensor("attn_buf", [c.QO, c.T], F32)

    stack = contextlib.ExitStack()

    def new_sem(name):
        return stack.enter_context(nc.semaphore(name))

    fs = {n: new_sem("f" + n) for n in
          ["FW", "Xa", "Xb", "S", "T", "V", "XWa", "XWb", "LWa", "LWb"]}
    sLd = new_sem("sLd")

    ident = stack.enter_context(nc.sbuf_tensor("c_ident", [128, 128], F32))
    m8 = stack.enter_context(nc.sbuf_tensor("c_m8", [c.E, c.ER], F32))
    dmask = stack.enter_context(nc.sbuf_tensor("c_dmask", [128, c.RG * 128], F32))
    routeT = stack.enter_context(nc.sbuf_tensor("c_routeT", [128, c.DC * c.E], F32))

    def ffn_phase(tag, src_dram, dst_dram, second_ln, ln2_dst, B, entry_waits):
        NB = c.D // 512
        OS = c.D // 512
        SW = 1 + c.DC + 1 + OS + ((1 + c.DC) if second_ln else 0)
        def ss_sqrt1(i): return B["S"] + i * SW + 1
        def ss_lnf(i, kc): return B["S"] + i * SW + 1 + kc + 1
        def ss_rwT(i): return B["S"] + i * SW + 1 + c.DC + 1
        def ss_f(i, os): return B["S"] + i * SW + 1 + c.DC + 1 + os + 1
        def ss_sqrt2(i): return B["S"] + i * SW + 1 + c.DC + 1 + OS + 1
        def ss_ln2f(i, kc): return B["S"] + i * SW + 1 + c.DC + 1 + OS + 1 + kc + 1
        TW = 3 * c.DC + 2 + OS + (c.DC if second_ln else 0)
        def st(i, k): return B["T"] + i * TW + k
        def st_tp(i, kc): return st(i, kc + 1)
        def st_route_all(i): return st(i, 2 * c.DC)
        def st_rwT(i): return st(i, 2 * c.DC + 1)
        def st_h_all(i): return st(i, 3 * c.DC + 2)
        def st_b(i, os): return st(i, 3 * c.DC + 2 + os + 1)
        def st_ln2tp(i, kc): return st(i, 3 * c.DC + 2 + OS + kc + 1)
        VW = 5 + (2 if second_ln else 0)
        def sv_veps(i): return B["V"] + i * VW + 1
        def sv_ln(i): return B["V"] + i * VW + 2
        def sv_rw(i): return B["V"] + i * VW + 3
        def sv_wh(i): return B["V"] + i * VW + 4
        def sv_xo(i): return B["V"] + i * VW + 5
        def sv_veps2(i): return B["V"] + i * VW + 6
        def sv_ln2(i): return B["V"] + i * VW + 7
        def par_ld(nm, i):
            return B[nm] + 16 * (i // 2 + 1)

        with contextlib.ExitStack() as est:
            en = est.enter_context
            a2T = en(nc.sbuf_tensor(tag + "a2T", [128, c.DC * c.ER], F32))
            b2 = en(nc.sbuf_tensor(tag + "b2", [c.ER, c.D], F32))
            xt2 = en(nc.sbuf_tensor(tag + "xt", [128, 2 * c.D], F32))
            stats = en(nc.sbuf_tensor(tag + "stats", [128, 6 * NB], F32))
            mv = en(nc.sbuf_tensor(tag + "mv", [128, 2], F32))
            veps = en(nc.sbuf_tensor(tag + "veps", [128, 1], F32))
            sd = en(nc.sbuf_tensor(tag + "sd", [128, 1], F32))
            rr = en(nc.sbuf_tensor(tag + "rr", [128, 1], F32))
            lnt = en(nc.sbuf_tensor(tag + "lnt", [128, c.D], F32))
            lnf = en(nc.sbuf_tensor(tag + "lnf", [128, c.D], F32))
            rwb = en(nc.sbuf_tensor(tag + "rw", [128, c.E + 1], F32))
            rwT = en(nc.sbuf_tensor(tag + "rwT", [c.E, 128], F32))
            wh = en(nc.sbuf_tensor(tag + "wh", [c.ER, 128], F32))
            pbs = en(nc.sbuf_tensor(tag + "pbs", [c.ER, 128], F32))
            fsb = en(nc.sbuf_tensor(tag + "fsb", [128, (c.D // 512) * 512], F32))
            xo2 = en(nc.sbuf_tensor(tag + "xo", [128, 2 * c.D], F32))
            ln2t = en(nc.sbuf_tensor(tag + "ln2t", [128, c.D], F32))
            ln2f2 = en(nc.sbuf_tensor(tag + "ln2f", [128, 2 * c.D], F32)) if second_ln else None
            ptA = en(nc.psum_tensor(tag + "ptA", [128, 512], F32))
            ptB = en(nc.psum_tensor(tag + "ptB", [128, 512], F32))
            prb = en(nc.psum_tensor(tag + "pr", [128, 512], F32))
            prwb = en(nc.psum_tensor(tag + "prw", [128, 512], F32))
            phb = en(nc.psum_tensor(tag + "ph", [128, 512], F32))
            pbb = en(nc.psum_tensor(tag + "pb", [128, 512], F32))
            pf = en(nc.psum_tensor(tag + "pf", [128, 2 * 512], F32))
            block = en(nc.Block())

            pr = prb[:, 0:c.E]
            prw = prwb[0:c.E, 0:128]
            ph = phb[:, 0:128]
            pb = pbb[:, 0:128]
            mxv = rwb[:, c.E:c.E + 1]
            rw = rwb[:, 0:c.E]

            @block.sync
            def _(sync):
                for s, v in entry_waits:
                    sync.wait_ge(s, v)
                srcA = bass.AP(a2T_d, 0, [[c.ER, 128], [128 * c.ER, c.DC], [1, c.ER]])
                sync.dma_start(out=a2T[:], in_=srcA).then_inc(fs["FW"], 16)
                sync.dma_start(out=b2[:], in_=b2_d[:]).then_inc(fs["FW"], 16)
                for i in range(c.TT):
                    if i >= 2:
                        sync.wait_ge(fs["V"], sv_xo(i - 2))
                    sx = fs["Xa"] if i % 2 == 0 else fs["Xb"]
                    sync.dma_start(out=xt2[:, (i % 2) * c.D:(i % 2 + 1) * c.D],
                                   in_=src_dram[i * 128:(i + 1) * 128, :]).then_inc(sx, 16)

            @block.vector
            def _(vector):
                for i in range(c.TT):
                    xt = xt2[:, (i % 2) * c.D:(i % 2 + 1) * c.D]
                    xo = xo2[:, (i % 2) * c.D:(i % 2 + 1) * c.D]
                    vector.wait_ge(fs["FW"], B["FW"] + 32)
                    vector.wait_ge(fs["Xa"] if i % 2 == 0 else fs["Xb"],
                                   par_ld("Xa" if i % 2 == 0 else "Xb", i))
                    for b in range(NB):
                        vector.bn_stats(stats[:, b * 6:(b + 1) * 6], xt[:, b * 512:(b + 1) * 512])
                    vector.drain()
                    vector.bn_aggr(mv[:], stats[:, 0:6 * NB])
                    vector.drain()
                    vector.tensor_scalar(veps[:], mv[:, 1:2], c.eps, None,
                                         op0=ALU.add).then_inc(fs["V"], 1)
                    vector.wait_ge(fs["S"], ss_sqrt1(i))
                    vector.reciprocal(rr[:], sd[:])
                    vector.drain()
                    if i >= 1:
                        vector.wait_ge(fs["T"], st(i - 1, c.DC))
                    vector.tensor_scalar(lnt[:], xt, mv[:, 0:1], rr[:],
                                         op0=ALU.subtract, op1=ALU.mult).then_inc(fs["V"], 1)
                    vector.wait_ge(fs["T"], st_route_all(i))
                    vector.reduce_max(mxv, pr, axis=AX)
                    vector.drain()
                    vector.tensor_scalar(rw, pr, mxv, mxv, op0=ALU.is_ge,
                                         op1=ALU.mult).then_inc(fs["V"], 1)
                    vector.wait_ge(fs["T"], st_h_all(i))
                    vector.tensor_copy(pbs[:], pb)
                    vector.drain()
                    vector.tensor_tensor(wh[:], ph, pbs[:], op=ALU.mult).then_inc(fs["V"], 1)
                    if i >= 2:
                        vector.wait_ge(fs["XWa"] if i % 2 == 0 else fs["XWb"],
                                       par_ld("XWa" if i % 2 == 0 else "XWb", i - 2))
                    for os in range(OS):
                        vector.wait_ge(fs["S"], ss_f(i, os))
                        ins = vector.tensor_tensor(xo[:, os * 512:(os + 1) * 512],
                                                   xt[:, os * 512:(os + 1) * 512],
                                                   fsb[:, os * 512:(os + 1) * 512],
                                                   op=ALU.add)
                        if os == OS - 1:
                            ins.then_inc(fs["V"], 1)
                    if second_ln:
                        vector.drain()
                        for b in range(NB):
                            vector.bn_stats(stats[:, b * 6:(b + 1) * 6], xo[:, b * 512:(b + 1) * 512])
                        vector.drain()
                        vector.bn_aggr(mv[:], stats[:, 0:6 * NB])
                        vector.drain()
                        vector.tensor_scalar(veps[:], mv[:, 1:2], c.eps, None,
                                             op0=ALU.add).then_inc(fs["V"], 1)
                        vector.wait_ge(fs["S"], ss_sqrt2(i))
                        vector.reciprocal(rr[:], sd[:])
                        vector.drain()
                        if i >= 1:
                            vector.wait_ge(fs["T"], st(i, 0))
                        vector.tensor_scalar(ln2t[:], xo, mv[:, 0:1], rr[:],
                                             op0=ALU.subtract, op1=ALU.mult).then_inc(fs["V"], 1)

            @block.scalar
            def _(scalar):
                for i in range(c.TT):
                    scalar.wait_ge(fs["V"], sv_veps(i))
                    scalar.activation(sd[:], veps[:], AF.Sqrt, bias=0.0,
                                      scale=1.0).then_inc(fs["S"], 1)
                    for kc in range(c.DC):
                        scalar.wait_ge(fs["T"], st_tp(i, kc))
                        pt = ptA if kc % 2 == 0 else ptB
                        scalar.copy(lnf[:, kc * 128:(kc + 1) * 128],
                                    pt[:, 0:128]).then_inc(fs["S"], 1)
                    scalar.wait_ge(fs["T"], st_rwT(i))
                    scalar.copy(rwT[:], prw).then_inc(fs["S"], 1)
                    for os in range(OS):
                        scalar.wait_ge(fs["T"], st_b(i, os))
                        scalar.mul(fsb[:, os * 512:(os + 1) * 512],
                                   pf[:, (os % 2) * 512:(os % 2 + 1) * 512],
                                   0.5).then_inc(fs["S"], 1)
                    if second_ln:
                        scalar.wait_ge(fs["V"], sv_veps2(i))
                        scalar.activation(sd[:], veps[:], AF.Sqrt, bias=0.0,
                                          scale=1.0).then_inc(fs["S"], 1)
                        if i >= 2:
                            scalar.wait_ge(fs["LWa"] if i % 2 == 0 else fs["LWb"],
                                           par_ld("LWa" if i % 2 == 0 else "LWb", i - 2))
                        ln2f = ln2f2[:, (i % 2) * c.D:(i % 2 + 1) * c.D]
                        for kc in range(c.DC):
                            scalar.wait_ge(fs["T"], st_ln2tp(i, kc))
                            pt = ptA if kc % 2 == 0 else ptB
                            scalar.copy(ln2f[:, kc * 128:(kc + 1) * 128],
                                        pt[:, 0:128]).then_inc(fs["S"], 1)

            @block.tensor
            def _(tensor):
                for i in range(c.TT):
                    tensor.wait_ge(fs["V"], sv_ln(i))
                    for kc in range(c.DC):
                        pt = ptA if kc % 2 == 0 else ptB
                        if kc >= 2:
                            tensor.wait_ge(fs["S"], ss_lnf(i, kc - 2))
                        elif i > 0:
                            tensor.wait_ge(fs["S"], B["S"] + i * SW)
                        tensor.transpose(pt[:, 0:128], lnt[:, kc * 128:(kc + 1) * 128],
                                         ident[:]).then_inc(fs["T"], 1)
                    for kc in range(c.DC):
                        tensor.wait_ge(fs["S"], ss_lnf(i, kc))
                        tensor.matmul(pr, lnf[:, kc * 128:(kc + 1) * 128],
                                      routeT[:, kc * c.E:(kc + 1) * c.E],
                                      start=(kc == 0), stop=(kc == c.DC - 1)).then_inc(fs["T"], 1)
                    tensor.wait_ge(fs["V"], sv_rw(i))
                    tensor.transpose(prw, rw, ident[:]).then_inc(fs["T"], 1)
                    tensor.wait_ge(fs["S"], ss_rwT(i))
                    tensor.matmul(pb, m8[:], rwT[:], start=True, stop=True).then_inc(fs["T"], 1)
                    for kc in range(c.DC):
                        tensor.matmul(ph, a2T[:, kc * c.ER:(kc + 1) * c.ER],
                                      lnf[:, kc * 128:(kc + 1) * 128],
                                      start=(kc == 0), stop=(kc == c.DC - 1)).then_inc(fs["T"], 1)
                    tensor.wait_ge(fs["V"], sv_wh(i))
                    for os in range(OS):
                        if os >= 2:
                            tensor.wait_ge(fs["S"], ss_f(i, os - 2))
                        elif i > 0:
                            tensor.wait_ge(fs["S"], ss_f(i - 1, OS - 1))
                        tensor.matmul(pf[:, (os % 2) * 512:(os % 2 + 1) * 512],
                                      wh[:], b2[:, os * 512:(os + 1) * 512],
                                      start=True, stop=True).then_inc(fs["T"], 1)
                    if second_ln:
                        tensor.wait_ge(fs["V"], sv_ln2(i))
                        for kc in range(c.DC):
                            pt = ptA if kc % 2 == 0 else ptB
                            if kc >= 2:
                                tensor.wait_ge(fs["S"], ss_ln2f(i, kc - 2))
                            else:
                                tensor.wait_ge(fs["S"], ss_f(i, OS - 1))
                            tensor.transpose(pt[:, 0:128], ln2t[:, kc * 128:(kc + 1) * 128],
                                             ident[:]).then_inc(fs["T"], 1)

            @block.gpsimd
            def _(gpsimd):
                for i in range(c.TT):
                    gpsimd.wait_ge(fs["V"], sv_xo(i))
                    sw = fs["XWa"] if i % 2 == 0 else fs["XWb"]
                    gpsimd.dma_start(out=dst_dram[i * 128:(i + 1) * 128, :],
                                     in_=xo2[:, (i % 2) * c.D:(i % 2 + 1) * c.D]).then_inc(sw, 16)
                    if second_ln:
                        gpsimd.wait_ge(fs["S"], B["S"] + (i + 1) * SW)
                        sl = fs["LWa"] if i % 2 == 0 else fs["LWb"]
                        dst = bass.AP(ln2_dst, i * 128,
                                      [[c.T, 128], [128 * c.T, c.DC], [1, 128]])
                        gpsimd.dma_start(out=dst,
                                         in_=ln2f2[:, (i % 2) * c.D:(i % 2 + 1) * c.D]).then_inc(sl, 16)
                na = (c.TT + 1) // 2
                nb = c.TT // 2
                gpsimd.wait_ge(fs["XWa"], B["XWa"] + 16 * na)
                gpsimd.wait_ge(fs["XWb"], B["XWb"] + 16 * nb)
                if second_ln:
                    gpsimd.wait_ge(fs["LWa"], B["LWa"] + 16 * na)
                    gpsimd.wait_ge(fs["LWb"], B["LWb"] + 16 * nb)

        na = (c.TT + 1) // 2
        nb = c.TT // 2
        return {"FW": B["FW"] + 32, "Xa": B["Xa"] + 16 * na, "Xb": B["Xb"] + 16 * nb,
                "S": B["S"] + c.TT * SW, "T": B["T"] + c.TT * TW, "V": B["V"] + c.TT * VW,
                "XWa": B["XWa"] + 16 * na, "XWb": B["XWb"] + 16 * nb,
                "LWa": B["LWa"] + (16 * na if second_ln else 0),
                "LWb": B["LWb"] + (16 * nb if second_ln else 0)}

    with stack:
        with nc.Block() as block:
            @block.sync
            def _(sync):
                sync.dma_start(out=ident[:], in_=ident_d[:]).then_inc(sLd, 16)
                sync.dma_start(out=m8[:], in_=m8_d[:]).then_inc(sLd, 16)
                sync.dma_start(out=dmask[:], in_=dmask_d[:]).then_inc(sLd, 16)
                src = bass.AP(routeT_d, 0, [[c.E, 128], [128 * c.E, c.DC], [1, c.E]])
                sync.dma_start(out=routeT[:], in_=src).then_inc(sLd, 16)
                sync.wait_ge(sLd, 64)

        B0 = {n: 0 for n in fs}
        if 1 in c.phases:
            B1 = ffn_phase("p1", x_in, x1_buf.ap(), True, ln2_buf, B0, [(sLd, 64)])
        else:
            B1 = B0

        if 2 in c.phases:
            NOC = c.QKO
            NTB = c.T // 512
            NVS = c.VO // 512
            sRope = new_sem("sRope")
            sLn2l = new_sem("sLn2l")
            sWla, sWlb = new_sem("sWla"), new_sem("sWlb")
            sWva, sWvb = new_sem("sWva"), new_sem("sWvb")
            sT2 = new_sem("sT2")
            sE2 = new_sem("sE2")
            sR2 = new_sem("sR2")
            sQwa, sQwb = new_sem("sQwa"), new_sem("sQwb")
            sVe = new_sem("sVe")
            sVwa, sVwb = new_sem("sVwa"), new_sem("sVwb")
            sT3 = new_sem("sT3")
            sCC = new_sem("sCC")
            with contextlib.ExitStack() as est:
                en = est.enter_context
                ln2f_all = en(nc.sbuf_tensor("p2ln2f", [128, c.DC * c.T], F32))
                wbuf = en(nc.sbuf_tensor("p2w", [128, 2 * 128], F32))
                wvbuf = en(nc.sbuf_tensor("p2wv", [128, 2 * 512], F32))
                qh = en(nc.sbuf_tensor("p2qh", [128, c.T], F32))
                qsw = en(nc.sbuf_tensor("p2qsw", [128, c.T], F32))
                t1b = en(nc.sbuf_tensor("p2t1", [128, c.T], F32))
                t2b = en(nc.sbuf_tensor("p2t2", [128, c.T], F32))
                rob = en(nc.sbuf_tensor("p2ro", [128, 2 * c.T], F32))
                cq = en(nc.sbuf_tensor("p2cq", [128, c.T], F32))
                sq = en(nc.sbuf_tensor("p2sq", [128, c.T], F32))
                ck = en(nc.sbuf_tensor("p2ck", [128, c.T], F32))
                sk = en(nc.sbuf_tensor("p2sk", [128, c.T], F32))
                vsb = en(nc.sbuf_tensor("p2v", [128, 2 * 512], F32))
                pbig = en(nc.psum_tensor("p2ps", [128, 8 * 512], F32))
                block = en(nc.Block())

                @block.sync
                def _(sync):
                    sync.dma_start(out=cq[:], in_=ropeq_c_d[:]).then_inc(sRope, 16)
                    sync.dma_start(out=sq[:], in_=ropeq_s_d[:]).then_inc(sRope, 16)
                    sync.dma_start(out=ck[:], in_=ropek_c_d[:]).then_inc(sRope, 16)
                    sync.dma_start(out=sk[:], in_=ropek_s_d[:]).then_inc(sRope, 16)
                    if 1 in c.phases:
                        sync.wait_ge(fs["LWa"], B1["LWa"])
                        sync.wait_ge(fs["LWb"], B1["LWb"])
                    q4 = c.DC // 4
                    for p in range(4):
                        src = bass.AP(ln2_buf, p * q4 * 128 * c.T,
                                      [[c.T, 128], [128 * c.T, q4], [1, c.T]])
                        sync.dma_start(out=ln2f_all[:, p * q4 * c.T:(p + 1) * q4 * c.T],
                                       in_=src).then_inc(sLn2l, 16)
                    for oc in range(NOC):
                        for kc in range(c.DC):
                            idx = oc * c.DC + kc
                            if idx >= 2:
                                sync.wait_ge(sT2, idx - 1)
                            sw = sWla if idx % 2 == 0 else sWlb
                            sync.dma_start(out=wbuf[:, (idx % 2) * 128:(idx % 2 + 1) * 128],
                                           in_=wqkvT[kc * 128:(kc + 1) * 128,
                                                     oc * 128:(oc + 1) * 128]).then_inc(sw, 16)
                    for os in range(NVS):
                        for kc in range(c.DC):
                            idx = os * c.DC + kc
                            if idx >= 2:
                                sync.wait_ge(sT3, idx - 1)
                            sw = sWva if idx % 2 == 0 else sWvb
                            co = c.QO + c.KO + os * 512
                            sync.dma_start(out=wvbuf[:, (idx % 2) * 512:(idx % 2 + 1) * 512],
                                           in_=wqkvT[kc * 128:(kc + 1) * 128,
                                                     co:co + 512]).then_inc(sw, 16)

                @block.tensor
                def _(tensor):
                    tensor.wait_ge(sLn2l, 64)
                    for oc in range(NOC):
                        for kc in range(c.DC):
                            idx = oc * c.DC + kc
                            tensor.wait_ge(sWla if idx % 2 == 0 else sWlb,
                                           16 * (idx // 2 + 1))
                            if kc == 0 and oc >= 2:
                                tensor.wait_ge(sE2, oc - 1)
                            w = wbuf[:, (idx % 2) * 128:(idx % 2 + 1) * 128]
                            for tb in range(NTB):
                                po = ((oc % 2) * NTB + tb) * 512
                                ins = tensor.matmul(pbig[:, po:po + 512], w,
                                                    ln2f_all[:, kc * c.T + tb * 512:kc * c.T + (tb + 1) * 512],
                                                    start=(kc == 0), stop=(kc == c.DC - 1))
                            ins.then_inc(sT2, 1)
                    for os in range(NVS):
                        for kc in range(c.DC):
                            idx = os * c.DC + kc
                            tensor.wait_ge(sWva if idx % 2 == 0 else sWvb,
                                           16 * (idx // 2 + 1))
                            if idx == 0:
                                tensor.wait_ge(sE2, NOC)
                            if kc == 0 and os >= 1:
                                tensor.wait_ge(sVe, os * c.TT)
                            w = wvbuf[:, (idx % 2) * 512:(idx % 2 + 1) * 512]
                            for tt in range(c.TT):
                                ins = tensor.matmul(pbig[:, tt * 512:(tt + 1) * 512],
                                                    ln2f_all[:, kc * c.T + tt * 128:kc * c.T + tt * 128 + 128],
                                                    w, start=(kc == 0), stop=(kc == c.DC - 1))
                            ins.then_inc(sT3, 1)

                @block.scalar
                def _(scalar):
                    for oc in range(NOC):
                        scalar.wait_ge(sT2, (oc + 1) * c.DC)
                        if oc >= 1:
                            scalar.wait_ge(sR2, oc)
                        for tb in range(NTB):
                            po = ((oc % 2) * NTB + tb) * 512
                            ins = scalar.copy(qh[:, tb * 512:(tb + 1) * 512], pbig[:, po:po + 512])
                        ins.then_inc(sE2, 1)
                    for os in range(NVS):
                        scalar.wait_ge(sT3, (os + 1) * c.DC)
                        for tt in range(c.TT):
                            idx = os * c.TT + tt
                            if idx >= 2:
                                scalar.wait_ge(sVwa if idx % 2 == 0 else sVwb,
                                               16 * ((idx - 2) // 2 + 1))
                            scalar.copy(vsb[:, (idx % 2) * 512:(idx % 2 + 1) * 512],
                                        pbig[:, tt * 512:(tt + 1) * 512]).then_inc(sVe, 1)

                @block.vector
                def _(vector):
                    H2 = c.HD // 2
                    vector.wait_ge(sRope, 64)
                    for oc in range(NOC):
                        cx, sx = (cq, sq) if oc < c.HQ else (ck, sk)
                        vector.wait_ge(sE2, oc + 1)
                        if oc >= 2:
                            vector.wait_ge(sQwa if oc % 2 == 0 else sQwb,
                                           16 * ((oc - 2) // 2 + 1))
                        ro = rob[:, (oc % 2) * c.T:(oc % 2 + 1) * c.T]
                        vector.tensor_copy(qsw[0:H2, :], qh[H2:2 * H2, :])
                        vector.tensor_tensor(t1b[:], qh[:], cx[:], op=ALU.mult)
                        vector.tensor_copy(qsw[H2:2 * H2, :], qh[0:H2, :])
                        vector.drain()
                        vector.tensor_tensor(t2b[0:H2, :], qsw[0:H2, :], sx[0:H2, :], op=ALU.mult)
                        vector.tensor_tensor(t2b[H2:2 * H2, :], qsw[H2:2 * H2, :], sx[H2:2 * H2, :], op=ALU.mult)
                        vector.drain()
                        vector.tensor_tensor(ro[0:H2, :], t1b[0:H2, :], t2b[0:H2, :], op=ALU.add)
                        vector.tensor_tensor(ro[H2:2 * H2, :], t1b[H2:2 * H2, :],
                                             t2b[H2:2 * H2, :], op=ALU.add).then_inc(sR2, 1)

                @block.gpsimd
                def _(gpsimd):
                    for oc in range(NOC):
                        gpsimd.wait_ge(sR2, oc + 1)
                        ro = rob[:, (oc % 2) * c.T:(oc % 2 + 1) * c.T]
                        sw = sQwa if oc % 2 == 0 else sQwb
                        if oc < c.HQ:
                            gpsimd.dma_start(out=q_buf.ap()[oc * 128:(oc + 1) * 128, :],
                                             in_=ro).then_inc(sw, 16)
                        else:
                            ko = oc - c.HQ
                            gpsimd.dma_start(out=agk_in.ap()[ko * 128:(ko + 1) * 128, :],
                                             in_=ro).then_inc(sw, 16)
                    gpsimd.wait_ge(sQwa, 16 * ((NOC + 1) // 2))
                    gpsimd.wait_ge(sQwb, 16 * (NOC // 2))
                    ng = c.n_cores // c.RG
                    rg = [[g * c.RG + r for r in range(c.RG)] for g in range(ng)]
                    gpsimd.collective_compute("AllGather", ALU.bypass, replica_groups=rg,
                                              ins=[agk_in.ap().opt()],
                                              outs=[agk_out.ap().opt()]).then_inc(sCC, 1)
                    for os in range(NVS):
                        for tt in range(c.TT):
                            idx = os * c.TT + tt
                            gpsimd.wait_ge(sVe, idx + 1)
                            sw = sVwa if idx % 2 == 0 else sVwb
                            gpsimd.dma_start(
                                out=agv_in.ap()[tt * 128:(tt + 1) * 128, os * 512:(os + 1) * 512],
                                in_=vsb[:, (idx % 2) * 512:(idx % 2 + 1) * 512]).then_inc(sw, 16)
                    nvt = NVS * c.TT
                    gpsimd.wait_ge(sVwa, 16 * ((nvt + 1) // 2))
                    gpsimd.wait_ge(sVwb, 16 * (nvt // 2))
                    gpsimd.collective_compute("AllGather", ALU.bypass, replica_groups=rg,
                                              ins=[agv_in.ap().opt()],
                                              outs=[agv_out.ap().opt()]).then_inc(sCC, 1)
                    gpsimd.wait_ge(sCC, 2)

        if 3 in c.phases:
            KW = c.RG * 128
            tiles = []
            kblocks = []
            for g in range(c.HKV):
                for qh_ in range(c.G4):
                    for u in range(c.TT):
                        j = len(tiles)
                        k0 = len(kblocks)
                        for t in range(u + 1):
                            kblocks.append((g, qh_, u, t, j))
                        tiles.append((g, qh_, u, k0, k0 + u))
            NT = len(tiles)
            NK = len(kblocks)
            sKVa, sKVb = new_sem("sKVa"), new_sem("sKVb")
            sQla, sQlb = new_sem("sQla"), new_sem("sQlb")
            sSb = new_sem("sSb")
            sSe = new_sem("sSe")
            sMsk = new_sem("sMsk")
            sNm = new_sem("sNm")
            sPx = new_sem("sPx")
            sLi = new_sem("sLi")
            sPT = new_sem("sPT")
            sPTe = new_sem("sPTe")
            sPV = new_sem("sPV")
            sVfin = new_sem("sVfin")
            sAtp = new_sem("sAtp")
            sAe = new_sem("sAe")
            sAwa, sAwb = new_sem("sAwa"), new_sem("sAwb")

            with contextlib.ExitStack() as est:
                en = est.enter_context
                Kg2 = en(nc.sbuf_tensor("p3K", [128, 2 * c.RG * c.T], F32))
                Vg2 = en(nc.sbuf_tensor("p3V", [128, 2 * c.RG * c.T], F32))
                qt2 = en(nc.sbuf_tensor("p3q", [128, 2 * 128], F32))
                Srow2 = en(nc.sbuf_tensor("p3S", [128, 3 * c.RG * c.T], F32))
                PTsb = en(nc.sbuf_tensor("p3PT", [128, 2 * KW], F32))
                out2 = en(nc.sbuf_tensor("p3out", [128, 2 * 128], F32))
                af2 = en(nc.sbuf_tensor("p3af", [128, 2 * 128], F32))
                mt2 = en(nc.sbuf_tensor("p3mt", [128, 3 * c.TT], F32))
                nm2 = en(nc.sbuf_tensor("p3nm", [128, 3], F32))
                lr2 = en(nc.sbuf_tensor("p3lr", [128, 3], F32))
                li2 = en(nc.sbuf_tensor("p3li", [128, 3], F32))
                pS = en(nc.psum_tensor("p3Sp", [128, 2 * KW], F32))
                pPT = en(nc.psum_tensor("p3PTp", [128, 2 * KW], F32))
                ppvb = en(nc.psum_tensor("p3pv", [128, 2 * 512], F32))
                ptpb = en(nc.psum_tensor("p3tp", [128, 512], F32))
                block = en(nc.Block())

                ptp = ptpb[:, 0:128]

                def ppv(par):
                    return ppvb[:, par * 512:par * 512 + 128]

                @block.sync
                def _(sync):
                    if 2 in c.phases:
                        sync.wait_ge(sCC, 2)
                    for g in range(c.HKV):
                        skv = sKVa if g % 2 == 0 else sKVb
                        if g >= 2:
                            sync.wait_ge(sSb, tiles[(g - 1) * c.G4 * c.TT - 1][4] + 1)
                            sync.wait_ge(sPV, tiles[(g - 1) * c.G4 * c.TT - 1][4] + 1)
                        srcK = bass.AP(agk_out, g * 128 * c.T,
                                       [[c.T, 128], [c.KO * c.T, c.RG], [1, c.T]])
                        sync.dma_start(out=Kg2[:, (g % 2) * c.RG * c.T:(g % 2 + 1) * c.RG * c.T],
                                       in_=srcK).then_inc(skv, 16)
                        srcV = bass.AP(agv_out, g * 128,
                                       [[c.VO, 128], [c.T * c.VO, c.RG],
                                        [128 * c.VO, c.TT], [1, 128]])
                        sync.dma_start(out=Vg2[:, (g % 2) * c.RG * c.T:(g % 2 + 1) * c.RG * c.T],
                                       in_=srcV).then_inc(skv, 16)
                        for qh_ in range(c.G4):
                            h = g * c.G4 + qh_
                            for u in range(c.TT):
                                j = (g * c.G4 + qh_) * c.TT + u
                                if j >= 2:
                                    sync.wait_ge(sSb, tiles[j - 2][4] + 1)
                                sq_ = sQla if j % 2 == 0 else sQlb
                                sync.dma_start(out=qt2[:, (j % 2) * 128:(j % 2 + 1) * 128],
                                               in_=q_buf.ap()[h * 128:(h + 1) * 128,
                                                              u * 128:(u + 1) * 128]).then_inc(sq_, 16)

                @block.tensor
                def _(tensor):
                    def emit_passB(jm):
                        g, qh_, u, k0, ke = tiles[jm]
                        js = jm % 3
                        jp = jm % 2
                        Prow = Srow2[:, js * c.RG * c.T:js * c.RG * c.T + (u + 1) * KW]
                        Vg = Vg2[:, (g % 2) * c.RG * c.T:(g % 2 + 1) * c.RG * c.T]
                        tensor.wait_ge(sPx, jm + 1)
                        if jm >= 2:
                            tensor.wait_ge(sVfin, jm - 1)

                        def tp(t):
                            k = k0 + t
                            par = k % 2
                            if k >= 2:
                                tensor.wait_ge(sPTe, k - 1)
                            for r in range(c.RG):
                                ins = tensor.transpose(
                                    pPT[:, par * KW + r * 128:par * KW + (r + 1) * 128],
                                    Prow[:, t * KW + r * 128:t * KW + (r + 1) * 128],
                                    ident[:])
                            ins.then_inc(sPT, 1)

                        def pv(t):
                            k = k0 + t
                            par = k % 2
                            tensor.wait_ge(sPTe, k + 1)
                            for r in range(c.RG):
                                ins = tensor.matmul(
                                    ppv(jp),
                                    PTsb[:, par * KW + r * 128:par * KW + (r + 1) * 128],
                                    Vg[:, (r * c.TT + t) * 128:(r * c.TT + t + 1) * 128],
                                    start=(t == 0 and r == 0), stop=(t == u and r == c.RG - 1))
                            ins.then_inc(sPV, 1)

                        tp(0)
                        for t in range(1, u + 1):
                            tp(t)
                            pv(t - 1)
                        pv(u)

                    def emit_ttp(jm):
                        tensor.wait_ge(sVfin, jm + 1)
                        if jm >= 1:
                            tensor.wait_ge(sAe, jm)
                        tensor.transpose(ptp, out2[:, (jm % 2) * 128:(jm % 2 + 1) * 128],
                                         ident[:]).then_inc(sAtp, 1)

                    for j, (g, qh_, u, k0, ke) in enumerate(tiles):
                        qt = qt2[:, (j % 2) * 128:(j % 2 + 1) * 128]
                        for t in range(u + 1):
                            k = k0 + t
                            par = k % 2
                            if k >= 2:
                                tensor.wait_ge(sSe, k - 1)
                            if t == 0:
                                tensor.wait_ge(sQla if j % 2 == 0 else sQlb,
                                               16 * (j // 2 + 1))
                                tensor.wait_ge(sKVa if g % 2 == 0 else sKVb,
                                               32 * (g // 2 + 1))
                            rhs = bass.AP(Kg2, (g % 2) * c.RG * c.T + t * 128,
                                          [[2 * c.RG * c.T, 128], [c.T, c.RG], [1, 128]])
                            tensor.matmul(pS[:, par * KW:(par + 1) * KW], qt, rhs,
                                          start=True, stop=True).then_inc(sSb, 1)
                        if j >= 1:
                            emit_passB(j - 1)
                            emit_ttp(j - 1)
                    emit_passB(NT - 1)
                    emit_ttp(NT - 1)

                @block.scalar
                def _(scalar):
                    def emit_sae(jm):
                        scalar.wait_ge(sAtp, jm + 1)
                        if jm >= 2:
                            scalar.wait_ge(sAwa if jm % 2 == 0 else sAwb,
                                           16 * ((jm - 2) // 2 + 1))
                        scalar.copy(af2[:, (jm % 2) * 128:(jm % 2 + 1) * 128],
                                    ptp).then_inc(sAe, 1)

                    for j, (g, qh_, u, k0, ke) in enumerate(tiles):
                        js = j % 3
                        Srow = Srow2[:, js * c.RG * c.T:js * c.RG * c.T + (u + 1) * KW]
                        for t in range(u + 1):
                            k = k0 + t
                            par = k % 2
                            scalar.wait_ge(sSb, k + 1)
                            if j >= 3 and t == 0:
                                scalar.wait_ge(sPT, tiles[j - 3][4] + 1)
                            scalar.copy(Srow[:, t * KW:(t + 1) * KW],
                                        pS[:, par * KW:(par + 1) * KW]).then_inc(sSe, 1)
                        if j >= 1:
                            jm = j - 1
                            gm, qm, um, km0, kme = tiles[jm]
                            for t in range(um + 1):
                                k = km0 + t
                                par = k % 2
                                scalar.wait_ge(sPT, k + 1)
                                if k >= 2:
                                    scalar.wait_ge(sPV, k - 1)
                                scalar.copy(PTsb[:, par * KW:(par + 1) * KW],
                                            pPT[:, par * KW:(par + 1) * KW]).then_inc(sPTe, 1)
                            emit_sae(jm)
                        scalar.wait_ge(sNm, j + 1)
                        scalar.activation(Srow[:], Srow[:], AF.Exp,
                                          bias=nm2[:, js:js + 1], scale=1.0,
                                          accum_out=lr2[:, js:js + 1]).then_inc(sPx, 1)
                    jm = NT - 1
                    gm, qm, um, km0, kme = tiles[jm]
                    for t in range(um + 1):
                        k = km0 + t
                        par = k % 2
                        scalar.wait_ge(sPT, k + 1)
                        if k >= 2:
                            scalar.wait_ge(sPV, k - 1)
                        scalar.copy(PTsb[:, par * KW:(par + 1) * KW],
                                    pPT[:, par * KW:(par + 1) * KW]).then_inc(sPTe, 1)
                    emit_sae(jm)

                @block.gpsimd
                def _(gpsimd):
                    for jm, (gg, qq, uu, _k0, _ke) in enumerate(tiles):
                        h = gg * c.G4 + qq
                        gpsimd.wait_ge(sAe, jm + 1)
                        sw = sAwa if jm % 2 == 0 else sAwb
                        gpsimd.dma_start(
                            out=attn_buf.ap()[h * 128:(h + 1) * 128, uu * 128:(uu + 1) * 128],
                            in_=af2[:, (jm % 2) * 128:(jm % 2 + 1) * 128]).then_inc(sw, 16)
                    gpsimd.wait_ge(sAwa, 16 * ((NT + 1) // 2))
                    gpsimd.wait_ge(sAwb, 16 * (NT // 2))

                @block.vector
                def _(vector):
                    for j, (g, qh_, u, k0, ke) in enumerate(tiles):
                        js = j % 3
                        jp = j % 2
                        Srow = Srow2[:, js * c.RG * c.T:js * c.RG * c.T + (u + 1) * KW]
                        mt = mt2[:, js * c.TT:js * c.TT + c.TT]
                        for t in range(u):
                            vector.wait_ge(sSe, k0 + t + 1)
                            vector.reduce_max(mt[:, t:t + 1], Srow[:, t * KW:(t + 1) * KW],
                                              axis=AX)
                        vector.wait_ge(sSe, ke + 1)
                        vector.tensor_tensor(Srow[:, u * KW:(u + 1) * KW],
                                             Srow[:, u * KW:(u + 1) * KW],
                                             dmask[:], op=ALU.add).then_inc(sMsk, 1)
                        vector.drain()
                        vector.reduce_max(mt[:, u:u + 1], Srow[:, u * KW:(u + 1) * KW],
                                          axis=AX)
                        vector.drain()
                        vector.reduce_max(nm2[:, js:js + 1], mt[:, 0:u + 1], axis=AX,
                                          negate=True).then_inc(sNm, 1)
                        vector.wait_ge(sPx, j + 1)
                        vector.reciprocal(li2[:, js:js + 1], lr2[:, js:js + 1])
                        vector.wait_ge(sPV, ke + 1)
                        if j >= 2:
                            vector.wait_ge(sAtp, j - 1)
                        vector.drain()
                        vector.tensor_scalar(out2[:, jp * 128:(jp + 1) * 128], ppv(jp),
                                             li2[:, js:js + 1], None,
                                             op0=ALU.mult).then_inc(sVfin, 1)

        if 4 in c.phases:
            OS4 = c.D // 512
            sWoa, sWob = new_sem("sWoa"), new_sem("sWob")
            sX1r = [new_sem("sX1r%d" % i) for i in range(4)]
            sT4 = new_sem("sT4")
            sXe = new_sem("sXe")
            sXna, sXnb = new_sem("sXna"), new_sem("sXnb")
            sAl = new_sem("sAl")
            with contextlib.ExitStack() as est:
                en = est.enter_context
                attn_f = en(nc.sbuf_tensor("p4attn", [128, c.HQ * c.T], F32))
                wob = en(nc.sbuf_tensor("p4w", [128, 2 * 512], F32))
                x1b = en(nc.sbuf_tensor("p4x1", [128, 4 * 512], F32))
                xnb = en(nc.sbuf_tensor("p4xn", [128, 2 * 512], F32))
                ps8 = en(nc.psum_tensor("p4ps", [128, 8 * 512], F32))
                block = en(nc.Block())

                @block.sync
                def _(sync):
                    if 3 in c.phases:
                        sync.wait_ge(sAwa, 16 * ((NT + 1) // 2))
                        sync.wait_ge(sAwb, 16 * (NT // 2))
                    q4 = c.HQ // 4
                    for p in range(4):
                        srcA = bass.AP(attn_buf, p * q4 * 128 * c.T,
                                       [[c.T, 128], [128 * c.T, q4], [1, c.T]])
                        sync.dma_start(out=attn_f[:, p * q4 * c.T:(p + 1) * q4 * c.T],
                                       in_=srcA).then_inc(sAl, 16)
                    for os in range(OS4):
                        for kc in range(c.HQ):
                            idx = os * c.HQ + kc
                            if idx >= 2:
                                sync.wait_ge(sT4, idx - 1)
                            sw = sWoa if idx % 2 == 0 else sWob
                            sync.dma_start(out=wob[:, (idx % 2) * 512:(idx % 2 + 1) * 512],
                                           in_=woT[kc * 128:(kc + 1) * 128,
                                                   os * 512:(os + 1) * 512]).then_inc(sw, 16)

                @block.gpsimd
                def _(gpsimd):
                    if 1 in c.phases:
                        gpsimd.wait_ge(fs["XWa"], B1["XWa"])
                        gpsimd.wait_ge(fs["XWb"], B1["XWb"])
                    tot = OS4 * c.TT

                    def x1_load(idx):
                        os_, tt_ = idx // c.TT, idx % c.TT
                        gpsimd.dma_start(out=x1b[:, (idx % 4) * 512:(idx % 4 + 1) * 512],
                                         in_=x1_buf.ap()[tt_ * 128:(tt_ + 1) * 128,
                                                         os_ * 512:(os_ + 1) * 512]).then_inc(sX1r[idx % 4], 16)

                    for idx in range(min(4, tot)):
                        x1_load(idx)
                    for idx in range(tot):
                        os_, tt_ = idx // c.TT, idx % c.TT
                        gpsimd.wait_ge(sXe, idx + 1)
                        if idx + 4 < tot:
                            x1_load(idx + 4)
                        sw = sXna if idx % 2 == 0 else sXnb
                        gpsimd.dma_start(out=xn_buf.ap()[tt_ * 128:(tt_ + 1) * 128,
                                                         os_ * 512:(os_ + 1) * 512],
                                         in_=xnb[:, (idx % 2) * 512:(idx % 2 + 1) * 512]).then_inc(sw, 16)
                    gpsimd.wait_ge(sXna, 16 * ((tot + 1) // 2))
                    gpsimd.wait_ge(sXnb, 16 * (tot // 2))

                @block.tensor
                def _(tensor):
                    tensor.wait_ge(sAl, 64)
                    for os in range(OS4):
                        for kc in range(c.HQ):
                            idx = os * c.HQ + kc
                            tensor.wait_ge(sWoa if idx % 2 == 0 else sWob,
                                           16 * (idx // 2 + 1))
                            if kc == 0 and os >= 1:
                                tensor.wait_ge(sXe, os * c.TT)
                            w = wob[:, (idx % 2) * 512:(idx % 2 + 1) * 512]
                            for tt in range(c.TT):
                                ins = tensor.matmul(ps8[:, tt * 512:(tt + 1) * 512],
                                                    attn_f[:, kc * c.T + tt * 128:kc * c.T + tt * 128 + 128],
                                                    w, start=(kc == 0), stop=(kc == c.HQ - 1))
                            ins.then_inc(sT4, 1)

                @block.vector
                def _(vector):
                    for os in range(OS4):
                        vector.wait_ge(sT4, (os + 1) * c.HQ)
                        for tt in range(c.TT):
                            idx = os * c.TT + tt
                            vector.wait_ge(sX1r[idx % 4], 16 * (idx // 4 + 1))
                            if idx >= 2:
                                vector.wait_ge(sXna if idx % 2 == 0 else sXnb,
                                               16 * ((idx - 2) // 2 + 1))
                            vector.tensor_tensor(xnb[:, (idx % 2) * 512:(idx % 2 + 1) * 512],
                                                 x1b[:, (idx % 4) * 512:(idx % 4 + 1) * 512],
                                                 ps8[:, tt * 512:(tt + 1) * 512],
                                                 op=ALU.add).then_inc(sXe, 1)

        if 5 in c.phases:
            entry = []
            if 4 in c.phases:
                tot = (c.D // 512) * c.TT
                entry = [(sXna, 16 * ((tot + 1) // 2)), (sXnb, 16 * (tot // 2))]
            ffn_phase("p5", xn_buf.ap(), y_out, False, None, B1, entry)

    return nc


def _host_prep(cfg):
    c = cfg
    ident = np.eye(128, dtype=np.float32)
    m8 = np.zeros((c.E, c.ER), dtype=np.float32)
    for e in range(c.E):
        m8[e, e * c.R:(e + 1) * c.R] = 1.0
    per_core = []
    for core in range(c.n_cores):
        g, l = core // c.RG, core % c.RG
        pos = (l + c.RG * np.arange(c.T)).astype(np.float64)
        inv = 1.0 / (c.theta ** (np.arange(0, c.HD, 2, dtype=np.float64) / c.HD))
        fr = np.outer(pos, inv)
        cos = np.cos(fr).T
        sin = np.sin(fr).T
        scale = 1.0 / math.sqrt(c.HD)
        ropeq_c = (np.concatenate([cos, cos], axis=0) * scale).astype(np.float32)
        ropeq_s = (np.concatenate([-sin, sin], axis=0) * scale).astype(np.float32)
        ropek_c = np.concatenate([cos, cos], axis=0).astype(np.float32)
        ropek_s = np.concatenate([-sin, sin], axis=0).astype(np.float32)
        i_ = np.arange(128)[:, None]
        j_ = np.arange(128)[None, :]
        dm = np.zeros((128, c.RG * 128), dtype=np.float32)
        for r in range(c.RG):
            d = 1 if r > l else 0
            dm[:, r * 128:(r + 1) * 128] = np.where(i_ >= j_ + d, 0.0, NEG).astype(np.float32)
        per_core.append({"ident": ident, "m8": m8,
                         "ropeq_c": np.ascontiguousarray(ropeq_c),
                         "ropeq_s": np.ascontiguousarray(ropeq_s),
                         "ropek_c": np.ascontiguousarray(ropek_c),
                         "ropek_s": np.ascontiguousarray(ropek_s),
                         "dmask": dm})
    return per_core


def make_in_maps(cfg, x, wqkv, wo, route, A, Bw):
    c = cfg
    wqkvT = np.ascontiguousarray(np.asarray(wqkv).T)
    woT = np.ascontiguousarray(np.asarray(wo).T)
    routeT = np.ascontiguousarray(np.asarray(route).T)
    a2T = np.ascontiguousarray(np.asarray(A).reshape(c.E * c.R, c.D).T)
    b2 = np.ascontiguousarray(np.transpose(np.asarray(Bw), (0, 2, 1)).reshape(c.E * c.R, c.D))
    consts = _host_prep(c)
    in_maps = []
    for core in range(c.n_cores):
        g, l = core // c.RG, core % c.RG
        x_own = np.ascontiguousarray(np.asarray(x)[g, l::c.RG, :], dtype=np.float32)
        m = dict(consts[core])
        m.update({"x_own": x_own, "wqkvT": wqkvT, "woT": woT, "routeT": routeT,
                  "a2T": a2T, "b2": b2})
        in_maps.append(m)
    return in_maps


def kernel(L0_attn_hidden_states, L0_attn_Wqkv, L0_attn_Wo,
           L0_ffn_route_w, L0_ffn_A, L0_ffn_Bw):
    cfg = Cfg()
    in_maps = make_in_maps(cfg, L0_attn_hidden_states, L0_attn_Wqkv, L0_attn_Wo,
                           L0_ffn_route_w, L0_ffn_A, L0_ffn_Bw)
    nc = build_nc(cfg)
    res = run_bass_kernel_spmd(nc, in_maps, core_ids=list(range(cfg.n_cores)))
    out = np.empty((cfg.n_cores // cfg.RG, cfg.S, cfg.D), dtype=np.float32)
    for core in range(cfg.n_cores):
        g, l = core // cfg.RG, core % cfg.RG
        out[g, l::cfg.RG, :] = res.results[core]["y"]
    return out


# version 08
# speedup vs baseline: 1.2196x; 1.1018x over previous
import math
import contextlib
import numpy as np
import concourse.bass as bass
import concourse.mybir as mybir
from concourse.bass_utils import run_bass_kernel_spmd

F32 = mybir.dt.float32
AF = mybir.ActivationFunctionType
ALU = mybir.AluOpType
AX = mybir.AxisListType.X

NEG = -1.0e30


class Cfg:
    def __init__(self, D=4096, S=4096, HQ=32, HKV=8, HD=128, E=8, R=16, RG=4,
                 eps=1e-5, theta=1e4, phases=(1, 2, 3, 4, 5), n_cores=8):
        self.D, self.S = D, S
        self.HQ, self.HKV, self.HD = HQ, HKV, HD
        self.E, self.R, self.RG = E, R, RG
        self.eps, self.theta = eps, theta
        self.phases = phases
        self.n_cores = n_cores
        self.T = S // RG
        self.TT = self.T // 128
        self.DC = D // 128
        self.ER = E * R
        self.G4 = HQ // HKV
        self.QKO = HQ + HKV
        self.QO = HQ * HD
        self.KO = HKV * HD
        self.VO = HKV * HD
        assert self.HD == 128 and self.ER == 128 and self.T % 128 == 0
        assert self.T % 512 == 0 and D % 512 == 0


def build_nc(cfg):
    c = cfg
    nc = bass.Bass("TRN2", target_bir_lowering=False)

    x_in = nc.dram_tensor("x_own", [c.T, c.D], F32, kind="ExternalInput").ap()
    wqkvT = nc.dram_tensor("wqkvT", [c.D, (c.HQ + 2 * c.HKV) * c.HD], F32, kind="ExternalInput").ap()
    woT = nc.dram_tensor("woT", [c.QO, c.D], F32, kind="ExternalInput").ap()
    routeT_d = nc.dram_tensor("routeT", [c.D, c.E], F32, kind="ExternalInput")
    a2T_d = nc.dram_tensor("a2T", [c.D, c.ER], F32, kind="ExternalInput")
    b2_d = nc.dram_tensor("b2", [c.ER, c.D], F32, kind="ExternalInput").ap()
    ident_d = nc.dram_tensor("ident", [128, 128], F32, kind="ExternalInput").ap()
    m8_d = nc.dram_tensor("m8", [c.E, c.ER], F32, kind="ExternalInput").ap()
    ropeq_c_d = nc.dram_tensor("ropeq_c", [c.HD, c.T], F32, kind="ExternalInput").ap()
    ropeq_s_d = nc.dram_tensor("ropeq_s", [c.HD, c.T], F32, kind="ExternalInput").ap()
    ropek_c_d = nc.dram_tensor("ropek_c", [c.HD, c.T], F32, kind="ExternalInput").ap()
    ropek_s_d = nc.dram_tensor("ropek_s", [c.HD, c.T], F32, kind="ExternalInput").ap()
    dmask_d = nc.dram_tensor("dmask", [128, c.RG * 128], F32, kind="ExternalInput").ap()
    y_out = nc.dram_tensor("y", [c.T, c.D], F32, kind="ExternalOutput").ap()

    x1_buf = nc.dram_tensor("x1_buf", [c.T, c.D], F32)
    ln2_buf = nc.dram_tensor("ln2_buf", [c.D, c.T], F32)
    q_buf = nc.dram_tensor("q_buf", [c.QO, c.T], F32)
    agk_in = nc.dram_tensor("agk_in", [c.KO, c.T], F32)
    agv_in = nc.dram_tensor("agv_in", [c.T, c.VO], F32)
    agk_out = nc.dram_tensor("agk_out", [c.RG * c.KO, c.T], F32)
    agv_out = nc.dram_tensor("agv_out", [c.RG * c.T, c.VO], F32)
    xn_buf = nc.dram_tensor("xn_buf", [c.T, c.D], F32)
    attn_buf = nc.dram_tensor("attn_buf", [c.QO, c.T], F32)

    stack = contextlib.ExitStack()

    def new_sem(name):
        return stack.enter_context(nc.semaphore(name))

    fs = {n: new_sem("f" + n) for n in
          ["FW", "Xa", "Xb", "S", "T", "V", "XWa", "XWb", "LWa", "LWb"]}
    sLd = new_sem("sLd")

    ident = stack.enter_context(nc.sbuf_tensor("c_ident", [128, 128], F32))
    m8 = stack.enter_context(nc.sbuf_tensor("c_m8", [c.E, c.ER], F32))
    dmask = stack.enter_context(nc.sbuf_tensor("c_dmask", [128, c.RG * 128], F32))
    routeT = stack.enter_context(nc.sbuf_tensor("c_routeT", [128, c.DC * c.E], F32))

    def ffn_phase(tag, src_dram, dst_dram, second_ln, ln2_dst, B, entry_waits):
        NB = c.D // 512
        OS = c.D // 512
        SW = 1 + c.DC + 1 + OS + ((1 + c.DC) if second_ln else 0)
        def ss_sqrt1(i): return B["S"] + i * SW + 1
        def ss_lnf(i, kc): return B["S"] + i * SW + 1 + kc + 1
        def ss_rwT(i): return B["S"] + i * SW + 1 + c.DC + 1
        def ss_f(i, os): return B["S"] + i * SW + 1 + c.DC + 1 + os + 1
        def ss_sqrt2(i): return B["S"] + i * SW + 1 + c.DC + 1 + OS + 1
        def ss_ln2f(i, kc): return B["S"] + i * SW + 1 + c.DC + 1 + OS + 1 + kc + 1
        TW = 3 * c.DC + 2 + OS + (c.DC if second_ln else 0)
        def st(i, k): return B["T"] + i * TW + k
        def st_tp(i, kc): return st(i, kc + 1)
        def st_route_all(i): return st(i, 2 * c.DC)
        def st_rwT(i): return st(i, 2 * c.DC + 1)
        def st_h_all(i): return st(i, 3 * c.DC + 2)
        def st_b(i, os): return st(i, 3 * c.DC + 2 + os + 1)
        def st_ln2tp(i, kc): return st(i, 3 * c.DC + 2 + OS + kc + 1)
        VW = 5 + (2 if second_ln else 0)
        def sv_veps(i): return B["V"] + i * VW + 1
        def sv_ln(i): return B["V"] + i * VW + 2
        def sv_rw(i): return B["V"] + i * VW + 3
        def sv_wh(i): return B["V"] + i * VW + 4
        def sv_xo(i): return B["V"] + i * VW + 5
        def sv_veps2(i): return B["V"] + i * VW + 6
        def sv_ln2(i): return B["V"] + i * VW + 7
        def par_ld(nm, i):
            return B[nm] + 16 * (i // 2 + 1)

        with contextlib.ExitStack() as est:
            en = est.enter_context
            a2T = en(nc.sbuf_tensor(tag + "a2T", [128, c.DC * c.ER], F32))
            b2 = en(nc.sbuf_tensor(tag + "b2", [c.ER, c.D], F32))
            xt2 = en(nc.sbuf_tensor(tag + "xt", [128, 2 * c.D], F32))
            stats = en(nc.sbuf_tensor(tag + "stats", [128, 6 * NB], F32))
            mv = en(nc.sbuf_tensor(tag + "mv", [128, 2], F32))
            veps = en(nc.sbuf_tensor(tag + "veps", [128, 1], F32))
            sd = en(nc.sbuf_tensor(tag + "sd", [128, 1], F32))
            rr = en(nc.sbuf_tensor(tag + "rr", [128, 1], F32))
            lnt = en(nc.sbuf_tensor(tag + "lnt", [128, c.D], F32))
            lnf = en(nc.sbuf_tensor(tag + "lnf", [128, c.D], F32))
            rwb = en(nc.sbuf_tensor(tag + "rw", [128, c.E + 1], F32))
            rwT = en(nc.sbuf_tensor(tag + "rwT", [c.E, 128], F32))
            wh = en(nc.sbuf_tensor(tag + "wh", [c.ER, 128], F32))
            pbs = en(nc.sbuf_tensor(tag + "pbs", [c.ER, 128], F32))
            fsb = en(nc.sbuf_tensor(tag + "fsb", [128, (c.D // 512) * 512], F32))
            xo2 = en(nc.sbuf_tensor(tag + "xo", [128, 2 * c.D], F32))
            ln2t = en(nc.sbuf_tensor(tag + "ln2t", [128, c.D], F32))
            ln2f2 = en(nc.sbuf_tensor(tag + "ln2f", [128, 2 * c.D], F32)) if second_ln else None
            ptA = en(nc.psum_tensor(tag + "ptA", [128, 512], F32))
            ptB = en(nc.psum_tensor(tag + "ptB", [128, 512], F32))
            prb = en(nc.psum_tensor(tag + "pr", [128, 512], F32))
            prwb = en(nc.psum_tensor(tag + "prw", [128, 512], F32))
            phb = en(nc.psum_tensor(tag + "ph", [128, 512], F32))
            pbb = en(nc.psum_tensor(tag + "pb", [128, 512], F32))
            pf = en(nc.psum_tensor(tag + "pf", [128, 2 * 512], F32))
            block = en(nc.Block())

            pr = prb[:, 0:c.E]
            prw = prwb[0:c.E, 0:128]
            ph = phb[:, 0:128]
            pb = pbb[:, 0:128]
            mxv = rwb[:, c.E:c.E + 1]
            rw = rwb[:, 0:c.E]

            @block.sync
            def _(sync):
                for s, v in entry_waits:
                    sync.wait_ge(s, v)
                srcA = bass.AP(a2T_d, 0, [[c.ER, 128], [128 * c.ER, c.DC], [1, c.ER]])
                sync.dma_start(out=a2T[:], in_=srcA).then_inc(fs["FW"], 16)
                sync.dma_start(out=b2[:], in_=b2_d[:]).then_inc(fs["FW"], 16)
                for i in range(c.TT):
                    if i >= 2:
                        sync.wait_ge(fs["V"], sv_xo(i - 2))
                    sx = fs["Xa"] if i % 2 == 0 else fs["Xb"]
                    sync.dma_start(out=xt2[:, (i % 2) * c.D:(i % 2 + 1) * c.D],
                                   in_=src_dram[i * 128:(i + 1) * 128, :]).then_inc(sx, 16)

            @block.vector
            def _(vector):
                for i in range(c.TT):
                    xt = xt2[:, (i % 2) * c.D:(i % 2 + 1) * c.D]
                    xo = xo2[:, (i % 2) * c.D:(i % 2 + 1) * c.D]
                    vector.wait_ge(fs["FW"], B["FW"] + 32)
                    vector.wait_ge(fs["Xa"] if i % 2 == 0 else fs["Xb"],
                                   par_ld("Xa" if i % 2 == 0 else "Xb", i))
                    for b in range(NB):
                        vector.bn_stats(stats[:, b * 6:(b + 1) * 6], xt[:, b * 512:(b + 1) * 512])
                    vector.drain()
                    vector.bn_aggr(mv[:], stats[:, 0:6 * NB])
                    vector.drain()
                    vector.tensor_scalar(veps[:], mv[:, 1:2], c.eps, None,
                                         op0=ALU.add).then_inc(fs["V"], 1)
                    vector.wait_ge(fs["S"], ss_sqrt1(i))
                    vector.reciprocal(rr[:], sd[:])
                    vector.drain()
                    if i >= 1:
                        vector.wait_ge(fs["T"], st(i - 1, c.DC))
                    vector.tensor_scalar(lnt[:], xt, mv[:, 0:1], rr[:],
                                         op0=ALU.subtract, op1=ALU.mult).then_inc(fs["V"], 1)
                    vector.wait_ge(fs["T"], st_route_all(i))
                    vector.reduce_max(mxv, pr, axis=AX)
                    vector.drain()
                    vector.tensor_scalar(rw, pr, mxv, mxv, op0=ALU.is_ge,
                                         op1=ALU.mult).then_inc(fs["V"], 1)
                    vector.wait_ge(fs["T"], st_h_all(i))
                    vector.tensor_copy(pbs[:], pb)
                    vector.drain()
                    vector.tensor_tensor(wh[:], ph, pbs[:], op=ALU.mult).then_inc(fs["V"], 1)
                    if i >= 2:
                        vector.wait_ge(fs["XWa"] if i % 2 == 0 else fs["XWb"],
                                       par_ld("XWa" if i % 2 == 0 else "XWb", i - 2))
                    for os in range(OS):
                        vector.wait_ge(fs["S"], ss_f(i, os))
                        ins = vector.tensor_tensor(xo[:, os * 512:(os + 1) * 512],
                                                   xt[:, os * 512:(os + 1) * 512],
                                                   fsb[:, os * 512:(os + 1) * 512],
                                                   op=ALU.add)
                        if os == OS - 1:
                            ins.then_inc(fs["V"], 1)
                    if second_ln:
                        vector.drain()
                        for b in range(NB):
                            vector.bn_stats(stats[:, b * 6:(b + 1) * 6], xo[:, b * 512:(b + 1) * 512])
                        vector.drain()
                        vector.bn_aggr(mv[:], stats[:, 0:6 * NB])
                        vector.drain()
                        vector.tensor_scalar(veps[:], mv[:, 1:2], c.eps, None,
                                             op0=ALU.add).then_inc(fs["V"], 1)
                        vector.wait_ge(fs["S"], ss_sqrt2(i))
                        vector.reciprocal(rr[:], sd[:])
                        vector.drain()
                        if i >= 1:
                            vector.wait_ge(fs["T"], st(i, 0))
                        vector.tensor_scalar(ln2t[:], xo, mv[:, 0:1], rr[:],
                                             op0=ALU.subtract, op1=ALU.mult).then_inc(fs["V"], 1)

            @block.scalar
            def _(scalar):
                for i in range(c.TT):
                    scalar.wait_ge(fs["V"], sv_veps(i))
                    scalar.activation(sd[:], veps[:], AF.Sqrt, bias=0.0,
                                      scale=1.0).then_inc(fs["S"], 1)
                    for kc in range(c.DC):
                        scalar.wait_ge(fs["T"], st_tp(i, kc))
                        pt = ptA if kc % 2 == 0 else ptB
                        scalar.copy(lnf[:, kc * 128:(kc + 1) * 128],
                                    pt[:, 0:128]).then_inc(fs["S"], 1)
                    scalar.wait_ge(fs["T"], st_rwT(i))
                    scalar.copy(rwT[:], prw).then_inc(fs["S"], 1)
                    for os in range(OS):
                        scalar.wait_ge(fs["T"], st_b(i, os))
                        scalar.mul(fsb[:, os * 512:(os + 1) * 512],
                                   pf[:, (os % 2) * 512:(os % 2 + 1) * 512],
                                   0.5).then_inc(fs["S"], 1)
                    if second_ln:
                        scalar.wait_ge(fs["V"], sv_veps2(i))
                        scalar.activation(sd[:], veps[:], AF.Sqrt, bias=0.0,
                                          scale=1.0).then_inc(fs["S"], 1)
                        if i >= 2:
                            scalar.wait_ge(fs["LWa"] if i % 2 == 0 else fs["LWb"],
                                           par_ld("LWa" if i % 2 == 0 else "LWb", i - 2))
                        ln2f = ln2f2[:, (i % 2) * c.D:(i % 2 + 1) * c.D]
                        for kc in range(c.DC):
                            scalar.wait_ge(fs["T"], st_ln2tp(i, kc))
                            pt = ptA if kc % 2 == 0 else ptB
                            scalar.copy(ln2f[:, kc * 128:(kc + 1) * 128],
                                        pt[:, 0:128]).then_inc(fs["S"], 1)

            @block.tensor
            def _(tensor):
                for i in range(c.TT):
                    tensor.wait_ge(fs["V"], sv_ln(i))
                    for kc in range(c.DC):
                        pt = ptA if kc % 2 == 0 else ptB
                        if kc >= 2:
                            tensor.wait_ge(fs["S"], ss_lnf(i, kc - 2))
                        elif i > 0:
                            tensor.wait_ge(fs["S"], B["S"] + i * SW)
                        tensor.transpose(pt[:, 0:128], lnt[:, kc * 128:(kc + 1) * 128],
                                         ident[:]).then_inc(fs["T"], 1)
                    for kc in range(c.DC):
                        tensor.wait_ge(fs["S"], ss_lnf(i, kc))
                        tensor.matmul(pr, lnf[:, kc * 128:(kc + 1) * 128],
                                      routeT[:, kc * c.E:(kc + 1) * c.E],
                                      start=(kc == 0), stop=(kc == c.DC - 1)).then_inc(fs["T"], 1)
                    tensor.wait_ge(fs["V"], sv_rw(i))
                    tensor.transpose(prw, rw, ident[:]).then_inc(fs["T"], 1)
                    tensor.wait_ge(fs["S"], ss_rwT(i))
                    tensor.matmul(pb, m8[:], rwT[:], start=True, stop=True).then_inc(fs["T"], 1)
                    for kc in range(c.DC):
                        tensor.matmul(ph, a2T[:, kc * c.ER:(kc + 1) * c.ER],
                                      lnf[:, kc * 128:(kc + 1) * 128],
                                      start=(kc == 0), stop=(kc == c.DC - 1)).then_inc(fs["T"], 1)
                    tensor.wait_ge(fs["V"], sv_wh(i))
                    for os in range(OS):
                        if os >= 2:
                            tensor.wait_ge(fs["S"], ss_f(i, os - 2))
                        elif i > 0:
                            tensor.wait_ge(fs["S"], ss_f(i - 1, OS - 1))
                        tensor.matmul(pf[:, (os % 2) * 512:(os % 2 + 1) * 512],
                                      wh[:], b2[:, os * 512:(os + 1) * 512],
                                      start=True, stop=True).then_inc(fs["T"], 1)
                    if second_ln:
                        tensor.wait_ge(fs["V"], sv_ln2(i))
                        for kc in range(c.DC):
                            pt = ptA if kc % 2 == 0 else ptB
                            if kc >= 2:
                                tensor.wait_ge(fs["S"], ss_ln2f(i, kc - 2))
                            else:
                                tensor.wait_ge(fs["S"], ss_f(i, OS - 1))
                            tensor.transpose(pt[:, 0:128], ln2t[:, kc * 128:(kc + 1) * 128],
                                             ident[:]).then_inc(fs["T"], 1)

            @block.gpsimd
            def _(gpsimd):
                for i in range(c.TT):
                    gpsimd.wait_ge(fs["V"], sv_xo(i))
                    sw = fs["XWa"] if i % 2 == 0 else fs["XWb"]
                    gpsimd.dma_start(out=dst_dram[i * 128:(i + 1) * 128, :],
                                     in_=xo2[:, (i % 2) * c.D:(i % 2 + 1) * c.D]).then_inc(sw, 16)
                    if second_ln:
                        gpsimd.wait_ge(fs["S"], B["S"] + (i + 1) * SW)
                        sl = fs["LWa"] if i % 2 == 0 else fs["LWb"]
                        dst = bass.AP(ln2_dst, i * 128,
                                      [[c.T, 128], [128 * c.T, c.DC], [1, 128]])
                        gpsimd.dma_start(out=dst,
                                         in_=ln2f2[:, (i % 2) * c.D:(i % 2 + 1) * c.D]).then_inc(sl, 16)
                na = (c.TT + 1) // 2
                nb = c.TT // 2
                gpsimd.wait_ge(fs["XWa"], B["XWa"] + 16 * na)
                gpsimd.wait_ge(fs["XWb"], B["XWb"] + 16 * nb)
                if second_ln:
                    gpsimd.wait_ge(fs["LWa"], B["LWa"] + 16 * na)
                    gpsimd.wait_ge(fs["LWb"], B["LWb"] + 16 * nb)

        na = (c.TT + 1) // 2
        nb = c.TT // 2
        return {"FW": B["FW"] + 32, "Xa": B["Xa"] + 16 * na, "Xb": B["Xb"] + 16 * nb,
                "S": B["S"] + c.TT * SW, "T": B["T"] + c.TT * TW, "V": B["V"] + c.TT * VW,
                "XWa": B["XWa"] + 16 * na, "XWb": B["XWb"] + 16 * nb,
                "LWa": B["LWa"] + (16 * na if second_ln else 0),
                "LWb": B["LWb"] + (16 * nb if second_ln else 0)}

    with stack:
        with nc.Block() as block:
            @block.sync
            def _(sync):
                sync.dma_start(out=ident[:], in_=ident_d[:]).then_inc(sLd, 16)
                sync.dma_start(out=m8[:], in_=m8_d[:]).then_inc(sLd, 16)
                sync.dma_start(out=dmask[:], in_=dmask_d[:]).then_inc(sLd, 16)
                src = bass.AP(routeT_d, 0, [[c.E, 128], [128 * c.E, c.DC], [1, c.E]])
                sync.dma_start(out=routeT[:], in_=src).then_inc(sLd, 16)
                sync.wait_ge(sLd, 64)

        B0 = {n: 0 for n in fs}
        if 1 in c.phases:
            B1 = ffn_phase("p1", x_in, x1_buf.ap(), True, ln2_buf, B0, [(sLd, 64)])
        else:
            B1 = B0

        if 2 in c.phases:
            NOC = c.QKO
            NTB = c.T // 512
            NVS = c.VO // 512
            sRope = new_sem("sRope")
            sLn2l = new_sem("sLn2l")
            sWla, sWlb = new_sem("sWla"), new_sem("sWlb")
            sWva, sWvb = new_sem("sWva"), new_sem("sWvb")
            sT2 = new_sem("sT2")
            sE2 = new_sem("sE2")
            sR2 = new_sem("sR2")
            sQwa, sQwb = new_sem("sQwa"), new_sem("sQwb")
            sVe = new_sem("sVe")
            sVwa, sVwb = new_sem("sVwa"), new_sem("sVwb")
            sT3 = new_sem("sT3")
            sCC = new_sem("sCC")
            with contextlib.ExitStack() as est:
                en = est.enter_context
                ln2f_all = en(nc.sbuf_tensor("p2ln2f", [128, c.DC * c.T], F32))
                wbuf = en(nc.sbuf_tensor("p2w", [128, 2 * 128], F32))
                wvbuf = en(nc.sbuf_tensor("p2wv", [128, 2 * 512], F32))
                qh = en(nc.sbuf_tensor("p2qh", [128, c.T], F32))
                qsw = en(nc.sbuf_tensor("p2qsw", [128, c.T], F32))
                t1b = en(nc.sbuf_tensor("p2t1", [128, c.T], F32))
                t2b = en(nc.sbuf_tensor("p2t2", [128, c.T], F32))
                rob = en(nc.sbuf_tensor("p2ro", [128, 2 * c.T], F32))
                cq = en(nc.sbuf_tensor("p2cq", [128, c.T], F32))
                sq = en(nc.sbuf_tensor("p2sq", [128, c.T], F32))
                ck = en(nc.sbuf_tensor("p2ck", [128, c.T], F32))
                sk = en(nc.sbuf_tensor("p2sk", [128, c.T], F32))
                vsb = en(nc.sbuf_tensor("p2v", [128, 2 * 512], F32))
                pbig = en(nc.psum_tensor("p2ps", [128, 8 * 512], F32))
                block = en(nc.Block())

                @block.sync
                def _(sync):
                    sync.dma_start(out=cq[:], in_=ropeq_c_d[:]).then_inc(sRope, 16)
                    sync.dma_start(out=sq[:], in_=ropeq_s_d[:]).then_inc(sRope, 16)
                    sync.dma_start(out=ck[:], in_=ropek_c_d[:]).then_inc(sRope, 16)
                    sync.dma_start(out=sk[:], in_=ropek_s_d[:]).then_inc(sRope, 16)
                    if 1 in c.phases:
                        sync.wait_ge(fs["LWa"], B1["LWa"])
                        sync.wait_ge(fs["LWb"], B1["LWb"])
                    q4 = c.DC // 4
                    for p in range(4):
                        src = bass.AP(ln2_buf, p * q4 * 128 * c.T,
                                      [[c.T, 128], [128 * c.T, q4], [1, c.T]])
                        sync.dma_start(out=ln2f_all[:, p * q4 * c.T:(p + 1) * q4 * c.T],
                                       in_=src).then_inc(sLn2l, 16)
                    for oc in range(NOC):
                        for kc in range(c.DC):
                            idx = oc * c.DC + kc
                            if idx >= 2:
                                sync.wait_ge(sT2, idx - 1)
                            sw = sWla if idx % 2 == 0 else sWlb
                            sync.dma_start(out=wbuf[:, (idx % 2) * 128:(idx % 2 + 1) * 128],
                                           in_=wqkvT[kc * 128:(kc + 1) * 128,
                                                     oc * 128:(oc + 1) * 128]).then_inc(sw, 16)
                    for os in range(NVS):
                        for kc in range(c.DC):
                            idx = os * c.DC + kc
                            if idx >= 2:
                                sync.wait_ge(sT3, idx - 1)
                            sw = sWva if idx % 2 == 0 else sWvb
                            co = c.QO + c.KO + os * 512
                            sync.dma_start(out=wvbuf[:, (idx % 2) * 512:(idx % 2 + 1) * 512],
                                           in_=wqkvT[kc * 128:(kc + 1) * 128,
                                                     co:co + 512]).then_inc(sw, 16)

                @block.tensor
                def _(tensor):
                    tensor.wait_ge(sLn2l, 64)
                    for oc in range(NOC):
                        for kc in range(c.DC):
                            idx = oc * c.DC + kc
                            tensor.wait_ge(sWla if idx % 2 == 0 else sWlb,
                                           16 * (idx // 2 + 1))
                            if kc == 0 and oc >= 2:
                                tensor.wait_ge(sE2, oc - 1)
                            w = wbuf[:, (idx % 2) * 128:(idx % 2 + 1) * 128]
                            for tb in range(NTB):
                                po = ((oc % 2) * NTB + tb) * 512
                                ins = tensor.matmul(pbig[:, po:po + 512], w,
                                                    ln2f_all[:, kc * c.T + tb * 512:kc * c.T + (tb + 1) * 512],
                                                    start=(kc == 0), stop=(kc == c.DC - 1))
                            ins.then_inc(sT2, 1)
                    for os in range(NVS):
                        for kc in range(c.DC):
                            idx = os * c.DC + kc
                            tensor.wait_ge(sWva if idx % 2 == 0 else sWvb,
                                           16 * (idx // 2 + 1))
                            if idx == 0:
                                tensor.wait_ge(sE2, NOC)
                            if kc == 0 and os >= 1:
                                tensor.wait_ge(sVe, os * c.TT)
                            w = wvbuf[:, (idx % 2) * 512:(idx % 2 + 1) * 512]
                            for tt in range(c.TT):
                                ins = tensor.matmul(pbig[:, tt * 512:(tt + 1) * 512],
                                                    ln2f_all[:, kc * c.T + tt * 128:kc * c.T + tt * 128 + 128],
                                                    w, start=(kc == 0), stop=(kc == c.DC - 1))
                            ins.then_inc(sT3, 1)

                @block.scalar
                def _(scalar):
                    for oc in range(NOC):
                        scalar.wait_ge(sT2, (oc + 1) * c.DC)
                        if oc >= 1:
                            scalar.wait_ge(sR2, oc)
                        for tb in range(NTB):
                            po = ((oc % 2) * NTB + tb) * 512
                            ins = scalar.copy(qh[:, tb * 512:(tb + 1) * 512], pbig[:, po:po + 512])
                        ins.then_inc(sE2, 1)
                    for os in range(NVS):
                        scalar.wait_ge(sT3, (os + 1) * c.DC)
                        for tt in range(c.TT):
                            idx = os * c.TT + tt
                            if idx >= 2:
                                scalar.wait_ge(sVwa if idx % 2 == 0 else sVwb,
                                               16 * ((idx - 2) // 2 + 1))
                            scalar.copy(vsb[:, (idx % 2) * 512:(idx % 2 + 1) * 512],
                                        pbig[:, tt * 512:(tt + 1) * 512]).then_inc(sVe, 1)

                @block.vector
                def _(vector):
                    H2 = c.HD // 2
                    vector.wait_ge(sRope, 64)
                    for oc in range(NOC):
                        cx, sx = (cq, sq) if oc < c.HQ else (ck, sk)
                        vector.wait_ge(sE2, oc + 1)
                        if oc >= 2:
                            vector.wait_ge(sQwa if oc % 2 == 0 else sQwb,
                                           16 * ((oc - 2) // 2 + 1))
                        ro = rob[:, (oc % 2) * c.T:(oc % 2 + 1) * c.T]
                        vector.tensor_copy(qsw[0:H2, :], qh[H2:2 * H2, :])
                        vector.tensor_tensor(t1b[:], qh[:], cx[:], op=ALU.mult)
                        vector.tensor_copy(qsw[H2:2 * H2, :], qh[0:H2, :])
                        vector.drain()
                        vector.tensor_tensor(t2b[0:H2, :], qsw[0:H2, :], sx[0:H2, :], op=ALU.mult)
                        vector.tensor_tensor(t2b[H2:2 * H2, :], qsw[H2:2 * H2, :], sx[H2:2 * H2, :], op=ALU.mult)
                        vector.drain()
                        vector.tensor_tensor(ro[0:H2, :], t1b[0:H2, :], t2b[0:H2, :], op=ALU.add)
                        vector.tensor_tensor(ro[H2:2 * H2, :], t1b[H2:2 * H2, :],
                                             t2b[H2:2 * H2, :], op=ALU.add).then_inc(sR2, 1)

                @block.gpsimd
                def _(gpsimd):
                    for oc in range(NOC):
                        gpsimd.wait_ge(sR2, oc + 1)
                        ro = rob[:, (oc % 2) * c.T:(oc % 2 + 1) * c.T]
                        sw = sQwa if oc % 2 == 0 else sQwb
                        if oc < c.HQ:
                            gpsimd.dma_start(out=q_buf.ap()[oc * 128:(oc + 1) * 128, :],
                                             in_=ro).then_inc(sw, 16)
                        else:
                            ko = oc - c.HQ
                            gpsimd.dma_start(out=agk_in.ap()[ko * 128:(ko + 1) * 128, :],
                                             in_=ro).then_inc(sw, 16)
                    gpsimd.wait_ge(sQwa, 16 * ((NOC + 1) // 2))
                    gpsimd.wait_ge(sQwb, 16 * (NOC // 2))
                    ng = c.n_cores // c.RG
                    rg = [[g * c.RG + r for r in range(c.RG)] for g in range(ng)]
                    gpsimd.collective_compute("AllGather", ALU.bypass, replica_groups=rg,
                                              ins=[agk_in.ap().opt()],
                                              outs=[agk_out.ap().opt()]).then_inc(sCC, 1)
                    for os in range(NVS):
                        for tt in range(c.TT):
                            idx = os * c.TT + tt
                            gpsimd.wait_ge(sVe, idx + 1)
                            sw = sVwa if idx % 2 == 0 else sVwb
                            gpsimd.dma_start(
                                out=agv_in.ap()[tt * 128:(tt + 1) * 128, os * 512:(os + 1) * 512],
                                in_=vsb[:, (idx % 2) * 512:(idx % 2 + 1) * 512]).then_inc(sw, 16)
                    nvt = NVS * c.TT
                    gpsimd.wait_ge(sVwa, 16 * ((nvt + 1) // 2))
                    gpsimd.wait_ge(sVwb, 16 * (nvt // 2))
                    gpsimd.collective_compute("AllGather", ALU.bypass, replica_groups=rg,
                                              ins=[agv_in.ap().opt()],
                                              outs=[agv_out.ap().opt()]).then_inc(sCC, 1)
                    gpsimd.wait_ge(sCC, 2)

        if 3 in c.phases:
            KW = c.RG * 128
            tiles = []
            kblocks = []
            for g in range(c.HKV):
                for qh_ in range(c.G4):
                    for u in range(c.TT):
                        j = len(tiles)
                        k0 = len(kblocks)
                        for t in range(u + 1):
                            kblocks.append((g, qh_, u, t, j))
                        tiles.append((g, qh_, u, k0, k0 + u))
            NT = len(tiles)
            NK = len(kblocks)
            sKVa, sKVb = new_sem("sKVa"), new_sem("sKVb")
            sVla, sVlb = new_sem("sVla"), new_sem("sVlb")
            sQla, sQlb = new_sem("sQla"), new_sem("sQlb")
            sSb = new_sem("sSb")
            sSe = new_sem("sSe")
            sMsk = new_sem("sMsk")
            sNm = new_sem("sNm")
            sPx = new_sem("sPx")
            sLi = new_sem("sLi")
            sPT = new_sem("sPT")
            sPTe = new_sem("sPTe")
            sPV = new_sem("sPV")
            sVfin = new_sem("sVfin")
            sAtp = new_sem("sAtp")
            sAe = new_sem("sAe")
            sAwa, sAwb = new_sem("sAwa"), new_sem("sAwb")

            with contextlib.ExitStack() as est:
                en = est.enter_context
                Kg2 = en(nc.sbuf_tensor("p3K", [128, 2 * c.RG * c.T], F32))
                Vg2 = en(nc.sbuf_tensor("p3V", [128, 2 * c.RG * c.T], F32))
                qt2 = en(nc.sbuf_tensor("p3q", [128, 2 * 128], F32))
                Srow2 = en(nc.sbuf_tensor("p3S", [128, 3 * c.RG * c.T], F32))
                PTsb = en(nc.sbuf_tensor("p3PT", [128, 2 * KW], F32))
                out2 = en(nc.sbuf_tensor("p3out", [128, 2 * 128], F32))
                af2 = en(nc.sbuf_tensor("p3af", [128, 2 * 128], F32))
                mt2 = en(nc.sbuf_tensor("p3mt", [128, 3 * c.TT], F32))
                nm2 = en(nc.sbuf_tensor("p3nm", [128, 3], F32))
                lr2 = en(nc.sbuf_tensor("p3lr", [128, 3], F32))
                li2 = en(nc.sbuf_tensor("p3li", [128, 3], F32))
                pS = en(nc.psum_tensor("p3Sp", [128, 2 * KW], F32))
                pPT = en(nc.psum_tensor("p3PTp", [128, 2 * KW], F32))
                ppvb = en(nc.psum_tensor("p3pv", [128, 2 * 512], F32))
                ptpb = en(nc.psum_tensor("p3tp", [128, 512], F32))
                block = en(nc.Block())

                ptp = ptpb[:, 0:128]

                def ppv(par):
                    return ppvb[:, par * 512:par * 512 + 128]

                @block.sync
                def _(sync):
                    for g in range(c.HKV):
                        skv = sKVa if g % 2 == 0 else sKVb
                        if g == 0 and 2 in c.phases:
                            sync.wait_ge(sCC, 1)
                        if g >= 2:
                            sync.wait_ge(sSb, tiles[(g - 1) * c.G4 * c.TT - 1][4] + 1)
                            sync.wait_ge(sPV, tiles[(g - 1) * c.G4 * c.TT - 1][4] + 1)
                        srcK = bass.AP(agk_out, g * 128 * c.T,
                                       [[c.T, 128], [c.KO * c.T, c.RG], [1, c.T]])
                        sync.dma_start(out=Kg2[:, (g % 2) * c.RG * c.T:(g % 2 + 1) * c.RG * c.T],
                                       in_=srcK).then_inc(skv, 16)
                        if g == 0 and 2 in c.phases:
                            sync.wait_ge(sCC, 2)
                        srcV = bass.AP(agv_out, g * 128,
                                       [[c.VO, 128], [c.T * c.VO, c.RG],
                                        [128 * c.VO, c.TT], [1, 128]])
                        sync.dma_start(out=Vg2[:, (g % 2) * c.RG * c.T:(g % 2 + 1) * c.RG * c.T],
                                       in_=srcV).then_inc(sVla if g % 2 == 0 else sVlb, 16)
                        for qh_ in range(c.G4):
                            h = g * c.G4 + qh_
                            for u in range(c.TT):
                                j = (g * c.G4 + qh_) * c.TT + u
                                if j >= 2:
                                    sync.wait_ge(sSb, tiles[j - 2][4] + 1)
                                sq_ = sQla if j % 2 == 0 else sQlb
                                sync.dma_start(out=qt2[:, (j % 2) * 128:(j % 2 + 1) * 128],
                                               in_=q_buf.ap()[h * 128:(h + 1) * 128,
                                                              u * 128:(u + 1) * 128]).then_inc(sq_, 16)

                @block.tensor
                def _(tensor):
                    def emit_passB(jm):
                        g, qh_, u, k0, ke = tiles[jm]
                        js = jm % 3
                        jp = jm % 2
                        Prow = Srow2[:, js * c.RG * c.T:js * c.RG * c.T + (u + 1) * KW]
                        Vg = Vg2[:, (g % 2) * c.RG * c.T:(g % 2 + 1) * c.RG * c.T]
                        tensor.wait_ge(sPx, jm + 1)
                        tensor.wait_ge(sVla if g % 2 == 0 else sVlb, 16 * (g // 2 + 1))
                        if jm >= 2:
                            tensor.wait_ge(sVfin, jm - 1)

                        def tp(t):
                            k = k0 + t
                            par = k % 2
                            if k >= 2:
                                tensor.wait_ge(sPTe, k - 1)
                            for r in range(c.RG):
                                ins = tensor.transpose(
                                    pPT[:, par * KW + r * 128:par * KW + (r + 1) * 128],
                                    Prow[:, t * KW + r * 128:t * KW + (r + 1) * 128],
                                    ident[:])
                            ins.then_inc(sPT, 1)

                        def pv(t):
                            k = k0 + t
                            par = k % 2
                            tensor.wait_ge(sPTe, k + 1)
                            for r in range(c.RG):
                                ins = tensor.matmul(
                                    ppv(jp),
                                    PTsb[:, par * KW + r * 128:par * KW + (r + 1) * 128],
                                    Vg[:, (r * c.TT + t) * 128:(r * c.TT + t + 1) * 128],
                                    start=(t == 0 and r == 0), stop=(t == u and r == c.RG - 1))
                            ins.then_inc(sPV, 1)

                        tp(0)
                        for t in range(1, u + 1):
                            tp(t)
                            pv(t - 1)
                        pv(u)

                    def emit_ttp(jm):
                        tensor.wait_ge(sVfin, jm + 1)
                        if jm >= 1:
                            tensor.wait_ge(sAe, jm)
                        tensor.transpose(ptp, out2[:, (jm % 2) * 128:(jm % 2 + 1) * 128],
                                         ident[:]).then_inc(sAtp, 1)

                    for j, (g, qh_, u, k0, ke) in enumerate(tiles):
                        qt = qt2[:, (j % 2) * 128:(j % 2 + 1) * 128]
                        for t in range(u + 1):
                            k = k0 + t
                            par = k % 2
                            if k >= 2:
                                tensor.wait_ge(sSe, k - 1)
                            if t == 0:
                                tensor.wait_ge(sQla if j % 2 == 0 else sQlb,
                                               16 * (j // 2 + 1))
                                tensor.wait_ge(sKVa if g % 2 == 0 else sKVb,
                                               16 * (g // 2 + 1))
                            rhs = bass.AP(Kg2, (g % 2) * c.RG * c.T + t * 128,
                                          [[2 * c.RG * c.T, 128], [c.T, c.RG], [1, 128]])
                            tensor.matmul(pS[:, par * KW:(par + 1) * KW], qt, rhs,
                                          start=True, stop=True).then_inc(sSb, 1)
                        if j >= 1:
                            emit_passB(j - 1)
                            emit_ttp(j - 1)
                    emit_passB(NT - 1)
                    emit_ttp(NT - 1)

                @block.scalar
                def _(scalar):
                    def emit_sae(jm):
                        scalar.wait_ge(sAtp, jm + 1)
                        if jm >= 2:
                            scalar.wait_ge(sAwa if jm % 2 == 0 else sAwb,
                                           16 * ((jm - 2) // 2 + 1))
                        scalar.copy(af2[:, (jm % 2) * 128:(jm % 2 + 1) * 128],
                                    ptp).then_inc(sAe, 1)

                    for j, (g, qh_, u, k0, ke) in enumerate(tiles):
                        js = j % 3
                        Srow = Srow2[:, js * c.RG * c.T:js * c.RG * c.T + (u + 1) * KW]
                        if j >= 1:
                            jm = j - 1
                            gm, qm, um, km0, kme = tiles[jm]
                            for t in range(um + 1):
                                k = km0 + t
                                par = k % 2
                                scalar.wait_ge(sPT, k + 1)
                                if k >= 2:
                                    scalar.wait_ge(sPV, k - 1)
                                scalar.copy(PTsb[:, par * KW:(par + 1) * KW],
                                            pPT[:, par * KW:(par + 1) * KW]).then_inc(sPTe, 1)
                            emit_sae(jm)
                        scalar.wait_ge(sNm, j + 1)
                        scalar.activation(Srow[:], Srow[:], AF.Exp,
                                          bias=nm2[:, js:js + 1], scale=1.0,
                                          accum_out=lr2[:, js:js + 1]).then_inc(sPx, 1)
                    jm = NT - 1
                    gm, qm, um, km0, kme = tiles[jm]
                    for t in range(um + 1):
                        k = km0 + t
                        par = k % 2
                        scalar.wait_ge(sPT, k + 1)
                        if k >= 2:
                            scalar.wait_ge(sPV, k - 1)
                        scalar.copy(PTsb[:, par * KW:(par + 1) * KW],
                                    pPT[:, par * KW:(par + 1) * KW]).then_inc(sPTe, 1)
                    emit_sae(jm)

                @block.gpsimd
                def _(gpsimd):
                    for jm, (gg, qq, uu, _k0, _ke) in enumerate(tiles):
                        h = gg * c.G4 + qq
                        gpsimd.wait_ge(sAe, jm + 1)
                        sw = sAwa if jm % 2 == 0 else sAwb
                        gpsimd.dma_start(
                            out=attn_buf.ap()[h * 128:(h + 1) * 128, uu * 128:(uu + 1) * 128],
                            in_=af2[:, (jm % 2) * 128:(jm % 2 + 1) * 128]).then_inc(sw, 16)
                    gpsimd.wait_ge(sAwa, 16 * ((NT + 1) // 2))
                    gpsimd.wait_ge(sAwb, 16 * (NT // 2))

                @block.vector
                def _(vector):
                    def finalize(jm):
                        gm, qm, um, km0, kme = tiles[jm]
                        jms = jm % 3
                        jmp = jm % 2
                        vector.wait_ge(sPx, jm + 1)
                        vector.reciprocal(li2[:, jms:jms + 1], lr2[:, jms:jms + 1])
                        vector.wait_ge(sPV, kme + 1)
                        if jm >= 2:
                            vector.wait_ge(sAtp, jm - 1)
                        vector.drain()
                        vector.tensor_scalar(out2[:, jmp * 128:(jmp + 1) * 128], ppv(jmp),
                                             li2[:, jms:jms + 1], None,
                                             op0=ALU.mult).then_inc(sVfin, 1)

                    for j, (g, qh_, u, k0, ke) in enumerate(tiles):
                        js = j % 3
                        jp = j % 2
                        Srow = Srow2[:, js * c.RG * c.T:js * c.RG * c.T + (u + 1) * KW]
                        mt = mt2[:, js * c.TT:js * c.TT + c.TT]
                        for t in range(u + 1):
                            k = k0 + t
                            par = k % 2
                            vector.wait_ge(sSb, k + 1)
                            if j >= 3 and t == 0:
                                vector.wait_ge(sPT, tiles[j - 3][4] + 1)
                            vector.tensor_copy(Srow[:, t * KW:(t + 1) * KW],
                                               pS[:, par * KW:(par + 1) * KW]).then_inc(sSe, 1)
                            if t >= 1:
                                vector.wait_ge(sSe, k)
                                vector.reduce_max(mt[:, t - 1:t], Srow[:, (t - 1) * KW:t * KW],
                                                  axis=AX)
                        vector.wait_ge(sSe, ke + 1)
                        vector.tensor_tensor(Srow[:, u * KW:(u + 1) * KW],
                                             Srow[:, u * KW:(u + 1) * KW],
                                             dmask[:], op=ALU.add)
                        vector.drain()
                        vector.reduce_max(mt[:, u:u + 1], Srow[:, u * KW:(u + 1) * KW],
                                          axis=AX)
                        vector.drain()
                        vector.reduce_max(nm2[:, js:js + 1], mt[:, 0:u + 1], axis=AX,
                                          negate=True).then_inc(sNm, 1)
                        if j >= 1:
                            finalize(j - 1)
                    finalize(NT - 1)

        if 4 in c.phases:
            OS4 = c.D // 512
            sWoa, sWob = new_sem("sWoa"), new_sem("sWob")
            sX1r = [new_sem("sX1r%d" % i) for i in range(4)]
            sT4 = new_sem("sT4")
            sXe = new_sem("sXe")
            sXna, sXnb = new_sem("sXna"), new_sem("sXnb")
            sAl = new_sem("sAl")
            with contextlib.ExitStack() as est:
                en = est.enter_context
                attn_f = en(nc.sbuf_tensor("p4attn", [128, c.HQ * c.T], F32))
                wob = en(nc.sbuf_tensor("p4w", [128, 2 * 512], F32))
                x1b = en(nc.sbuf_tensor("p4x1", [128, 4 * 512], F32))
                xnb = en(nc.sbuf_tensor("p4xn", [128, 2 * 512], F32))
                ps8 = en(nc.psum_tensor("p4ps", [128, 8 * 512], F32))
                block = en(nc.Block())

                @block.sync
                def _(sync):
                    if 3 in c.phases:
                        sync.wait_ge(sAwa, 16 * ((NT + 1) // 2))
                        sync.wait_ge(sAwb, 16 * (NT // 2))
                    q4 = c.HQ // 4
                    for p in range(4):
                        srcA = bass.AP(attn_buf, p * q4 * 128 * c.T,
                                       [[c.T, 128], [128 * c.T, q4], [1, c.T]])
                        sync.dma_start(out=attn_f[:, p * q4 * c.T:(p + 1) * q4 * c.T],
                                       in_=srcA).then_inc(sAl, 16)
                    for os in range(OS4):
                        for kc in range(c.HQ):
                            idx = os * c.HQ + kc
                            if idx >= 2:
                                sync.wait_ge(sT4, idx - 1)
                            sw = sWoa if idx % 2 == 0 else sWob
                            sync.dma_start(out=wob[:, (idx % 2) * 512:(idx % 2 + 1) * 512],
                                           in_=woT[kc * 128:(kc + 1) * 128,
                                                   os * 512:(os + 1) * 512]).then_inc(sw, 16)

                @block.gpsimd
                def _(gpsimd):
                    if 1 in c.phases:
                        gpsimd.wait_ge(fs["XWa"], B1["XWa"])
                        gpsimd.wait_ge(fs["XWb"], B1["XWb"])
                    tot = OS4 * c.TT

                    def x1_load(idx):
                        os_, tt_ = idx // c.TT, idx % c.TT
                        gpsimd.dma_start(out=x1b[:, (idx % 4) * 512:(idx % 4 + 1) * 512],
                                         in_=x1_buf.ap()[tt_ * 128:(tt_ + 1) * 128,
                                                         os_ * 512:(os_ + 1) * 512]).then_inc(sX1r[idx % 4], 16)

                    for idx in range(min(4, tot)):
                        x1_load(idx)
                    for idx in range(tot):
                        os_, tt_ = idx // c.TT, idx % c.TT
                        gpsimd.wait_ge(sXe, idx + 1)
                        if idx + 4 < tot:
                            x1_load(idx + 4)
                        sw = sXna if idx % 2 == 0 else sXnb
                        gpsimd.dma_start(out=xn_buf.ap()[tt_ * 128:(tt_ + 1) * 128,
                                                         os_ * 512:(os_ + 1) * 512],
                                         in_=xnb[:, (idx % 2) * 512:(idx % 2 + 1) * 512]).then_inc(sw, 16)
                    gpsimd.wait_ge(sXna, 16 * ((tot + 1) // 2))
                    gpsimd.wait_ge(sXnb, 16 * (tot // 2))

                @block.tensor
                def _(tensor):
                    tensor.wait_ge(sAl, 64)
                    for os in range(OS4):
                        for kc in range(c.HQ):
                            idx = os * c.HQ + kc
                            tensor.wait_ge(sWoa if idx % 2 == 0 else sWob,
                                           16 * (idx // 2 + 1))
                            if kc == 0 and os >= 1:
                                tensor.wait_ge(sXe, os * c.TT)
                            w = wob[:, (idx % 2) * 512:(idx % 2 + 1) * 512]
                            for tt in range(c.TT):
                                ins = tensor.matmul(ps8[:, tt * 512:(tt + 1) * 512],
                                                    attn_f[:, kc * c.T + tt * 128:kc * c.T + tt * 128 + 128],
                                                    w, start=(kc == 0), stop=(kc == c.HQ - 1))
                            ins.then_inc(sT4, 1)

                @block.vector
                def _(vector):
                    for os in range(OS4):
                        vector.wait_ge(sT4, (os + 1) * c.HQ)
                        for tt in range(c.TT):
                            idx = os * c.TT + tt
                            vector.wait_ge(sX1r[idx % 4], 16 * (idx // 4 + 1))
                            if idx >= 2:
                                vector.wait_ge(sXna if idx % 2 == 0 else sXnb,
                                               16 * ((idx - 2) // 2 + 1))
                            vector.tensor_tensor(xnb[:, (idx % 2) * 512:(idx % 2 + 1) * 512],
                                                 x1b[:, (idx % 4) * 512:(idx % 4 + 1) * 512],
                                                 ps8[:, tt * 512:(tt + 1) * 512],
                                                 op=ALU.add).then_inc(sXe, 1)

        if 5 in c.phases:
            entry = []
            if 4 in c.phases:
                tot = (c.D // 512) * c.TT
                entry = [(sXna, 16 * ((tot + 1) // 2)), (sXnb, 16 * (tot // 2))]
            ffn_phase("p5", xn_buf.ap(), y_out, False, None, B1, entry)

    return nc


def _host_prep(cfg):
    c = cfg
    ident = np.eye(128, dtype=np.float32)
    m8 = np.zeros((c.E, c.ER), dtype=np.float32)
    for e in range(c.E):
        m8[e, e * c.R:(e + 1) * c.R] = 1.0
    per_core = []
    for core in range(c.n_cores):
        g, l = core // c.RG, core % c.RG
        pos = (l + c.RG * np.arange(c.T)).astype(np.float64)
        inv = 1.0 / (c.theta ** (np.arange(0, c.HD, 2, dtype=np.float64) / c.HD))
        fr = np.outer(pos, inv)
        cos = np.cos(fr).T
        sin = np.sin(fr).T
        scale = 1.0 / math.sqrt(c.HD)
        ropeq_c = (np.concatenate([cos, cos], axis=0) * scale).astype(np.float32)
        ropeq_s = (np.concatenate([-sin, sin], axis=0) * scale).astype(np.float32)
        ropek_c = np.concatenate([cos, cos], axis=0).astype(np.float32)
        ropek_s = np.concatenate([-sin, sin], axis=0).astype(np.float32)
        i_ = np.arange(128)[:, None]
        j_ = np.arange(128)[None, :]
        dm = np.zeros((128, c.RG * 128), dtype=np.float32)
        for r in range(c.RG):
            d = 1 if r > l else 0
            dm[:, r * 128:(r + 1) * 128] = np.where(i_ >= j_ + d, 0.0, NEG).astype(np.float32)
        per_core.append({"ident": ident, "m8": m8,
                         "ropeq_c": np.ascontiguousarray(ropeq_c),
                         "ropeq_s": np.ascontiguousarray(ropeq_s),
                         "ropek_c": np.ascontiguousarray(ropek_c),
                         "ropek_s": np.ascontiguousarray(ropek_s),
                         "dmask": dm})
    return per_core


def make_in_maps(cfg, x, wqkv, wo, route, A, Bw):
    c = cfg
    wqkvT = np.ascontiguousarray(np.asarray(wqkv).T)
    woT = np.ascontiguousarray(np.asarray(wo).T)
    routeT = np.ascontiguousarray(np.asarray(route).T)
    a2T = np.ascontiguousarray(np.asarray(A).reshape(c.E * c.R, c.D).T)
    b2 = np.ascontiguousarray(np.transpose(np.asarray(Bw), (0, 2, 1)).reshape(c.E * c.R, c.D))
    consts = _host_prep(c)
    in_maps = []
    for core in range(c.n_cores):
        g, l = core // c.RG, core % c.RG
        x_own = np.ascontiguousarray(np.asarray(x)[g, l::c.RG, :], dtype=np.float32)
        m = dict(consts[core])
        m.update({"x_own": x_own, "wqkvT": wqkvT, "woT": woT, "routeT": routeT,
                  "a2T": a2T, "b2": b2})
        in_maps.append(m)
    return in_maps


def kernel(L0_attn_hidden_states, L0_attn_Wqkv, L0_attn_Wo,
           L0_ffn_route_w, L0_ffn_A, L0_ffn_Bw):
    cfg = Cfg()
    in_maps = make_in_maps(cfg, L0_attn_hidden_states, L0_attn_Wqkv, L0_attn_Wo,
                           L0_ffn_route_w, L0_ffn_A, L0_ffn_Bw)
    nc = build_nc(cfg)
    res = run_bass_kernel_spmd(nc, in_maps, core_ids=list(range(cfg.n_cores)))
    out = np.empty((cfg.n_cores // cfg.RG, cfg.S, cfg.D), dtype=np.float32)
    for core in range(cfg.n_cores):
        g, l = core // cfg.RG, core % cfg.RG
        out[g, l::cfg.RG, :] = res.results[core]["y"]
    return out
